# version 07
# baseline (speedup 1.0000x reference)
import numpy as np
from contextlib import ExitStack

import concourse.bass as bass
import concourse.mybir as mybir
from concourse.bass_utils import run_bass_kernel_spmd

F32 = mybir.dt.float32
BF16 = mybir.dt.bfloat16
AF = mybir.ActivationFunctionType
ALU = mybir.AluOpType
AX = mybir.AxisListType

T = 2048
CD = 1024
NCC = 8
NTT = 16
NTC = 4
H = 16
HD = 64
KB = 1024
EXP_C_A = 12.0
EXP_C_B = 6.0
NEG = -1e30

ENGINES = ("tensor", "vector", "scalar", "gpsimd", "sync")
AUTOSEM = {"vector": "VE", "scalar": "SE", "gpsimd": "GE"}


class Prog:
    def __init__(self, nc, stack):
        self.nc = nc
        self.stack = stack
        self.ops = {e: [] for e in ENGINES}
        self.sems = {}

    def sem(self, name):
        if name not in self.sems:
            h = self.stack.enter_context(self.nc.semaphore(name))
            self.sems[name] = [h, 0]
        return self.sems[name]

    def count(self, name):
        return self.sem(name)[1]

    def emit(self, eng, fn, waits=(), incs=(), auto=True):
        incs = list(incs)
        if auto and eng in AUTOSEM:
            incs.append((AUTOSEM[eng], 1))
        inc_t = []
        for name, k in incs:
            s = self.sem(name)
            s[1] += k
            inc_t.append((s[0], k))
        wait_t = []
        for name, cnt in waits:
            s = self.sem(name)
            assert cnt <= s[1], f"wait on future count {name} {cnt}>{s[1]}"
            if cnt > 0:
                wait_t.append((s[0], cnt))

        def run(e, wait_t=wait_t, fn=fn, inc_t=inc_t):
            for h, cnt in wait_t:
                e.wait_ge(h, cnt)
            ins = fn(e)
            for h, k in inc_t:
                assert ins is not None, "then_inc on missing instruction"
                ins = ins.then_inc(h, k)
        self.ops[eng].append(run)

    def wait(self, eng, name, cnt=None):
        cnt = self.count(name) if cnt is None else cnt
        if cnt > 0:
            self.emit(eng, lambda e: None, waits=((name, cnt),), auto=False)

    def snap(self):
        return {n: s[1] for n, s in self.sems.items()}

    def fence(self, engines=ENGINES):
        snap = self.snap()
        src = ("TE", "VE", "SE", "GE", "DW", "DX")
        for w in engines:
            for sname in src:
                if sname in snap:
                    self.wait(w, sname, snap[sname])

    def finalize(self):
        nc = self.nc
        with nc.Block() as block:
            for eng in ENGINES:
                getattr(block, eng)(lambda e, eng=eng: [f(e) for f in self.ops[eng]])


class PsumPool:
    def __init__(self, P, ap):
        self.P = P
        self.ap = ap
        self.guard = [None] * 8
        self.held = set()
        self.rr = 0

    def take(self, hold=False):
        for _ in range(16):
            b = self.rr % 8
            self.rr += 1
            if b not in self.held:
                break
        else:
            raise RuntimeError("no free psum bank")
        if hold:
            self.held.add(b)
        g = self.guard[b]
        self.guard[b] = None
        return b, ([g] if g else [])

    def release(self, b, semname):
        self.guard[b] = (semname, self.P.count(semname))
        self.held.discard(b)

    def bank(self, b, n=512):
        return self.ap[:, b, 0:n]

    def bank_bf16(self, b):
        return self.ap[:, b, 0:64].bitcast(BF16)


def carve(arena, off, shape, dtype):
    esz = {F32: 4, BF16: 2}[dtype]
    n = int(np.prod(shape))
    assert off % 32 == 0, off
    v = arena[:, off:off + n * esz].bitcast(dtype)
    if len(shape) == 1:
        return v
    pat = " ".join(f"d{i}" for i in range(len(shape)))
    kw = {f"d{i}": int(shape[i]) for i in range(len(shape))}
    return v.rearrange(f"p ({pat}) -> p {pat}", **kw)


def mm_group(P, pool, pairs, n_cols, waits=(), hold=False, rows=128):
    b, gw = pool.take(hold=hold)
    out = pool.bank(b, n_cols)
    if rows != 128:
        out = out[0:rows, :]
    n = len(pairs)
    for i, (l, r) in enumerate(pairs):
        w = tuple(waits) + tuple(gw) if i == 0 else ()
        P.emit("tensor",
               lambda e, l=l, r=r, s=(i == 0), t=(i == n - 1), out=out:
               e.matmul(out, l, r, start=s, stop=t),
               waits=w, incs=((("TE", 1),) if i == n - 1 else ()))
    return b, out


def mm_into(P, out, pairs, waits=(), start=True, stop=True, inc=True):
    n = len(pairs)
    for i, (l, r) in enumerate(pairs):
        w = tuple(waits) if i == 0 else ()
        P.emit("tensor",
               lambda e, l=l, r=r, s=(start and i == 0), t=(stop and i == n - 1),
               out=out: e.matmul(out, l, r, start=s, stop=t),
               waits=w,
               incs=((("TE", 1),) if (inc and i == n - 1) else ()))


def transpose128(P, pool, src_bf16, dst_bf16, ident, waits=()):
    b, gw = pool.take()
    out = pool.bank_bf16(b)
    P.emit("tensor", lambda e, out=out, s=src_bf16: e.transpose(out, s, ident),
           waits=tuple(waits) + tuple(gw), incs=(("TE", 1),))
    te = P.count("TE")
    P.emit("vector", lambda e, out=out, d=dst_bf16: e.tensor_copy(d, out),
           waits=(("TE", te),))
    pool.release(b, "VE")


WSPEC = {
    "L0_attn_w_qkv": (1024, 3072), "L0_attn_ln_w": (1024,),
    "L0_attn_w_proj": (1024, 1024),
    "L0_ffn_ln1_w": (1024,), "L0_ffn_c_attn_w": (1024, 3072),
    "L0_ffn_c_proj_w": (1024, 1024), "L0_ffn_ln2_w": (1024,),
    "L0_ffn_router_w": (1024, 8), "L0_ffn_w1": (1024, 16384),
    "L0_ffn_w2": (16384, 1024),
    "L1_attn_W1": (1024, 3072), "L1_attn_b1": (3072,),
    "L1_attn_W2": (1024, 1024), "L1_attn_b2": (1024,),
    "L1_ffn_controller": (1024, 32, 4), "L1_ffn_f1": (1024, 32, 4, 32),
    "L1_ffn_bias": (32, 4, 32), "L1_ffn_f2": (32, 4, 32, 1024),
}


class Mod:
    def __init__(self, nb=2):
        self.stack = ExitStack()
        nc = bass.Bass("TRN2", target_bir_lowering=False)
        self.nc = nc
        self.nb = nb
        self.x = nc.dram_tensor("x", [nb, T, CD], F32, kind="ExternalInput").ap()
        self.y = nc.dram_tensor("y", [nb, T, CD], F32, kind="ExternalOutput").ap()
        self.w = {n: nc.dram_tensor(n, list(s), F32, kind="ExternalInput").ap()
                  for n, s in WSPEC.items()}
        self.cdata = nc.dram_tensor("cdata", [128, 8], F32, kind="ExternalInput").ap()
        self.w1bf = nc.dram_tensor("w1bf", [1024, 16384], BF16, kind="Internal").ap()
        self.w2bf = nc.dram_tensor("w2bf", [16384, 1024], BF16, kind="Internal").ap()
        self.f1bf = nc.dram_tensor("f1bf", [1024, 128, 32], BF16, kind="Internal").ap()
        self.f2bf = nc.dram_tensor("f2bf", [128, 32, 1024], BF16, kind="Internal").ap()
        self._prepass_w = None
        self._prepass_f = None
        self.P = Prog(nc, self.stack)

        sb = lambda name, shape, dt: self.stack.enter_context(
            nc.sbuf_tensor(name, list(shape), dt))[:]
        self.ar = sb("arena", (128, 177 * KB), mybir.dt.uint8)
        self.S = carve(self.ar, 0, (NCC, T), BF16)
        self.off = {"S": 0, "R1": 32 * KB, "R2": 64 * KB, "R3": 96 * KB,
                    "R4": 129 * KB, "R5": 161 * KB}

        self.ident_b = sb("ident_b", (128, 128), BF16)
        self.ident_f = sb("ident_f", (8, 8), F32)
        self.ident_f128 = sb("ident_f128", (128, 128), F32)
        self.madd = sb("madd", (128, 896), BF16)
        self.tri01 = sb("tri01", (128, 896), BF16)
        self.ones_col_f = sb("ones_col_f", (128, 1), F32)
        self.ones_col_b = sb("ones_col_b", (128, 1), BF16)
        self.ones_row_f = sb("ones_row_f", (1, 128), F32)
        self.rows = sb("rows", (1, 1024), F32)
        self.vcols = sb("vcols", (128, 72), F32)
        self.consts = sb("consts", (128, 8), F32)
        self.tieq = sb("tieq", (128, 8), F32)
        self.sq = sb("sq", (128, 512), F32)
        self.sq2 = sb("sq2", (128, 512), F32)
        self.rcol = sb("rcol", (128, 1), F32)
        self.gscr = self.sq[:, 0:96].rearrange("p (a b) -> p a b", a=3)
        self.grows = self.sq[:, 96:112]
        self.hprime = carve(self.ar, self.off["R5"], (16, 256), BF16)
        self.gfm = carve(self.ar, self.off["R5"] + 8 * KB, (2048,), BF16)[0:8, :]
        self.onehot = carve(self.ar, self.off["R5"] + 12 * KB, (8, 128), BF16)[0:8, :, :]
        self.rwsb = carve(self.ar, self.off["R5"] + 14 * KB, (8, 8), BF16)
        self.vrow = carve(self.ar, self.off["R1"], (1024,), F32)[0:1, :]
        self.ostage = carve(self.ar, self.off["R1"] + 4 * KB, (1024,), F32)
        self.whb = sb("whb", (128, 8, 2, 2, 64), BF16)
        self.ea = sb("ea", (128, 384), BF16)
        ps = self.stack.enter_context(nc.psum_tensor("PS", [128, 8, 512], F32))
        self.pool = PsumPool(self.P, ps[:])
        self.vc_used = 0
        self.vc_cache = {}
        self._dbg_scr = None
        self._dbg_scr_ap = None
        self._wq = ()
        self.E = None
        self.WBUF = None

    def emit_setup(self):
        P = self.P
        g = "gpsimd"
        P.emit(g, lambda e: e.memset(self.ident_b, 0.0))
        P.emit(g, lambda e: e.affine_select(out=self.ident_b, in_=self.ident_b,
               compare_op=ALU.not_equal, fill=1.0, base=0, pattern=[[-1, 128]],
               channel_multiplier=1))
        P.emit(g, lambda e: e.memset(self.ident_f, 0.0))
        P.emit(g, lambda e: e.affine_select(out=self.ident_f, in_=self.ident_f,
               compare_op=ALU.not_equal, fill=1.0, base=0, pattern=[[-1, 8]],
               channel_multiplier=1))
        P.emit(g, lambda e: e.memset(self.ident_f128, 0.0))
        P.emit(g, lambda e: e.affine_select(out=self.ident_f128, in_=self.ident_f128,
               compare_op=ALU.not_equal, fill=1.0, base=0, pattern=[[-1, 128]],
               channel_multiplier=1))
        P.emit(g, lambda e: e.memset(self.madd, 0.0))
        P.emit(g, lambda e: e.affine_select(out=self.madd, in_=self.madd,
               compare_op=ALU.is_ge, fill=NEG, base=-384,
               pattern=[[1, 896]], channel_multiplier=-1))
        P.emit(g, lambda e: e.memset(self.tri01, 1.0))
        P.emit(g, lambda e: e.affine_select(out=self.tri01, in_=self.tri01,
               compare_op=ALU.is_ge, fill=0.0, base=-384,
               pattern=[[1, 896]], channel_multiplier=-1))
        for i, vv in enumerate((1e-6, 1e-5, -EXP_C_A, -EXP_C_B)):
            P.emit(g, lambda e, i=i, vv=vv: e.memset(self.consts[:, i:i + 1], vv))
        P.emit(g, lambda e: e.memset(self.ones_col_f, 1.0))
        P.emit(g, lambda e: e.memset(self.ones_col_b, 1.0))
        P.emit(g, lambda e: e.memset(self.ones_row_f, 1.0))

    def madd_o(self, o, n=512):
        return self.madd[:, 384 - 128 * o:384 - 128 * o + n]

    def load_vec_cols(self, wap, tag, scale=None, per_head=False):
        if tag in self.vc_cache:
            return self.vc_cache[tag]
        P, pool = self.P, self.pool
        n = 16 if per_head else NCC
        off = self.vc_used
        self.vc_used += n
        assert self.vc_used <= 72
        self.vc_cache[tag] = off
        te_pre = P.count("TE")
        P.emit("sync", lambda e, w=wap: e.dma_start(out=self.vrow,
               in_=w.unsqueeze(0)), waits=(("TE", te_pre),),
               incs=(("DX", 16),), auto=False)
        dx = P.count("DX")
        step = 64 if per_head else 128
        for i in range(n):
            b, gw = pool.take()
            out = pool.bank(b, 512)[0:step, 0:1]
            src = self.vrow[:, i * step:(i + 1) * step]
            P.emit("tensor", lambda e, out=out, src=src: e.transpose(
                out, src, self.ident_f[0:1, 0:1]),
                waits=(("DX", dx),) + tuple(gw), incs=(("TE", 1),))
            te = P.count("TE")
            dst = self.vcols[0:step, off + i:off + i + 1]
            if scale is None:
                P.emit("vector", lambda e, d=dst, s=out: e.tensor_copy(d, s),
                       waits=(("TE", te),))
            else:
                P.emit("vector", lambda e, d=dst, s=out, sc=scale:
                       e.tensor_scalar_mul(d, s, sc), waits=(("TE", te),))
            pool.release(b, "VE")
        return off

    def emit_stream_in(self, b):
        P, pool = self.P, self.pool
        xin = carve(self.ar, self.off["R4"], (NTT, CD), BF16)
        src = self.x[b].rearrange("(tt p) c -> p tt c", p=128)
        dws = []
        for ck in range(4):
            P.emit("gpsimd", lambda e, ck=ck: e.dma_start(
                out=xin[:, 4 * ck:4 * ck + 4, :], in_=src[:, 4 * ck:4 * ck + 4, :]),
                incs=(("DW", 16),), auto=False)
            dws.append(P.count("DW"))
        for tt in range(NTT):
            for cc in range(NCC):
                transpose128(P, pool, xin[:, tt, cc * 128:(cc + 1) * 128],
                             self.S[:, cc, tt * 128:(tt + 1) * 128], self.ident_b,
                             waits=(("DW", dws[tt // 4]),) if cc == 0 else ())

    def emit_stream_out(self, b):
        P, pool = self.P, self.pool
        dx_prev = 0
        for tt in range(NTT):
            first_evac = True
            for cc in range(NCC):
                bk, gw = pool.take()
                out = pool.bank_bf16(bk)
                src = self.S[:, cc, tt * 128:(tt + 1) * 128]
                P.emit("tensor", lambda e, out=out, s=src: e.transpose(
                    out, s, self.ident_b), waits=tuple(gw), incs=(("TE", 1),))
                te = P.count("TE")
                dst = self.ostage[:, cc * 128:(cc + 1) * 128]
                w = (("TE", te),)
                if first_evac and dx_prev:
                    w = w + (("DX", dx_prev),)
                    first_evac = False
                P.emit("scalar", lambda e, d=dst, s=out: e.copy(d, s), waits=w)
                pool.release(bk, "SE")
            se = P.count("SE")
            dst = self.y[b, tt * 128:(tt + 1) * 128, :]
            P.emit("sync", lambda e, d=dst: e.dma_start(out=d, in_=self.ostage),
                   waits=(("SE", se),), incs=(("DX", 16),), auto=False)
            dx_prev = P.count("DX")

    def emit_ln_stats(self, chunks, bf16_chunks, eps):
        P, pool = self.P, self.pool
        bs, gs = pool.take(hold=True)
        bq, gq = pool.take(hold=True)
        sm = pool.ap[0:1, bs, :]
        ss = pool.ap[0:1, bq, :]
        ones = self.ones_col_b if bf16_chunks else self.ones_col_f
        te_sq = [0, 0]
        ve0 = P.count("VE")
        for i, ch in enumerate(chunks):
            sqb = self.sq if i % 2 == 0 else self.sq2
            w = (("TE", te_sq[i % 2]),) if te_sq[i % 2] else ()
            P.emit("vector", lambda e, ch=ch, sqb=sqb: e.tensor_mul(sqb, ch, ch),
                   waits=w)
            ve = P.count("VE")
            mm_into(P, sm, [(ones, ch)],
                    waits=tuple(gs) + ((("VE", ve0),) if i == 0 else ()),
                    start=(i == 0), stop=(i == 7), inc=True)
            gs = ()
            mm_into(P, ss, [(self.ones_col_f, sqb)],
                    waits=(("VE", ve),) + tuple(gq),
                    start=(i == 0), stop=(i == 7), inc=True)
            gq = ()
            te_sq[i % 2] = P.count("TE")
        te = P.count("TE")
        mean = self.rows[0:1, 0:512]
        var = self.rows[0:1, 512:1024]
        rstd = mean
        tmp = self.sq[0:1, 0:512]
        P.emit("scalar", lambda e, sm=sm: e.mul(mean, sm, 1.0 / CD), waits=(("TE", te),))
        P.emit("scalar", lambda e, ss=ss: e.mul(var, ss, 1.0 / CD))
        pool.release(bs, "SE")
        pool.release(bq, "SE")
        se = P.count("SE")
        bm, gwm = pool.take(hold=True)
        mm_into(P, pool.bank(bm), [(self.ones_row_f, mean)],
                waits=(("SE", se),) + tuple(gwm))
        te_bm = P.count("TE")
        P.emit("vector", lambda e, tmp=tmp: e.tensor_mul(tmp, mean, mean),
               waits=(("SE", se),))
        P.emit("vector", lambda e, tmp=tmp: e.tensor_sub(var, var, tmp))
        ve = P.count("VE")
        epsc = self.consts[0:1, 0:1] if eps == 1e-6 else self.consts[0:1, 1:2]
        P.emit("scalar", lambda e, epsc=epsc: e.activation(var, var, AF.Ln, bias=epsc),
               waits=(("VE", ve), ("TE", te_bm)))
        P.emit("scalar", lambda e, rstd=rstd: e.activation(rstd, var, AF.Exp, scale=-0.5))
        se = P.count("SE")
        br, gwr = pool.take(hold=True)
        mm_into(P, pool.bank(br), [(self.ones_row_f, rstd)],
                waits=(("SE", se),) + tuple(gwr))
        return bm, br

    def emit_ln_apply(self, chunks, out_chunks, bm, br, scr, scale_off=None):
        P, pool = self.P, self.pool
        te = P.count("TE")
        mb = pool.bank(bm)
        rb = pool.bank(br)
        for i, (ch, outv) in enumerate(zip(chunks, out_chunks)):
            t = scr[:, i, :]
            P.emit("vector", lambda e, t=t, ch=ch, mb=mb: e.tensor_sub(t, ch, mb),
                   waits=(("TE", te),) if i == 0 else ())
            if scale_off is None:
                P.emit("vector", lambda e, o=outv, t=t, rb=rb: e.tensor_mul(o, t, rb))
            else:
                sc = self.vcols[:, scale_off + i:scale_off + i + 1]
                P.emit("vector", lambda e, o=outv, t=t, rb=rb, sc=sc:
                       e.scalar_tensor_tensor(o, t, sc, rb, ALU.mult, ALU.mult))
        pool.release(bm, "VE")
        pool.release(br, "VE")

    def emit_postln(self, scr, out_chunks, scale_off=None, eps=1e-5):
        chunks = [scr[:, jc, :] for jc in range(NCC)]
        bm, br = self.emit_ln_stats(chunks, False, eps)
        self.emit_ln_apply(chunks, out_chunks, bm, br, scr, scale_off=scale_off)



def dma_w(m, dst, src):
    P = m.P
    te = P.count("TE")
    P.emit("gpsimd", lambda e, d=dst, s=src: e.dma_start(out=d, in_=s),
           waits=(("TE", te),), incs=(("DW", 16),), auto=False)
    return P.count("DW")


def _vtm_make(m, wbuf, src, vtm, bias_row=None):
    P, pool = m.P, m.pool
    ge_pre = P.count("GE")
    P.emit("gpsimd", lambda e, v=vtm: e.memset(
        v.rearrange("p a b c -> p (a b c)"), 1.0))
    ge = P.count("GE")
    first = True
    for tt in range(NTT):
        for half in range(2):
            pairs = [(src[:, cc, tt * 128:(tt + 1) * 128],
                      wbuf[:, cc, half * 512:(half + 1) * 512])
                     for cc in range(NCC)]
            if bias_row is not None:
                pairs.append((m.ones_row_f, bias_row[0:1, half * 512:(half + 1) * 512]))
            bk, out = mm_group(P, pool, pairs, 512, waits=m._wq)
            m._wq = ()
            te = P.count("TE")
            dst = vtm[:, tt, 8 * half:8 * half + 8, 0:64]
            w = (("TE", te),)
            if first:
                w = w + (("GE", ge),)
                first = False
            P.emit("vector", lambda e, d=dst, s=out.rearrange("p (h d) -> p h d", h=8):
                   e.tensor_copy(d, s), waits=w)
            pool.release(bk, "VE")


def _attn_core(m, h, qv, kv, vtm, y_tok, softmax_c):
    P, pool = m.P, m.pool
    E = m.E
    se_kc = [0] * 16
    ge_kc = [0] * 16
    for q5 in range(4):
        for kc in range(4 * q5 + 4):
            bk, out = mm_group(
                P, pool, [(kv[0:64, kc * 128:(kc + 1) * 128],
                           qv[0:64, q5 * 512:(q5 + 1) * 512])], 512, waits=m._wq)
            m._wq = ()
            te = P.count("TE")
            o = kc - 4 * q5
            w = (("TE", te),)
            dst = E[:, kc, :]
            if softmax_c is not None:
                if o >= 0:
                    P.emit("vector", lambda e, out=out, o=o: e.tensor_add(
                        out, out, m.madd_o(o)), waits=w)
                    w = (("VE", P.count("VE")),)
                cb = m.consts[:, 3:4]
                P.emit("scalar", lambda e, d=dst, s=out, cb=cb:
                       e.activation(d, s, AF.Exp, bias=cb), waits=w)
                se_kc[kc] = P.count("SE")
                pool.release(bk, "SE")
            else:
                if o >= 0:
                    P.emit("vector", lambda e, d=dst, s=out, o=o: e.tensor_tensor(
                        d, s, m.tri01[:, 384 - 128 * o:896 - 128 * o], ALU.mult),
                        waits=w)
                    pool.release(bk, "VE")
                    ge_kc[kc] = ("VE", P.count("VE"))
                else:
                    P.emit("scalar", lambda e, d=dst, s=out: e.activation(
                        d, s, AF.Copy), waits=w)
                    pool.release(bk, "SE")
                    se_kc[kc] = P.count("SE")
                    ge_kc[kc] = None
        ve = P.count("VE")
        for qt in range(4 * q5, 4 * q5 + 4):
            ql = qt - 4 * q5
            pairs = [(E[:, kc, ql * 128:(ql + 1) * 128], vtm[:, kc, h, 0:65])
                     for kc in range(qt + 1)]
            bk, gw = pool.take()
            out = pool.bank(bk, 65)
            if softmax_c is None:
                wav = (("VE", ve),) + tuple(gw)
                if ge_kc[qt] is not None:
                    wav = wav + (ge_kc[qt],)
                else:
                    wav = wav + (("SE", se_kc[qt]),)
            else:
                wav = (("SE", se_kc[qt]), ("VE", ve)) + tuple(gw)
            mm_into(P, out, pairs, waits=wav)
            te = P.count("TE")
            dst = y_tok[:, qt, h * 64:(h + 1) * 64]
            if softmax_c is not None:
                P.emit("vector", lambda e, out=out: e.reciprocal(
                    m.rcol, out[:, 64:65]),
                    waits=(("TE", te), ("SE", P.count("SE"))))
                ve_r = P.count("VE")
                P.emit("scalar", lambda e, d=dst, out=out: e.activation(
                    d, out[:, 0:64], AF.Copy, scale=m.rcol), waits=(("VE", ve_r),))
                pool.release(bk, "SE")
            else:
                P.emit("vector", lambda e, d=dst, out=out: e.tensor_copy(
                    d, out[:, 0:64]), waits=(("TE", te),))
                pool.release(bk, "VE")


def _proj_epilogue(m, wbuf, y_tok, epi_chunk, epi_tc, pipeline=True):
    P, pool = m.P, m.pool
    yfm = carve(m.ar, m.off["R3"], (NCC, 512), BF16)
    w0 = (("SE", P.count("SE")), ("VE", P.count("VE")))
    for tc in range(NTC):
        for i, tt in enumerate(range(4 * tc, 4 * tc + 4)):
            for cc in range(NCC):
                transpose128(P, pool, y_tok[:, tt, cc * 128:(cc + 1) * 128],
                             yfm[:, cc, i * 128:(i + 1) * 128], m.ident_b,
                             waits=w0)
                w0 = ()
        ve = P.count("VE")
        for jc in range(NCC):
            pairs = [(wbuf[:, cc, jc * 128:(jc + 1) * 128],
                      yfm[:, cc, :]) for cc in range(NCC)]
            bk, out = mm_group(P, pool, pairs, 512,
                               waits=m._wq + ((("VE", ve),) if jc == 0 else ()))
            m._wq = ()
            epi_chunk(tc, jc, out)
            pool.release(bk, "VE")
        if not pipeline:
            epi_tc(tc)
        elif tc > 0:
            epi_tc(tc - 1)
    if pipeline:
        epi_tc(NTC - 1)



def emit_block_a(m, b):
    P, pool = m.P, m.pool
    w = m.w
    wq = w["L0_attn_w_qkv"].rearrange("(cc p) j -> p cc j", p=128)
    wp = w["L0_attn_w_proj"].rearrange("(cc p) j -> p cc j", p=128)
    WA = carve(m.ar, m.off["R5"], (NCC, 1024), BF16)
    WB = carve(m.ar, m.off["R4"], (NCC, 1024), BF16)
    qfm = carve(m.ar, m.off["R1"], (NCC, T), BF16)
    kfm = carve(m.ar, m.off["R2"], (NCC, T), BF16)
    vtm = carve(m.ar, m.off["R3"], (NTT, 16, 65), BF16)
    y_tok = carve(m.ar, m.off["R4"], (NTT, CD), BF16)

    lnw_off = m.load_vec_cols(w["L0_attn_ln_w"], "a_lnw")
    lnwq_off = m.load_vec_cols(w["L0_attn_ln_w"], "a_lnwq", scale=0.125)
    ve_pre = P.count("VE")

    dw = dma_w(m, WA, wq[:, :, 0:1024])
    dwk = dma_w(m, WB, wq[:, :, 1024:2048])
    for part, (dst, coloff, wbuf) in enumerate(
            ((qfm, lnwq_off, WA), (kfm, lnw_off, WB))):
        m._wq = (("DW", dw if part == 0 else dwk), ("VE", ve_pre))
        if part == 1:
            dw_v = dma_w(m, WA, wq[:, :, 2048:3072])
        for tc in range(NTC):
            bss, gss = pool.take(hold=True)
            ss = pool.ap[0:1, bss, :]
            te_sq = [0, 0]
            for jc in range(NCC):
                pairs = [(wbuf[:, cc, jc * 128:(jc + 1) * 128],
                          m.S[:, cc, tc * 512:(tc + 1) * 512]) for cc in range(NCC)]
                bk, out = mm_group(P, pool, pairs, 512, waits=m._wq)
                m._wq = ()
                te = P.count("TE")
                ch = dst[:, jc, tc * 512:(tc + 1) * 512]
                P.emit("vector", lambda e, d=ch, s=out: e.tensor_copy(d, s),
                       waits=(("TE", te),))
                pool.release(bk, "VE")
                sqb = m.sq if jc % 2 == 0 else m.sq2
                wsq = (("TE", te_sq[jc % 2]),) if te_sq[jc % 2] else ()
                P.emit("vector", lambda e, c=ch, sqb=sqb: e.tensor_mul(sqb, c, c),
                       waits=wsq)
                ve = P.count("VE")
                mm_into(P, ss, [(m.ones_col_f, sqb)],
                        waits=(("VE", ve),) + tuple(gss),
                        start=(jc == 0), stop=(jc == 7), inc=True)
                gss = ()
                te_sq[jc % 2] = P.count("TE")
            rstd = m.rows[0:1, 512:1024]
            P.emit("scalar", lambda e, ss=ss: e.activation(
                rstd, ss, AF.Ln, scale=1.0 / CD, bias=m.consts[0:1, 0:1]),
                waits=(("TE", max(te_sq)),))
            P.emit("scalar", lambda e: e.activation(rstd, rstd, AF.Exp, scale=-0.5))
            pool.release(bss, "SE")
            se = P.count("SE")
            br, gwr = pool.take(hold=True)
            mm_into(P, pool.bank(br), [(m.ones_row_f, rstd)],
                    waits=(("SE", se),) + tuple(gwr))
            te = P.count("TE")
            for jc in range(NCC):
                ch = dst[:, jc, tc * 512:(tc + 1) * 512]
                sc = m.vcols[:, coloff + jc:coloff + jc + 1]
                P.emit("vector", lambda e, c=ch, sc=sc, rb=pool.bank(br):
                       e.scalar_tensor_tensor(c, c, sc, rb, ALU.mult, ALU.mult),
                       waits=(("TE", te),) if jc == 0 else ())
            pool.release(br, "VE")
    dbg_dump(m, f"q{b}", qfm)
    dbg_dump(m, f"k{b}", kfm)
    m._wq = (("DW", dw_v),)
    _vtm_make(m, WA, m.S, vtm)
    dw = dma_w(m, WA, wp)
    dbg_dump(m, f"v{b}", vtm)
    ve_v = P.count("VE")
    first = True
    for h in range(H):
        ro, hc = (h % 2) * 64, h // 2
        qv = qfm[ro:ro + 64, hc, :] if ro == 0 else qfm[64:128, hc, :]
        kv = kfm[ro:ro + 64, hc, :] if ro == 0 else kfm[64:128, hc, :]
        for wdw in range(8):
            kt0, kt1 = 2 * wdw, 2 * wdw + 1
            wx = m._wq + ((("VE", ve_v),) if first else ())
            m._wq = ()
            first = False
            bk, out = mm_group(P, pool, [(kv[:, kt0 * 128:kt0 * 128 + 128],
                                          qv[:, wdw * 256:wdw * 256 + 256])], 256,
                               waits=wx)
            te = P.count("TE")
            P.emit("vector", lambda e, out=out: e.tensor_add(
                out[:, 0:128], out[:, 0:128], m.madd_o(0, 128)),
                waits=(("TE", te),))
            vew = P.count("VE")
            P.emit("scalar", lambda e, out=out: e.activation(
                m.ea[:, 0:256], out, AF.Exp, bias=m.consts[:, 2:3]), waits=(("VE", vew),))
            pool.release(bk, "SE")
            bk2, out2 = mm_group(P, pool, [(kv[:, kt1 * 128:kt1 * 128 + 128],
                                            qv[:, wdw * 256 + 128:wdw * 256 + 256])],
                                 128)
            te = P.count("TE")
            P.emit("vector", lambda e, out2=out2: e.tensor_add(
                out2, out2, m.madd_o(0, 128)), waits=(("TE", te),))
            vew = P.count("VE")
            P.emit("scalar", lambda e, out2=out2: e.activation(
                m.ea[:, 256:384], out2, AF.Exp, bias=m.consts[:, 2:3]), waits=(("VE", vew),))
            pool.release(bk2, "SE")
            if h == 0 and wdw == 0:
                dbg_dump(m, f"ea{b}", m.ea)
            se = P.count("SE")
            for qt, pairs in ((kt0, [(m.ea[:, 0:128], vtm[:, kt0, h, 0:65])]),
                              (kt1, [(m.ea[:, 128:256], vtm[:, kt0, h, 0:65]),
                                     (m.ea[:, 256:384], vtm[:, kt1, h, 0:65])])):
                bky, gwy = pool.take()
                outy = pool.bank(bky, 65)
                mm_into(P, outy, pairs, waits=(("SE", se),) + tuple(gwy))
                te = P.count("TE")
                dst = y_tok[:, qt, h * 64:(h + 1) * 64]
                P.emit("vector", lambda e, outy=outy: e.reciprocal(
                    m.rcol, outy[:, 64:65]),
                    waits=(("TE", te), ("SE", P.count("SE"))))
                ve_r = P.count("VE")
                P.emit("scalar", lambda e, d=dst, outy=outy: e.activation(
                    d, outy[:, 0:64], AF.Copy, scale=m.rcol), waits=(("VE", ve_r),))
                pool.release(bky, "SE")
    dbg_dump(m, f"y{b}", y_tok)
    m._wq = (("DW", dw),)
    scr2d = carve(m.ar, m.off["R2"], (2, NCC, 512), F32)

    def epi_chunk(tc, jc, psum):
        te = P.count("TE")
        xch = m.S[:, jc, tc * 512:(tc + 1) * 512]
        P.emit("vector", lambda e, p=psum, x=xch, d=scr2d[:, tc % 2, jc, :]:
               e.tensor_add(d, p, x), waits=(("TE", te),))

    def epi_tc(tc):
        outs = [m.S[:, jc, tc * 512:(tc + 1) * 512] for jc in range(NCC)]
        m.emit_postln(scr2d[:, tc % 2, :, :], outs)

    _proj_epilogue(m, WA, y_tok, epi_chunk, epi_tc)



def _emit_whb_dma(m, wsrc, h, qoff, koff):
    P = m.P
    dst = m.whb[:, :, h % 2, :, :]
    te = P.count("TE")
    P.emit("gpsimd", lambda e, d=dst[:, :, 0, :],
           s=wsrc[:, :, qoff + h * 64:qoff + h * 64 + 64]:
           e.dma_start(out=d, in_=s), waits=(("TE", te),),
           incs=(("DW", 16),), auto=False)
    P.emit("gpsimd", lambda e, d=dst[:, :, 1, :],
           s=wsrc[:, :, koff + h * 64:koff + h * 64 + 64]:
           e.dma_start(out=d, in_=s), incs=(("DW", 16),), auto=False)
    return P.count("DW")


def _perh_qk(m, wsrc, h, qh, kh, src, qoff, koff, dw, dw_next_cb=None,
             qscale=None, bq_col=None, bk_col=None):
    P, pool = m.P, m.pool
    par = h % 2
    dst = m.whb[:, :, par, :, :]
    for i, (out_t, col) in enumerate(((qh, bq_col), (kh, bk_col))):
        for tc in range(NTC):
            pairs = [(dst[:, cc, i, :], src[:, cc, tc * 512:(tc + 1) * 512])
                     for cc in range(NCC)]
            wx = ((("DW", dw),) + m._wq) if (tc == 0 and i == 0) else ()
            bk_, out = mm_group(P, pool, pairs, 512, waits=wx, rows=64)
            if tc == 0 and i == 0:
                m._wq = ()
            te = P.count("TE")
            d = out_t[0:64, tc * 512:(tc + 1) * 512]
            s64 = out
            if col is not None:
                P.emit("vector", lambda e, d=d, s=s64, c=col:
                       e.scalar_tensor_tensor(d, s, 1.0,
                                              c.broadcast_to([64, 512]),
                                              ALU.mult, ALU.add),
                       waits=(("TE", te),))
                pool.release(bk_, "VE")
            elif i == 0 and qscale is not None:
                P.emit("vector", lambda e, d=d, s=s64, q=qscale:
                       e.tensor_scalar_mul(d, s, q), waits=(("TE", te),))
                pool.release(bk_, "VE")
            else:
                P.emit("vector", lambda e, d=d, s=s64: e.tensor_copy(d, s),
                       waits=(("TE", te),))
                pool.release(bk_, "VE")
            if i == 1 and tc == 0 and dw_next_cb is not None:
                dw_next_cb()


def emit_block_b_attn(m, b):
    P, pool = m.P, m.pool
    w = m.w
    wsrc = w["L0_ffn_c_attn_w"].rearrange("(cc p) j -> p cc j", p=128)
    wp = w["L0_ffn_c_proj_w"].rearrange("(cc p) j -> p cc j", p=128)
    WA = carve(m.ar, m.off["R5"], (NCC, 1024), BF16)
    h1 = carve(m.ar, m.off["R1"], (NCC, T), BF16)
    qh = carve(m.ar, m.off["R2"], (T,), BF16)
    kh = carve(m.ar, m.off["R2"] + 4 * KB, (T,), BF16)
    m.E = carve(m.ar, m.off["R2"] + 8 * KB, (16, 512), BF16)
    vtm = carve(m.ar, m.off["R3"], (NTT, 16, 65), BF16)
    y_tok = carve(m.ar, m.off["R4"], (NTT, CD), BF16)
    scr = carve(m.ar, m.off["R2"] + 8 * KB, (NCC, 512), F32)

    m._h1ap = h1
    ln1_off = m.load_vec_cols(w["L0_ffn_ln1_w"], "b_ln1")
    ln2_off = m.load_vec_cols(w["L0_ffn_ln2_w"], "b_ln2")
    ve_pre = P.count("VE")
    for tc in range(NTC):
        chunks = [m.S[:, cc, tc * 512:(tc + 1) * 512] for cc in range(NCC)]
        outs = [h1[:, cc, tc * 512:(tc + 1) * 512] for cc in range(NCC)]
        bm, br = m.emit_ln_stats(chunks, True, 1e-5)
        if tc == 0:
            P.wait("vector", "VE", ve_pre)
        m.emit_ln_apply(chunks, outs, bm, br, scr, scale_off=ln1_off)
    dw = dma_w(m, WA, wsrc[:, :, 2048:3072])
    ve = P.count("VE")
    m._wq = (("DW", dw), ("VE", ve))
    _vtm_make(m, WA, h1, vtm)
    dw = None
    dwh = {0: _emit_whb_dma(m, wsrc, 0, 0, 1024)}

    for h in range(H):
        m._wq = ()

        def pref(h=h):
            if h + 1 < H:
                dwh[h + 1] = _emit_whb_dma(m, wsrc, h + 1, 0, 1024)
            if m._prepass_w is None or m._prepass_w[1] == 0:
                pass
        _perh_qk(m, wsrc, h, qh, kh, h1, 0, 1024, dwh[h], pref, qscale=0.125)
        if b == 0:
            if h < 8:
                P.emit("gpsimd", lambda e, h=h: e.dma_start(
                    out=m.w1bf[:, h * 2048:(h + 1) * 2048],
                    in_=m.w["L0_ffn_w1"][:, h * 2048:(h + 1) * 2048]),
                    incs=(("DW", 16),), auto=False)
            else:
                hh = h - 8
                P.emit("gpsimd", lambda e, hh=hh: e.dma_start(
                    out=m.w2bf[hh * 2048:(hh + 1) * 2048, :],
                    in_=m.w["L0_ffn_w2"][hh * 2048:(hh + 1) * 2048, :]),
                    incs=(("DW", 16),), auto=False)
            if h == 15:
                m._prepass_w = ("DW", P.count("DW"))
        if h == 0:
            dw = dma_w(m, WA, wp)
        _attn_core(m, h, qh, kh, vtm, y_tok, EXP_C_B)
    m._wq = (("DW", dw),)
    h2 = h1

    scr2d = carve(m.ar, m.off["R2"], (2, NCC, 512), F32)

    def epi_chunk(tc, jc, psum):
        te = P.count("TE")
        xch = m.S[:, jc, tc * 512:(tc + 1) * 512]
        P.emit("vector", lambda e, p=psum, x=xch, d=scr2d[:, tc % 2, jc, :]:
               e.tensor_add(d, p, x), waits=(("TE", te),))
        P.emit("vector", lambda e, x=xch, d=scr2d[:, tc % 2, jc, :]:
               e.tensor_add(x, d, x))

    def epi_tc(tc):
        outs = [h2[:, jc, tc * 512:(tc + 1) * 512] for jc in range(NCC)]
        m.emit_postln(scr2d[:, tc % 2, :, :], outs, scale_off=ln2_off)

    _proj_epilogue(m, WA, y_tok, epi_chunk, epi_tc)


def emit_block_c(m, b):
    P, pool = m.P, m.pool
    w = m.w
    wsrc = w["L1_attn_W1"].rearrange("(cc p) j -> p cc j", p=128)
    wp = w["L1_attn_W2"].rearrange("(cc p) j -> p cc j", p=128)
    WA = carve(m.ar, m.off["R5"], (NCC, 1024), BF16)
    qh = carve(m.ar, m.off["R2"], (T,), BF16)
    kh = carve(m.ar, m.off["R2"] + 4 * KB, (T,), BF16)
    m.E = carve(m.ar, m.off["R2"] + 8 * KB, (16, 512), BF16)
    vtm = carve(m.ar, m.off["R3"], (NTT, 16, 65), BF16)
    y_tok = carve(m.ar, m.off["R4"], (NTT, CD), BF16)
    scr2d = carve(m.ar, m.off["R2"], (2, NCC, 512), F32)

    b1k_off = m.load_vec_cols(w["L1_attn_b1"][0:1024], "c_b1k", per_head=True)
    b1q_off = m.load_vec_cols(w["L1_attn_b1"][1024:2048], "c_b1q", per_head=True)
    b2_off = m.load_vec_cols(w["L1_attn_b2"], "c_b2")
    te = P.count("TE")
    P.emit("sync", lambda e: e.dma_start(out=m.vrow,
           in_=w["L1_attn_b1"][2048:3072].unsqueeze(0)), waits=(("TE", te),),
           incs=(("DX", 16),), auto=False)
    dx = P.count("DX")
    ve_pre = P.count("VE")

    dw = dma_w(m, WA, wsrc[:, :, 2048:3072])
    m._wq = (("DW", dw), ("DX", dx), ("VE", ve_pre))
    _vtm_make(m, WA, m.S, vtm, bias_row=m.vrow)
    dw = None
    dwh = {0: _emit_whb_dma(m, wsrc, 0, 1024, 0)}
    for h in range(H):
        m._wq = ()
        bqc = m.vcols[0:64, b1q_off + h:b1q_off + h + 1]
        bkc = m.vcols[0:64, b1k_off + h:b1k_off + h + 1]

        def pref(h=h):
            if h + 1 < H:
                dwh[h + 1] = _emit_whb_dma(m, wsrc, h + 1, 1024, 0)
        _perh_qk(m, wsrc, h, qh, kh, m.S, 1024, 0, dwh[h], pref,
                 bq_col=bqc, bk_col=bkc)
        if b == 0 and h < 4:
            P.emit("gpsimd", lambda e, h=h: e.dma_start(
                out=m.f1bf.rearrange("c es f -> c (es f)")[
                    h * 256:(h + 1) * 256, :],
                in_=m.w["L1_ffn_f1"].rearrange("c e s2 f -> c (e s2 f)")[
                    h * 256:(h + 1) * 256, :]), incs=(("DW", 16),), auto=False)
        if b == 0 and 4 <= h < 8:
            hh = h - 4
            P.emit("gpsimd", lambda e, hh=hh: e.dma_start(
                out=m.f2bf.rearrange("es f c -> (es f) c")[
                    hh * 1024:(hh + 1) * 1024, :],
                in_=m.w["L1_ffn_f2"].rearrange("e s2 f c -> (e s2 f) c")[
                    hh * 1024:(hh + 1) * 1024, :]), incs=(("DW", 16),), auto=False)
        if b == 0 and h == 8:
            m._prepass_f = ("DW", P.count("DW"))
        if h == 0:
            dw = dma_w(m, WA, wp)
        _attn_core(m, h, qh, kh, vtm, y_tok, None)
    m._wq = (("DW", dw),)

    def epi_chunk(tc, jc, psum):
        te = P.count("TE")
        c = m.vcols[:, b2_off + jc:b2_off + jc + 1]
        P.emit("vector", lambda e, p=psum, d=scr2d[:, tc % 2, jc, :], c=c:
               e.scalar_tensor_tensor(d, p, 1.0, c.broadcast_to([128, 512]),
                                      ALU.mult, ALU.add), waits=(("TE", te),))
        xch = m.S[:, jc, tc * 512:(tc + 1) * 512]
        P.emit("vector", lambda e, x=xch, d=scr2d[:, tc % 2, jc, :]:
               e.tensor_add(d, d, x))

    def epi_tc(tc):
        outs = [m.S[:, jc, tc * 512:(tc + 1) * 512] for jc in range(NCC)]
        m.emit_postln(scr2d[:, tc % 2, :, :], outs)

    _proj_epilogue(m, WA, y_tok, epi_chunk, epi_tc)


def emit_block_b(m, b):
    emit_block_b_attn(m, b)
    m.P.fence()
    dbg_dump(m, f"bs{b}", m.S)
    dbg_dump(m, f"bh2_{b}", m._h1ap)
    dbg_dump(m, f"bgf{b}", m.gfm) if False else None
    emit_block_b_moe(m, b)
    dbg_dump(m, f"bgate{b}", m.gfm)


def emit_block_b_moe(m, b):
    P, pool = m.P, m.pool
    w = m.w
    h2 = carve(m.ar, m.off["R1"], (NCC, T), BF16)
    y_acc = carve(m.ar, m.off["R3"], (NCC, 1024), F32)
    w1b = carve(m.ar, m.off["R2"], (NCC, T), BF16)
    w2b = carve(m.ar, m.off["R4"], (16, CD), BF16)
    w1r = m.w1bf.rearrange("(cc p) f -> p cc f", p=128)
    w2r = m.w2bf.rearrange("(fc p) c -> p fc c", p=128)
    pw = m._prepass_w if m._prepass_w else ("DW", 0)

    te0 = P.count("TE")
    P.emit("gpsimd", lambda e: e.dma_start(
        out=m.rwsb, in_=w["L0_ffn_router_w"].rearrange("(cc p) e -> p cc e", p=128)),
        waits=(("TE", te0),), incs=(("DW", 16),), auto=False)
    dw_rw = P.count("DW")
    for e8 in range(8):
        P.emit("gpsimd", lambda e, e8=e8: e.memset(m.onehot[:, e8, :], 1.0))
        P.emit("gpsimd", lambda e, e8=e8: e.affine_select(
            out=m.onehot[:, e8, :], in_=m.onehot[:, e8, :],
            compare_op=ALU.is_equal, fill=0.0, base=-e8,
            pattern=[[0, 128]], channel_multiplier=1))
    for tc in range(NTC):
        bk_rl, gwl = pool.take()
        rl = pool.ap[0:8, bk_rl, :]
        mm_into(P, rl, [(m.rwsb[:, cc, :], h2[:, cc, tc * 512:(tc + 1) * 512])
                        for cc in range(NCC)],
                waits=(("DW", dw_rw),) + tuple(gwl))
        te = P.count("TE")
        P.emit("scalar", lambda e, rl=rl: e.copy(m.sq[0:8, 0:512], rl),
               waits=(("TE", te),))
        pool.release(bk_rl, "SE")
        se = P.count("SE")
        ltm_sb = m.sq[:, 112:144]
        for i in range(4):
            bk_t, gwt = pool.take()
            outv = pool.ap[:, bk_t, 0:8]
            P.emit("tensor", lambda e, o=outv,
                   s=m.sq[0:8, i * 128:(i + 1) * 128]:
                   e.transpose(o, s, m.ident_f),
                   waits=(("SE", se),) + tuple(gwt),
                   incs=(("TE", 1),))
            te = P.count("TE")
            P.emit("vector", lambda e, o=outv, d=ltm_sb[:, i * 8:(i + 1) * 8]:
                   e.tensor_copy(d, o), waits=(("TE", te),))
            pool.release(bk_t, "VE")
        ltm = ltm_sb.rearrange("p (a b) -> p a b", a=4)

        def vv(fn):
            P.emit("vector", fn, waits=(("VE", P.count("VE")),))
        m1 = m.grows[:, 0:4]
        m2 = m.grows[:, 4:8]
        p1 = m.grows[:, 8:12]
        p2 = m.grows[:, 12:16]
        eq1 = m.gscr[:, 0, :].rearrange("p (a b) -> p a b", a=4)
        lm = m.gscr[:, 1, :].rearrange("p (a b) -> p a b", a=4)
        gg = m.gscr[:, 2, 0:32].bitcast(BF16)[:, 0:32].rearrange(
            "p (a b) -> p a b", a=4)
        t1 = m.gscr[:, 1, 0:16].rearrange("p (a b) -> p a b", a=4)
        t2 = m.gscr[:, 1, 16:24].rearrange("p (a b) -> p a b", a=4)
        vv(lambda e, ltm=ltm, t1=t1: e.tensor_max(
            t1, ltm[:, :, 0::2], ltm[:, :, 1::2]))
        vv(lambda e, t1=t1, t2=t2: e.tensor_max(
            t2, t1[:, :, 0::2], t1[:, :, 1::2]))
        vv(lambda e, t2=t2, m1=m1: e.tensor_max(
            m1.rearrange("p (a o) -> p a o", o=1), t2[:, :, 0:1], t2[:, :, 1:2]))
        vv(lambda e, ltm=ltm, eq1=eq1, m1=m1: e.tensor_tensor(
            eq1, ltm, m1.rearrange("p (a o) -> p a o", o=1).broadcast_to([128, 4, 8]),
            ALU.is_equal))
        vv(lambda e, ltm=ltm, eq1=eq1, lm=lm: e.scalar_tensor_tensor(
            lm, eq1, NEG, ltm, ALU.mult, ALU.add))
        u1 = m.gscr[:, 2, 0:16].rearrange("p (a b) -> p a b", a=4)
        u2 = m.gscr[:, 2, 16:24].rearrange("p (a b) -> p a b", a=4)
        vv(lambda e, lm=lm, u1=u1: e.tensor_max(
            u1, lm[:, :, 0::2], lm[:, :, 1::2]))
        vv(lambda e, u1=u1, u2=u2: e.tensor_max(
            u2, u1[:, :, 0::2], u1[:, :, 1::2]))
        vv(lambda e, u2=u2, m2=m2: e.tensor_max(
            m2.rearrange("p (a o) -> p a o", o=1), u2[:, :, 0:1], u2[:, :, 1:2]))
        vv(lambda e, lm=lm, m2=m2: e.tensor_tensor(
            lm, lm, m2.rearrange("p (a o) -> p a o", o=1).broadcast_to([128, 4, 8]),
            ALU.is_equal))
        vv(lambda e, m1=m1, m2=m2: e.tensor_sub(m2, m2, m1))
        ve = P.count("VE")
        P.emit("scalar", lambda e, p1=p1, m2=m2: e.activation(
            p1, m2, AF.Sigmoid, scale=-1.0), waits=(("VE", ve),))
        se = P.count("SE")
        P.emit("vector", lambda e, p1=p1, p2=p2: e.tensor_scalar(
            p2, p1, -1.0, 1.0, ALU.mult, ALU.add),
            waits=(("SE", se), ("VE", P.count("VE"))))
        vv(lambda e, eq1=eq1, p1=p1: e.tensor_mul(
            eq1, eq1, p1.rearrange("p (a o) -> p a o", o=1).broadcast_to([128, 4, 8])))
        vv(lambda e, lm=lm, p2=p2: e.tensor_mul(
            lm, lm, p2.rearrange("p (a o) -> p a o", o=1).broadcast_to([128, 4, 8])))
        vv(lambda e, gg=gg, eq1=eq1, lm=lm: e.tensor_add(gg, eq1, lm))
        if tc == 0:
            dbg_dump(m, f"gsq{b}", m.sq)
        ve = P.count("VE")
        for i in range(4):
            bk_g, gwg = pool.take()
            outg = pool.ap[0:8, bk_g, 0:64].bitcast(BF16)
            P.emit("tensor", lambda e, o=outg,
                   s=m.gscr[:, 2, 0:32].bitcast(BF16)[:, i * 8:(i + 1) * 8]:
                   e.transpose(o, s, m.ident_b), waits=(("VE", ve),) + tuple(gwg),
                   incs=(("TE", 1),))
            te = P.count("TE")
            P.emit("scalar", lambda e, o=outg, d=m.gfm[:, tc * 512 + i * 128:
                   tc * 512 + (i + 1) * 128]: e.copy(d, o), waits=(("TE", te),))
            pool.release(bk_g, "SE")
    se_g = P.count("SE")
    ge_g = P.count("GE")
    for half in range(2):
        t0 = half * 1024
        P.emit("gpsimd", lambda e: e.memset(
            y_acc.rearrange("p a b -> p (a b)"), 0.0))
        ge_z = P.count("GE")
        te = P.count("TE")
        P.emit("sync", lambda e: e.dma_start(out=w1b, in_=w1r[:, :, 0:2048]),
               waits=(("TE", te), pw), incs=(("DX", 16),), auto=False)
        dw1 = ("DX", P.count("DX"))
        P.emit("sync", lambda e: e.dma_start(out=w2b, in_=w2r[:, 0:16, :]),
               waits=(pw,), incs=(("DX", 16),), auto=False)
        dw2 = ("DX", P.count("DX"))
        for e8 in range(8):
            dw1_next = []
            for tq in range(4):
                tg = t0 + tq * 256
                bg, gwg = pool.take(hold=True)
                gb = pool.bank(bg, 256)
                mm_into(P, gb, [(m.onehot[:, e8, :], m.gfm[:, tg:tg + 256])],
                        waits=(("SE", se_g), ("GE", ge_g)) + tuple(gwg))
                ve_fc = []
                for fc in range(16):
                    pairs = [(w1b[:, cc, fc * 128:(fc + 1) * 128],
                              h2[:, cc, tg:tg + 256]) for cc in range(NCC)]
                    wts = ()
                    if isinstance(dw1, list):
                        if tq == 0:
                            wts = (dw1[fc],)
                    elif tq == 0 and fc == 0:
                        wts = (dw1,)
                    bh, out = mm_group(P, pool, pairs, 256, waits=wts)
                    te = P.count("TE")
                    P.emit("scalar", lambda e, d=m.hprime[:, fc, :], s=out:
                           e.activation(d, s, AF.Gelu), waits=(("TE", te),))
                    pool.release(bh, "SE")
                    se = P.count("SE")
                    P.emit("vector", lambda e, d=m.hprime[:, fc, :], gb=gb:
                           e.tensor_mul(d, d, gb), waits=(("SE", se),))
                    ve_fc.append(P.count("VE"))
                    if e8 < 7 and tq == 3:
                        te_p = P.count("TE")
                        P.emit("sync", lambda e, e8=e8, fc=fc: e.dma_start(
                            out=w1b[:, :, fc * 128:(fc + 1) * 128],
                            in_=w1r[:, :, (e8 + 1) * 2048 + fc * 128:
                                    (e8 + 1) * 2048 + (fc + 1) * 128]),
                            waits=(("TE", te_p),), incs=(("DX", 16),), auto=False)
                        dw1_next.append(("DX", P.count("DX")))
                pool.release(bg, "VE")
                if False:
                    te = P.count("TE")
                    P.emit("gpsimd", lambda e, e8=e8: e.dma_start(
                        out=w1b, in_=w1r[:, :, (e8 + 1) * 2048:(e8 + 2) * 2048]),
                        waits=(("TE", te),), incs=(("DW", 16),), auto=False)
                    dw1 = P.count("DW")
                for cc in range(NCC):
                    by, gwy = pool.take()
                    out = pool.bank(by, 256)
                    for fc in range(16):
                        wts = [("VE", ve_fc[fc])]
                        if fc == 0:
                            wts += list(gwy)
                        if tq == 0 and cc == 0 and fc == 0:
                            wts.append(dw2)
                        P.emit("tensor",
                               lambda e, l=w2b[:, fc, cc * 128:(cc + 1) * 128],
                               r=m.hprime[:, fc, :], s=(fc == 0), t=(fc == 15),
                               out=out: e.matmul(out, l, r, start=s, stop=t),
                               waits=tuple(wts),
                               incs=((("TE", 1),) if fc == 15 else ()))
                    te = P.count("TE")
                    dsty = y_acc[:, cc, tq * 256:(tq + 1) * 256]
                    wv = (("TE", te),)
                    if e8 == 0 and tq == 0 and cc == 0:
                        wv = wv + (("GE", ge_z),)
                    P.emit("vector", lambda e, d=dsty, s=out: e.tensor_add(d, d, s),
                           waits=wv)
                    pool.release(by, "VE")
            if e8 < 7:
                te = P.count("TE")
                P.emit("sync", lambda e, e8=e8: e.dma_start(
                    out=w2b, in_=w2r[:, (e8 + 1) * 16:(e8 + 2) * 16, :]),
                    waits=(("TE", te),), incs=(("DX", 16),), auto=False)
                dw2 = ("DX", P.count("DX"))
                dw1 = dw1_next
        for tc in range(2):
            for jc in range(NCC):
                ch = y_acc[:, jc, tc * 512:(tc + 1) * 512]
                xch = m.S[:, jc, t0 + tc * 512:t0 + (tc + 1) * 512]
                P.emit("vector", lambda e, c=ch, x=xch: e.tensor_add(c, c, x))
            scrv = y_acc[:, :, tc * 512:(tc + 1) * 512]
            outs = [m.S[:, jc, t0 + tc * 512:t0 + (tc + 1) * 512]
                    for jc in range(NCC)]
            m.emit_postln(scrv, outs)
        m.P.fence(("gpsimd",))


def emit_block_d(m, b):
    P, pool = m.P, m.pool
    w = m.w
    R1, R2, R3, R4, R5 = (m.off[k] for k in ("R1", "R2", "R3", "R4", "R5"))
    x_tok = carve(m.ar, R1, (NTT, CD), BF16)
    disp = carve(m.ar, R2, (NCC, 16, 128), BF16)
    packs = carve(m.ar, R3, (NCC, 16, 128), BF16)
    logits_tm = carve(m.ar, R4, (NTT, 128), F32)
    l_fm = carve(m.ar, R4 + 8 * KB, (16, 128), F32)
    maxg = carve(m.ar, R4 + 16 * KB, (64,), F32)
    P_fm = carve(m.ar, R4 + 17 * KB, (NTT, 128), BF16)
    P4 = carve(m.ar, R4, (NTT, 4, 128), BF16)
    inner = carve(m.ar, R4 + 21 * KB, (128, 16), BF16)[0:32, :, :]
    bias_nat = carve(m.ar, R4 + 29 * KB, (32,), F32)
    interT = carve(m.ar, R4 + 30 * KB, (4, 128), BF16)
    ctrl_sb = carve(m.ar, R5, (NCC, 128), BF16)
    f1blk = carve(m.ar, R5, (NCC, 2, 16, 32), BF16)
    f2str = carve(m.ar, R5, (2, 4, NCC, 128), BF16)[0:32, :, :, :, :]
    bias_sb = m.sq[0:32, 128:256]
    r1 = carve(m.ar, R2, (1024,), F32)
    r2 = carve(m.ar, R2 + 4 * KB, (512,), F32)
    r3 = carve(m.ar, R2 + 6 * KB, (256,), F32)
    r4 = carve(m.ar, R2 + 7 * KB, (128,), F32)

    te0 = P.count("TE")
    P.emit("sync", lambda e: e.dma_start(out=m.tieq, in_=m.cdata),
           incs=(("DX", 16),), auto=False)
    P.emit("gpsimd", lambda e: e.dma_start(
        out=ctrl_sb, in_=w["L1_ffn_controller"].rearrange(
            "(cc p) e s2 -> p cc (e s2)", p=128)), waits=(("TE", te0),),
        incs=(("DW", 16),), auto=False)
    P.emit("gpsimd", lambda e: e.dma_start(
        out=bias_nat, in_=w["L1_ffn_bias"].rearrange("e s2 f -> (e s2) f")),
        incs=(("DW", 16),), auto=False)
    dw0 = P.count("DX")
    dwc = P.count("DW")
    bkb, gwb = pool.take()
    outb = pool.bank(bkb, 128)[0:32, :]
    P.emit("tensor", lambda e: e.transpose(outb, bias_nat, m.ident_f128),
           waits=(("DW", dwc),) + tuple(gwb), incs=(("TE", 1),))
    te = P.count("TE")
    P.emit("vector", lambda e: e.tensor_copy(bias_sb, outb), waits=(("TE", te),))
    pool.release(bkb, "VE")
    for tt in range(NTT):
        for cc in range(NCC):
            transpose128(P, pool, m.S[:, cc, tt * 128:(tt + 1) * 128],
                         x_tok[:, tt, cc * 128:(cc + 1) * 128], m.ident_b)
    dx0 = P.count("DX")
    for tt in range(NTT):
        pairs = [(m.S[:, cc, tt * 128:(tt + 1) * 128], ctrl_sb[:, cc, :])
                 for cc in range(NCC)]
        bk, out = mm_group(P, pool, pairs, 128,
                           waits=(("DW", dwc),) if tt == 0 else ())
        te = P.count("TE")
        wts = (("TE", te),)
        if tt == 0:
            wts = wts + (("DX", dx0),)
        P.emit("vector", lambda e, d=logits_tm[:, tt, :], s=out:
               e.scalar_tensor_tensor(d, s, 1.0,
                                      m.tieq[:, 0:1].broadcast_to([128, 128]),
                                      ALU.mult, ALU.add), waits=wts)
        pool.release(bk, "VE")
    ve = P.count("VE")
    for tt in range(NTT):
        bk, gw = pool.take()
        out = pool.bank(bk, 128)
        P.emit("tensor", lambda e, o=out, s=logits_tm[:, tt, :]:
               e.transpose(o, s, m.ident_f128),
               waits=(("VE", ve),) + tuple(gw) if tt == 0 else tuple(gw),
               incs=(("TE", 1),))
        te = P.count("TE")
        P.emit("vector", lambda e, d=l_fm[:, tt, :], s=out: e.tensor_copy(d, s),
               waits=(("TE", te),))
        pool.release(bk, "VE")

    def vv(fn):
        P.emit("vector", fn, waits=(("VE", P.count("VE")),))

    L = l_fm.rearrange("p a b -> p (a b)")
    vv(lambda e: e.tensor_max(r1, L[:, 0::2], L[:, 1::2]))
    vv(lambda e: e.tensor_max(r2, r1[:, 0::2], r1[:, 1::2]))
    vv(lambda e: e.tensor_max(r3, r2[:, 0::2], r2[:, 1::2]))
    vv(lambda e: e.tensor_max(r4, r3[:, 0::2], r3[:, 1::2]))
    vv(lambda e: e.tensor_max(maxg, r4[:, 0::2], r4[:, 1::2]))
    vv(lambda e: e.tensor_tensor(
        P_fm.rearrange("p a b -> p (a b)").rearrange("p (g t) -> p g t", g=64),
        L.rearrange("p (g t) -> p g t", g=64),
        maxg.unsqueeze(2).broadcast_to([128, 64, 32]), ALU.is_equal))
    ve_p = P.count("VE")
    for tt in range(NTT):
        bk, gw = pool.take()
        out = pool.bank_bf16(bk)
        P.emit("tensor", lambda e, o=out, s=P_fm[:, tt, :]:
               e.transpose(o, s, m.ident_b),
               waits=(("VE", ve_p),) + tuple(gw), incs=(("TE", 1),))
        te = P.count("TE")
        for gl in range(4):
            P.emit("vector", lambda e, d=P4[:, tt, gl, :], s=out, gl=gl:
                   e.scalar_tensor_tensor(
                       d, s, 1.0, m.tieq[:, 1 + gl:2 + gl].broadcast_to([128, 128]),
                       ALU.mult, ALU.mult),
                   waits=(("TE", te),) if gl == 0 else ())
        pool.release(bk, "VE")
    ve_p4 = P.count("VE")
    dbg_dump(m, f"dP{b}", P_fm)
    dbg_dump(m, f"dL{b}", l_fm)

    if m._prepass_f is None:
        P.emit("gpsimd", lambda e: e.dma_start(
            out=m.f1bf.rearrange("c es f -> c (es f)"),
            in_=w["L1_ffn_f1"].rearrange("c e s2 f -> c (e s2 f)")),
            incs=(("DW", 16),), auto=False)
        P.emit("gpsimd", lambda e: e.dma_start(
            out=m.f2bf.rearrange("es f c -> (es f) c"),
            in_=w["L1_ffn_f2"].rearrange("e s2 f c -> (e s2 f) c")),
            incs=(("DW", 16),), auto=False)
        m._prepass_f = ("DW", P.count("DW"))
    f1r = m.f1bf.rearrange("(cc p) es f -> p cc es f", p=128)
    f2w = m.f2bf
    for q in range(4):
        tts = range(4 * q, 4 * q + 4)
        te_q = P.count("TE")
        P.emit("gpsimd", lambda e: e.dma_start(
            out=f1blk[:, :, 0, :, :], in_=f1r[:, :, 0:16, :]),
            waits=(("TE", te_q), m._prepass_f), incs=(("DW", 16),), auto=False)
        dwf0 = P.count("DW")
        for i, tt in enumerate(tts):
            for cc in range(NCC):
                pairs = [(x_tok[:, tt, cc * 128:(cc + 1) * 128],
                          P4[:, tt, :, :].rearrange("p a b -> p (a b)"))]
                bk, out = mm_group(P, pool, pairs, 512,
                                   waits=(("VE", ve_p4),) if (i == 0 and cc == 0 and q == 0) else ())
                te = P.count("TE")
                dst = disp[:, cc, 4 * i:4 * i + 4, :]
                P.emit("vector", lambda e, d=dst, s=out.rearrange(
                    "p (a b) -> p a b", a=4): e.tensor_copy(d, s),
                    waits=(("TE", te),))
                pool.release(bk, "VE")
        dwf = dwf0
        ve_disp = P.count("VE")
        for blk in range(8):
            if blk < 7:
                te_b = P.count("TE")
                P.emit("gpsimd", lambda e, blk=blk: e.dma_start(
                    out=f1blk[:, :, (blk + 1) % 2, :, :],
                    in_=f1r[:, :, 16 * (blk + 1):16 * (blk + 2), :]),
                    waits=(("TE", te_b),), incs=(("DW", 16),), auto=False)
            for el in range(16):
                es = blk * 16 + el
                pairs = [(f1blk[:, cc, blk % 2, el, :],
                          disp[:, cc, :, es]) for cc in range(NCC)]
                wts = (("DW", dwf), ("VE", ve_disp)) if el == 0 else ()
                bk, out = mm_group(P, pool, pairs, 16, rows=32, waits=wts)
                te = P.count("TE")
                P.emit("scalar", lambda e, d=inner[:, es, :], s=out, es=es:
                       e.activation(d, s, AF.Relu, bias=bias_sb[:, es:es + 1]),
                       waits=(("TE", te),))
                pool.release(bk, "SE")
            dwf = P.count("DW")
        se_f1 = P.count("SE")
        nblk2 = 32
        P.emit("gpsimd", lambda e: e.dma_start(
            out=f2str[:, 0, :, :, :].rearrange("p a b c -> p a (b c)"),
            in_=f2w[0:4].rearrange("x f c -> f x c")),
            waits=(m._prepass_f,), incs=(("DW", 16),), auto=False)
        dw_blk = [P.count("DW")]
        for grp in range(4):
            banks = []
            for cc in range(NCC):
                bk, gw = pool.take(hold=True)
                banks.append((bk, gw))
            for el in range(32):
                es = grp * 32 + el
                if es % 4 == 0:
                    nb = es // 4 + 1
                    if nb < nblk2:
                        te_b = P.count("TE")
                        P.emit("gpsimd", lambda e, nb=nb: e.dma_start(
                            out=f2str[:, nb % 2, :, :, :].rearrange(
                                "p a b c -> p a (b c)"),
                            in_=f2w[4 * nb:4 * nb + 4].rearrange("x f c -> f x c")),
                            waits=(("TE", te_b),), incs=(("DW", 16),), auto=False)
                        dw_blk.append(P.count("DW"))
                for cc in range(NCC):
                    bk, gw = banks[cc]
                    outv = pool.ap[:, bk, el * 16:(el + 1) * 16]
                    wts = tuple(gw)
                    if cc == 0 and es % 4 == 0:
                        wts = wts + (("DW", dw_blk[es // 4]),)
                    if el == 0 and cc == 0:
                        wts = wts + (("SE", se_f1),)
                    mm_into(P, outv,
                            [(f2str[:, (es // 4) % 2, es % 4, cc, :],
                              inner[:, es, :])], waits=wts)
                    if el == 0:
                        banks[cc] = (bk, ())
            te = P.count("TE")
            for cc in range(NCC):
                bk, _ = banks[cc]
                src = pool.bank(bk).rearrange("p (a b) -> p a b", a=32)
                src = src.transpose([0, 2, 1])
                dst = packs[:, cc, :, grp * 32:(grp + 1) * 32]
                P.emit("vector", lambda e, d=dst, s=src: e.tensor_copy(d, s),
                       waits=(("TE", te),) if cc == 0 else ())
                pool.release(bk, "VE")
        ve_pk = P.count("VE")
        items = []
        for cc in range(NCC):
            for gl in range(16):
                items.append((cc, gl))
        pend = {}
        DEPTH = 3

        def emit_mm(idx):
            cc, gl, ve_i = pend.pop(idx)
            g = q * 16 + gl
            tt, t0 = g // 4, (g % 4) * 32
            it = interT[:, idx % 4, :]
            bk2, gw2 = pool.take()
            out2 = pool.bank(bk2, 32)
            mm_into(P, out2, [(it, P_fm[:, tt, t0:t0 + 32])],
                    waits=(("VE", ve_i),) + tuple(gw2))
            te = P.count("TE")
            dst = m.S[:, cc, g * 32:(g + 1) * 32]
            P.emit("vector", lambda e, d=dst, s=out2: e.tensor_add(d, s, d),
                   waits=(("TE", te),))
            pool.release(bk2, "VE")

        for idx, (cc, gl) in enumerate(items):
            sl = idx % 4
            it = interT[:, sl, :]
            bk, gw = pool.take()
            outt = pool.bank_bf16(bk)
            w0 = (("VE", ve_pk),) if idx == 0 else ()
            P.emit("tensor", lambda e, o=outt, s=packs[:, cc, gl, :]:
                   e.transpose(o, s, m.ident_b),
                   waits=w0 + tuple(gw), incs=(("TE", 1),))
            te = P.count("TE")
            P.emit("vector", lambda e, o=outt, it=it: e.tensor_copy(it, o),
                   waits=(("TE", te),))
            pool.release(bk, "VE")
            pend[idx] = (cc, gl, P.count("VE"))
            if idx >= DEPTH:
                emit_mm(idx - DEPTH)
        for idx in sorted(pend):
            emit_mm(idx)
    scr = carve(m.ar, R2, (NCC, 512), F32)
    ve_s = P.count("VE")
    P.wait("vector", "VE", ve_s)
    for tc in range(NTC):
        chunks = [m.S[:, cc, tc * 512:(tc + 1) * 512] for cc in range(NCC)]
        outs = chunks
        bm, br = m.emit_ln_stats(chunks, True, 1e-5)
        m.emit_ln_apply(chunks, outs, bm, br, scr)


def make_cdata():
    tie = np.tile(np.linspace(0.0, 1e-6, 32, dtype=np.float32), 4)
    cd = np.zeros((128, 8), np.float32)
    cd[:, 0] = tie
    for gl in range(4):
        cd[32 * gl:32 * gl + 32, 1 + gl] = 1.0
    return cd



def build_module(blocks=("A", "B", "C", "D"), nb=2):
    m = Mod(nb=nb)
    m.emit_setup()
    for b in range(nb):
        m.emit_stream_in(b)
        m.P.fence()
        dbg_dump(m, f"s{b}", m.S)
        if "A" in blocks:
            emit_block_a(m, b)
            m.P.fence()
            dbg_dump(m, f"postA{b}", m.S)
        if "B" in blocks:
            emit_block_b(m, b)
            m.P.fence()
            dbg_dump(m, f"postB{b}", m.S)
        if "C" in blocks:
            emit_block_c(m, b)
            m.P.fence()
            dbg_dump(m, f"postC{b}", m.S)
        if "D" in blocks:
            emit_block_d(m, b)
            m.P.fence()
        m.emit_stream_out(b)
        m.P.fence()
    m.P.wait("sync", "DX")
    m.P.finalize()
    return m


DBG_TAGS = set()


def dbg_dump(m, tag, ap):
    if tag not in DBG_TAGS:
        return
    P = m.P
    flat = ap if ap.ndim == 2 else ap.rearrange(
        "p " + " ".join(f"d{i}" for i in range(ap.ndim - 1)) + " -> p (" +
        " ".join(f"d{i}" for i in range(ap.ndim - 1)) + ")")
    t = m.nc.dram_tensor("dbg_" + tag, [flat.shape[0], flat.shape[1]],
                         flat.dtype, kind="ExternalOutput").ap()
    P.fence()
    P.emit("sync", lambda e: e.dma_start(out=t, in_=flat),
           incs=(("DX", 16),), auto=False)
    P.fence()


GROUPS = [("A", "B", "C", "D")]


def kernel(**inputs):
    xfull = np.ascontiguousarray(inputs["L0_attn_x"], dtype=np.float32)
    shards = [xfull[2 * i:2 * i + 2] for i in range(8)]
    wmap = {n: np.ascontiguousarray(inputs[n], dtype=np.float32) for n in WSPEC}
    cur = shards
    for grp in GROUPS:
        m = build_module(blocks=grp, nb=2)
        cd = make_cdata()
        in_maps = [dict(wmap, x=cur[i], cdata=cd) for i in range(8)]
        res = run_bass_kernel_spmd(m.nc, in_maps, core_ids=list(range(8)))
        cur = [np.asarray(r["y"]) for r in res.results]
    return np.concatenate(cur, axis=0)


# version 08
# speedup vs baseline: 1.1688x; 1.1688x over previous
import numpy as np
from contextlib import ExitStack

import concourse.bass as bass
import concourse.mybir as mybir
from concourse.bass_utils import run_bass_kernel_spmd

F32 = mybir.dt.float32
BF16 = mybir.dt.bfloat16
AF = mybir.ActivationFunctionType
ALU = mybir.AluOpType
AX = mybir.AxisListType

T = 2048
CD = 1024
NCC = 8
NTT = 16
NTC = 4
H = 16
HD = 64
KB = 1024
EXP_C_A = 12.0
EXP_C_B = 6.0
NEG = -1e30

ENGINES = ("tensor", "vector", "scalar", "gpsimd", "sync")
AUTOSEM = {"vector": "VE", "scalar": "SE", "gpsimd": "GE"}


class Prog:
    def __init__(self, nc, stack):
        self.nc = nc
        self.stack = stack
        self.ops = {e: [] for e in ENGINES}
        self.sems = {}

    def sem(self, name):
        if name not in self.sems:
            h = self.stack.enter_context(self.nc.semaphore(name))
            self.sems[name] = [h, 0]
        return self.sems[name]

    def count(self, name):
        return self.sem(name)[1]

    def emit(self, eng, fn, waits=(), incs=(), auto=True):
        incs = list(incs)
        if auto and eng in AUTOSEM:
            incs.append((AUTOSEM[eng], 1))
        inc_t = []
        for name, k in incs:
            s = self.sem(name)
            s[1] += k
            inc_t.append((s[0], k))
        wait_t = []
        for name, cnt in waits:
            s = self.sem(name)
            assert cnt <= s[1], f"wait on future count {name} {cnt}>{s[1]}"
            if cnt > 0:
                wait_t.append((s[0], cnt))

        def run(e, wait_t=wait_t, fn=fn, inc_t=inc_t):
            for h, cnt in wait_t:
                e.wait_ge(h, cnt)
            ins = fn(e)
            for h, k in inc_t:
                assert ins is not None, "then_inc on missing instruction"
                ins = ins.then_inc(h, k)
        self.ops[eng].append(run)

    def wait(self, eng, name, cnt=None):
        cnt = self.count(name) if cnt is None else cnt
        if cnt > 0:
            self.emit(eng, lambda e: None, waits=((name, cnt),), auto=False)

    def snap(self):
        return {n: s[1] for n, s in self.sems.items()}

    def fence(self, engines=ENGINES):
        snap = self.snap()
        src = ("TE", "VE", "SE", "GE", "DW", "DX")
        for w in engines:
            for sname in src:
                if sname in snap:
                    self.wait(w, sname, snap[sname])

    def finalize(self):
        nc = self.nc
        with nc.Block() as block:
            for eng in ENGINES:
                getattr(block, eng)(lambda e, eng=eng: [f(e) for f in self.ops[eng]])


class PsumPool:
    def __init__(self, P, ap):
        self.P = P
        self.ap = ap
        self.guard = [None] * 8
        self.held = set()
        self.rr = 0

    def take(self, hold=False):
        for _ in range(16):
            b = self.rr % 8
            self.rr += 1
            if b not in self.held:
                break
        else:
            raise RuntimeError("no free psum bank")
        if hold:
            self.held.add(b)
        g = self.guard[b]
        self.guard[b] = None
        return b, ([g] if g else [])

    def release(self, b, semname):
        self.guard[b] = (semname, self.P.count(semname))
        self.held.discard(b)

    def bank(self, b, n=512):
        return self.ap[:, b, 0:n]

    def bank_bf16(self, b):
        return self.ap[:, b, 0:64].bitcast(BF16)


def carve(arena, off, shape, dtype):
    esz = {F32: 4, BF16: 2}[dtype]
    n = int(np.prod(shape))
    assert off % 32 == 0, off
    v = arena[:, off:off + n * esz].bitcast(dtype)
    if len(shape) == 1:
        return v
    pat = " ".join(f"d{i}" for i in range(len(shape)))
    kw = {f"d{i}": int(shape[i]) for i in range(len(shape))}
    return v.rearrange(f"p ({pat}) -> p {pat}", **kw)


def mm_group(P, pool, pairs, n_cols, waits=(), hold=False, rows=128):
    b, gw = pool.take(hold=hold)
    out = pool.bank(b, n_cols)
    if rows != 128:
        out = out[0:rows, :]
    n = len(pairs)
    for i, (l, r) in enumerate(pairs):
        w = tuple(waits) + tuple(gw) if i == 0 else ()
        P.emit("tensor",
               lambda e, l=l, r=r, s=(i == 0), t=(i == n - 1), out=out:
               e.matmul(out, l, r, start=s, stop=t),
               waits=w, incs=((("TE", 1),) if i == n - 1 else ()))
    return b, out


def mm_into(P, out, pairs, waits=(), start=True, stop=True, inc=True):
    n = len(pairs)
    for i, (l, r) in enumerate(pairs):
        w = tuple(waits) if i == 0 else ()
        P.emit("tensor",
               lambda e, l=l, r=r, s=(start and i == 0), t=(stop and i == n - 1),
               out=out: e.matmul(out, l, r, start=s, stop=t),
               waits=w,
               incs=((("TE", 1),) if (inc and i == n - 1) else ()))


def transpose128(P, pool, src_bf16, dst_bf16, ident, waits=()):
    b, gw = pool.take()
    out = pool.bank_bf16(b)
    P.emit("tensor", lambda e, out=out, s=src_bf16: e.transpose(out, s, ident),
           waits=tuple(waits) + tuple(gw), incs=(("TE", 1),))
    te = P.count("TE")
    P.emit("vector", lambda e, out=out, d=dst_bf16: e.tensor_copy(d, out),
           waits=(("TE", te),))
    pool.release(b, "VE")


WSPEC = {
    "L0_attn_w_qkv": (1024, 3072), "L0_attn_ln_w": (1024,),
    "L0_attn_w_proj": (1024, 1024),
    "L0_ffn_ln1_w": (1024,), "L0_ffn_c_attn_w": (1024, 3072),
    "L0_ffn_c_proj_w": (1024, 1024), "L0_ffn_ln2_w": (1024,),
    "L0_ffn_router_w": (1024, 8), "L0_ffn_w1": (1024, 16384),
    "L0_ffn_w2": (16384, 1024),
    "L1_attn_W1": (1024, 3072), "L1_attn_b1": (3072,),
    "L1_attn_W2": (1024, 1024), "L1_attn_b2": (1024,),
    "L1_ffn_controller": (1024, 32, 4), "L1_ffn_f1": (1024, 32, 4, 32),
    "L1_ffn_bias": (32, 4, 32), "L1_ffn_f2": (32, 4, 32, 1024),
}


class Mod:
    def __init__(self, nb=2):
        self.stack = ExitStack()
        nc = bass.Bass("TRN2", target_bir_lowering=False)
        self.nc = nc
        self.nb = nb
        self.x = nc.dram_tensor("x", [nb, T, CD], F32, kind="ExternalInput").ap()
        self.y = nc.dram_tensor("y", [nb, T, CD], F32, kind="ExternalOutput").ap()
        self.w = {n: nc.dram_tensor(n, list(s), F32, kind="ExternalInput").ap()
                  for n, s in WSPEC.items()}
        self.cdata = nc.dram_tensor("cdata", [128, 8], F32, kind="ExternalInput").ap()
        self.w1bf = nc.dram_tensor("w1bf", [1024, 16384], BF16, kind="Internal").ap()
        self.w2bf = nc.dram_tensor("w2bf", [16384, 1024], BF16, kind="Internal").ap()
        self.f1bf = nc.dram_tensor("f1bf", [1024, 128, 32], BF16, kind="Internal").ap()
        self.f2bf = nc.dram_tensor("f2bf", [128, 32, 1024], BF16, kind="Internal").ap()
        self._prepass_w = None
        self._prepass_f = None
        self.P = Prog(nc, self.stack)

        sb = lambda name, shape, dt: self.stack.enter_context(
            nc.sbuf_tensor(name, list(shape), dt))[:]
        self.ar = sb("arena", (128, 177 * KB), mybir.dt.uint8)
        self.S = carve(self.ar, 0, (NCC, T), BF16)
        self.off = {"S": 0, "R1": 32 * KB, "R2": 64 * KB, "R3": 96 * KB,
                    "R4": 129 * KB, "R5": 161 * KB}

        self.ident_b = sb("ident_b", (128, 128), BF16)
        self.ident_f = sb("ident_f", (8, 8), F32)
        self.ident_f128 = sb("ident_f128", (128, 128), F32)
        self.madd = sb("madd", (128, 896), BF16)
        self.tri01 = sb("tri01", (128, 896), BF16)
        self.ones_col_f = sb("ones_col_f", (128, 1), F32)
        self.ones_col_b = sb("ones_col_b", (128, 1), BF16)
        self.ones_row_f = sb("ones_row_f", (1, 128), F32)
        self.rows = sb("rows", (1, 1024), F32)
        self.vcols = sb("vcols", (128, 72), F32)
        self.consts = sb("consts", (128, 8), F32)
        self.tieq = sb("tieq", (128, 8), F32)
        self.sq = sb("sq", (128, 512), F32)
        self.sq2 = sb("sq2", (128, 512), F32)
        self.rcol = sb("rcol", (128, 1), F32)
        self.gscr = self.sq[:, 0:96].rearrange("p (a b) -> p a b", a=3)
        self.grows = self.sq[:, 96:112]
        self.hprime = carve(self.ar, self.off["R5"], (16, 256), BF16)
        self.gfm = carve(self.ar, self.off["R5"] + 8 * KB, (2048,), BF16)[0:8, :]
        self.onehot = carve(self.ar, self.off["R5"] + 12 * KB, (8, 128), BF16)[0:8, :, :]
        self.rwsb = carve(self.ar, self.off["R5"] + 14 * KB, (8, 8), BF16)
        self.vrow = carve(self.ar, self.off["R1"], (1024,), F32)[0:1, :]
        self.ostage = carve(self.ar, self.off["R1"] + 4 * KB, (1024,), F32)
        self.whb = sb("whb", (128, 8, 2, 2, 64), BF16)
        self.ea = sb("ea", (128, 384), BF16)
        ps = self.stack.enter_context(nc.psum_tensor("PS", [128, 8, 512], F32))
        self.pool = PsumPool(self.P, ps[:])
        self.vc_used = 0
        self.vc_cache = {}
        self._dbg_scr = None
        self._dbg_scr_ap = None
        self._wq = ()
        self.E = None
        self.WBUF = None

    def emit_setup(self):
        P = self.P
        g = "gpsimd"
        P.emit(g, lambda e: e.memset(self.ident_b, 0.0))
        P.emit(g, lambda e: e.affine_select(out=self.ident_b, in_=self.ident_b,
               compare_op=ALU.not_equal, fill=1.0, base=0, pattern=[[-1, 128]],
               channel_multiplier=1))
        P.emit(g, lambda e: e.memset(self.ident_f, 0.0))
        P.emit(g, lambda e: e.affine_select(out=self.ident_f, in_=self.ident_f,
               compare_op=ALU.not_equal, fill=1.0, base=0, pattern=[[-1, 8]],
               channel_multiplier=1))
        P.emit(g, lambda e: e.memset(self.ident_f128, 0.0))
        P.emit(g, lambda e: e.affine_select(out=self.ident_f128, in_=self.ident_f128,
               compare_op=ALU.not_equal, fill=1.0, base=0, pattern=[[-1, 128]],
               channel_multiplier=1))
        P.emit(g, lambda e: e.memset(self.madd, 0.0))
        P.emit(g, lambda e: e.affine_select(out=self.madd, in_=self.madd,
               compare_op=ALU.is_ge, fill=NEG, base=-384,
               pattern=[[1, 896]], channel_multiplier=-1))
        P.emit(g, lambda e: e.memset(self.tri01, 1.0))
        P.emit(g, lambda e: e.affine_select(out=self.tri01, in_=self.tri01,
               compare_op=ALU.is_ge, fill=0.0, base=-384,
               pattern=[[1, 896]], channel_multiplier=-1))
        for i, vv in enumerate((1e-6, 1e-5, -EXP_C_A, -EXP_C_B)):
            P.emit(g, lambda e, i=i, vv=vv: e.memset(self.consts[:, i:i + 1], vv))
        P.emit(g, lambda e: e.memset(self.ones_col_f, 1.0))
        P.emit(g, lambda e: e.memset(self.ones_col_b, 1.0))
        P.emit(g, lambda e: e.memset(self.ones_row_f, 1.0))

    def madd_o(self, o, n=512):
        return self.madd[:, 384 - 128 * o:384 - 128 * o + n]

    def load_vec_cols(self, wap, tag, scale=None, per_head=False):
        if tag in self.vc_cache:
            return self.vc_cache[tag]
        P, pool = self.P, self.pool
        n = 16 if per_head else NCC
        off = self.vc_used
        self.vc_used += n
        assert self.vc_used <= 72
        self.vc_cache[tag] = off
        te_pre = P.count("TE")
        P.emit("sync", lambda e, w=wap: e.dma_start(out=self.vrow,
               in_=w.unsqueeze(0)), waits=(("TE", te_pre),),
               incs=(("DX", 16),), auto=False)
        dx = P.count("DX")
        step = 64 if per_head else 128
        for i in range(n):
            b, gw = pool.take()
            out = pool.bank(b, 512)[0:step, 0:1]
            src = self.vrow[:, i * step:(i + 1) * step]
            P.emit("tensor", lambda e, out=out, src=src: e.transpose(
                out, src, self.ident_f[0:1, 0:1]),
                waits=(("DX", dx),) + tuple(gw), incs=(("TE", 1),))
            te = P.count("TE")
            dst = self.vcols[0:step, off + i:off + i + 1]
            if scale is None:
                P.emit("vector", lambda e, d=dst, s=out: e.tensor_copy(d, s),
                       waits=(("TE", te),))
            else:
                P.emit("vector", lambda e, d=dst, s=out, sc=scale:
                       e.tensor_scalar_mul(d, s, sc), waits=(("TE", te),))
            pool.release(b, "VE")
        return off

    def emit_stream_in(self, b):
        P, pool = self.P, self.pool
        xin = carve(self.ar, self.off["R4"], (NTT, CD), BF16)
        src = self.x[b].rearrange("(tt p) c -> p tt c", p=128)
        dws = []
        for ck in range(4):
            P.emit("gpsimd", lambda e, ck=ck: e.dma_start(
                out=xin[:, 4 * ck:4 * ck + 4, :], in_=src[:, 4 * ck:4 * ck + 4, :]),
                incs=(("DW", 16),), auto=False)
            dws.append(P.count("DW"))
        for tt in range(NTT):
            for cc in range(NCC):
                transpose128(P, pool, xin[:, tt, cc * 128:(cc + 1) * 128],
                             self.S[:, cc, tt * 128:(tt + 1) * 128], self.ident_b,
                             waits=(("DW", dws[tt // 4]),) if cc == 0 else ())

    def emit_stream_out(self, b):
        P, pool = self.P, self.pool
        dx_prev = 0
        for tt in range(NTT):
            first_evac = True
            for cc in range(NCC):
                bk, gw = pool.take()
                out = pool.bank_bf16(bk)
                src = self.S[:, cc, tt * 128:(tt + 1) * 128]
                P.emit("tensor", lambda e, out=out, s=src: e.transpose(
                    out, s, self.ident_b), waits=tuple(gw), incs=(("TE", 1),))
                te = P.count("TE")
                dst = self.ostage[:, cc * 128:(cc + 1) * 128]
                w = (("TE", te),)
                if first_evac and dx_prev:
                    w = w + (("DX", dx_prev),)
                    first_evac = False
                P.emit("scalar", lambda e, d=dst, s=out: e.copy(d, s), waits=w)
                pool.release(bk, "SE")
            se = P.count("SE")
            dst = self.y[b, tt * 128:(tt + 1) * 128, :]
            P.emit("sync", lambda e, d=dst: e.dma_start(out=d, in_=self.ostage),
                   waits=(("SE", se),), incs=(("DX", 16),), auto=False)
            dx_prev = P.count("DX")

    def emit_ln_stats(self, chunks, bf16_chunks, eps):
        P, pool = self.P, self.pool
        bs, gs = pool.take(hold=True)
        bq, gq = pool.take(hold=True)
        sm = pool.ap[0:1, bs, :]
        ss = pool.ap[0:1, bq, :]
        ones = self.ones_col_b if bf16_chunks else self.ones_col_f
        te_sq = [0, 0]
        ve0 = P.count("VE")
        for i, ch in enumerate(chunks):
            sqb = self.sq if i % 2 == 0 else self.sq2
            w = (("TE", te_sq[i % 2]),) if te_sq[i % 2] else ()
            P.emit("vector", lambda e, ch=ch, sqb=sqb: e.tensor_mul(sqb, ch, ch),
                   waits=w)
            ve = P.count("VE")
            mm_into(P, sm, [(ones, ch)],
                    waits=tuple(gs) + ((("VE", ve0),) if i == 0 else ()),
                    start=(i == 0), stop=(i == 7), inc=True)
            gs = ()
            mm_into(P, ss, [(self.ones_col_f, sqb)],
                    waits=(("VE", ve),) + tuple(gq),
                    start=(i == 0), stop=(i == 7), inc=True)
            gq = ()
            te_sq[i % 2] = P.count("TE")
        te = P.count("TE")
        mean = self.rows[0:1, 0:512]
        var = self.rows[0:1, 512:1024]
        rstd = mean
        tmp = self.sq[0:1, 0:512]
        P.emit("scalar", lambda e, sm=sm: e.mul(mean, sm, 1.0 / CD), waits=(("TE", te),))
        P.emit("scalar", lambda e, ss=ss: e.mul(var, ss, 1.0 / CD))
        pool.release(bs, "SE")
        pool.release(bq, "SE")
        se = P.count("SE")
        bm, gwm = pool.take(hold=True)
        mm_into(P, pool.bank(bm), [(self.ones_row_f, mean)],
                waits=(("SE", se),) + tuple(gwm))
        te_bm = P.count("TE")
        P.emit("vector", lambda e, tmp=tmp: e.tensor_mul(tmp, mean, mean),
               waits=(("SE", se),))
        P.emit("vector", lambda e, tmp=tmp: e.tensor_sub(var, var, tmp))
        ve = P.count("VE")
        epsc = self.consts[0:1, 0:1] if eps == 1e-6 else self.consts[0:1, 1:2]
        P.emit("scalar", lambda e, epsc=epsc: e.activation(var, var, AF.Ln, bias=epsc),
               waits=(("VE", ve), ("TE", te_bm)))
        P.emit("scalar", lambda e, rstd=rstd: e.activation(rstd, var, AF.Exp, scale=-0.5))
        se = P.count("SE")
        br, gwr = pool.take(hold=True)
        mm_into(P, pool.bank(br), [(self.ones_row_f, rstd)],
                waits=(("SE", se),) + tuple(gwr))
        return bm, br

    def emit_ln_apply(self, chunks, out_chunks, bm, br, scr, scale_off=None):
        P, pool = self.P, self.pool
        te = P.count("TE")
        mb = pool.bank(bm)
        rb = pool.bank(br)
        for i, (ch, outv) in enumerate(zip(chunks, out_chunks)):
            t = scr[:, i, :]
            P.emit("vector", lambda e, t=t, ch=ch, mb=mb: e.tensor_sub(t, ch, mb),
                   waits=(("TE", te),) if i == 0 else ())
            if scale_off is None:
                P.emit("vector", lambda e, o=outv, t=t, rb=rb: e.tensor_mul(o, t, rb))
            else:
                sc = self.vcols[:, scale_off + i:scale_off + i + 1]
                P.emit("vector", lambda e, o=outv, t=t, rb=rb, sc=sc:
                       e.scalar_tensor_tensor(o, t, sc, rb, ALU.mult, ALU.mult))
        pool.release(bm, "VE")
        pool.release(br, "VE")

    def emit_postln(self, scr, out_chunks, scale_off=None, eps=1e-5):
        chunks = [scr[:, jc, :] for jc in range(NCC)]
        bm, br = self.emit_ln_stats(chunks, False, eps)
        self.emit_ln_apply(chunks, out_chunks, bm, br, scr, scale_off=scale_off)



def dma_w(m, dst, src):
    P = m.P
    te = P.count("TE")
    P.emit("gpsimd", lambda e, d=dst, s=src: e.dma_start(out=d, in_=s),
           waits=(("TE", te),), incs=(("DW", 16),), auto=False)
    return P.count("DW")


def _vtm_make(m, wbuf, src, vtm, bias_row=None):
    P, pool = m.P, m.pool
    ge_pre = P.count("GE")
    P.emit("gpsimd", lambda e, v=vtm: e.memset(
        v.rearrange("p a b c -> p (a b c)"), 1.0))
    ge = P.count("GE")
    first = True
    for tt in range(NTT):
        for half in range(2):
            pairs = [(src[:, cc, tt * 128:(tt + 1) * 128],
                      wbuf[:, cc, half * 512:(half + 1) * 512])
                     for cc in range(NCC)]
            if bias_row is not None:
                pairs.append((m.ones_row_f, bias_row[0:1, half * 512:(half + 1) * 512]))
            bk, out = mm_group(P, pool, pairs, 512, waits=m._wq)
            m._wq = ()
            te = P.count("TE")
            dst = vtm[:, tt, 8 * half:8 * half + 8, 0:64]
            w = (("TE", te),)
            if first:
                w = w + (("GE", ge),)
                first = False
            P.emit("vector", lambda e, d=dst, s=out.rearrange("p (h d) -> p h d", h=8):
                   e.tensor_copy(d, s), waits=w)
            pool.release(bk, "VE")


def _attn_core(m, h, qv, kv, vtm, y_tok, softmax_c):
    P, pool = m.P, m.pool
    E = m.E
    se_kc = [0] * 16
    ge_kc = [0] * 16
    for q5 in range(4):
        for kc in range(4 * q5 + 4):
            bk, out = mm_group(
                P, pool, [(kv[0:64, kc * 128:(kc + 1) * 128],
                           qv[0:64, q5 * 512:(q5 + 1) * 512])], 512, waits=m._wq)
            m._wq = ()
            te = P.count("TE")
            o = kc - 4 * q5
            w = (("TE", te),)
            dst = E[:, kc, :]
            if softmax_c is not None:
                if o >= 0:
                    P.emit("vector", lambda e, out=out, o=o: e.tensor_add(
                        out, out, m.madd_o(o)), waits=w)
                    w = (("VE", P.count("VE")),)
                cb = m.consts[:, 3:4]
                P.emit("scalar", lambda e, d=dst, s=out, cb=cb:
                       e.activation(d, s, AF.Exp, bias=cb), waits=w)
                se_kc[kc] = P.count("SE")
                pool.release(bk, "SE")
            else:
                if o >= 0:
                    P.emit("vector", lambda e, d=dst, s=out, o=o: e.tensor_tensor(
                        d, s, m.tri01[:, 384 - 128 * o:896 - 128 * o], ALU.mult),
                        waits=w)
                    pool.release(bk, "VE")
                    ge_kc[kc] = ("VE", P.count("VE"))
                else:
                    P.emit("scalar", lambda e, d=dst, s=out: e.activation(
                        d, s, AF.Copy), waits=w)
                    pool.release(bk, "SE")
                    se_kc[kc] = P.count("SE")
                    ge_kc[kc] = None
        ve = P.count("VE")
        for qt in range(4 * q5, 4 * q5 + 4):
            ql = qt - 4 * q5
            pairs = [(E[:, kc, ql * 128:(ql + 1) * 128], vtm[:, kc, h, 0:65])
                     for kc in range(qt + 1)]
            bk, gw = pool.take()
            out = pool.bank(bk, 65)
            if softmax_c is None:
                wav = (("VE", ve),) + tuple(gw)
                if ge_kc[qt] is not None:
                    wav = wav + (ge_kc[qt],)
                else:
                    wav = wav + (("SE", se_kc[qt]),)
            else:
                wav = (("SE", se_kc[qt]), ("VE", ve)) + tuple(gw)
            mm_into(P, out, pairs, waits=wav)
            te = P.count("TE")
            dst = y_tok[:, qt, h * 64:(h + 1) * 64]
            if softmax_c is not None:
                P.emit("vector", lambda e, out=out: e.reciprocal(
                    m.rcol, out[:, 64:65]),
                    waits=(("TE", te), ("SE", P.count("SE"))))
                ve_r = P.count("VE")
                P.emit("scalar", lambda e, d=dst, out=out: e.activation(
                    d, out[:, 0:64], AF.Copy, scale=m.rcol), waits=(("VE", ve_r),))
                pool.release(bk, "SE")
            else:
                P.emit("vector", lambda e, d=dst, out=out: e.tensor_copy(
                    d, out[:, 0:64]), waits=(("TE", te),))
                pool.release(bk, "VE")


def _proj_epilogue(m, wbuf, y_tok, epi_chunk, epi_tc, pipeline=True):
    P, pool = m.P, m.pool
    yfm = carve(m.ar, m.off["R3"], (NCC, 512), BF16)
    w0 = (("SE", P.count("SE")), ("VE", P.count("VE")))
    for tc in range(NTC):
        for i, tt in enumerate(range(4 * tc, 4 * tc + 4)):
            for cc in range(NCC):
                transpose128(P, pool, y_tok[:, tt, cc * 128:(cc + 1) * 128],
                             yfm[:, cc, i * 128:(i + 1) * 128], m.ident_b,
                             waits=w0)
                w0 = ()
        ve = P.count("VE")
        for jc in range(NCC):
            pairs = [(wbuf[:, cc, jc * 128:(jc + 1) * 128],
                      yfm[:, cc, :]) for cc in range(NCC)]
            bk, out = mm_group(P, pool, pairs, 512,
                               waits=m._wq + ((("VE", ve),) if jc == 0 else ()))
            m._wq = ()
            epi_chunk(tc, jc, out)
            pool.release(bk, "VE")
        if not pipeline:
            epi_tc(tc)
        elif tc > 0:
            epi_tc(tc - 1)
    if pipeline:
        epi_tc(NTC - 1)



def emit_block_a(m, b):
    P, pool = m.P, m.pool
    w = m.w
    wq = w["L0_attn_w_qkv"].rearrange("(cc p) j -> p cc j", p=128)
    wp = w["L0_attn_w_proj"].rearrange("(cc p) j -> p cc j", p=128)
    WA = carve(m.ar, m.off["R5"], (NCC, 1024), BF16)
    WB = carve(m.ar, m.off["R4"], (NCC, 1024), BF16)
    qfm = carve(m.ar, m.off["R1"], (NCC, T), BF16)
    kfm = carve(m.ar, m.off["R2"], (NCC, T), BF16)
    vtm = carve(m.ar, m.off["R3"], (NTT, 16, 65), BF16)
    y_tok = carve(m.ar, m.off["R4"], (NTT, CD), BF16)

    lnw_off = m.load_vec_cols(w["L0_attn_ln_w"], "a_lnw")
    lnwq_off = m.load_vec_cols(w["L0_attn_ln_w"], "a_lnwq", scale=0.125)
    ve_pre = P.count("VE")

    dw = dma_w(m, WA, wq[:, :, 0:1024])
    dwk = dma_w(m, WB, wq[:, :, 1024:2048])
    for part, (dst, coloff, wbuf) in enumerate(
            ((qfm, lnwq_off, WA), (kfm, lnw_off, WB))):
        m._wq = (("DW", dw if part == 0 else dwk), ("VE", ve_pre))
        if part == 1:
            dw_v = dma_w(m, WA, wq[:, :, 2048:3072])
        for tc in range(NTC):
            bss, gss = pool.take(hold=True)
            ss = pool.ap[0:1, bss, :]
            te_sq = [0, 0]
            for jc in range(NCC):
                pairs = [(wbuf[:, cc, jc * 128:(jc + 1) * 128],
                          m.S[:, cc, tc * 512:(tc + 1) * 512]) for cc in range(NCC)]
                bk, out = mm_group(P, pool, pairs, 512, waits=m._wq)
                m._wq = ()
                te = P.count("TE")
                ch = dst[:, jc, tc * 512:(tc + 1) * 512]
                P.emit("vector", lambda e, d=ch, s=out: e.tensor_copy(d, s),
                       waits=(("TE", te),))
                pool.release(bk, "VE")
                sqb = m.sq if jc % 2 == 0 else m.sq2
                wsq = (("TE", te_sq[jc % 2]),) if te_sq[jc % 2] else ()
                P.emit("vector", lambda e, c=ch, sqb=sqb: e.tensor_mul(sqb, c, c),
                       waits=wsq)
                ve = P.count("VE")
                mm_into(P, ss, [(m.ones_col_f, sqb)],
                        waits=(("VE", ve),) + tuple(gss),
                        start=(jc == 0), stop=(jc == 7), inc=True)
                gss = ()
                te_sq[jc % 2] = P.count("TE")
            rstd = m.rows[0:1, 512:1024]
            P.emit("scalar", lambda e, ss=ss: e.activation(
                rstd, ss, AF.Ln, scale=1.0 / CD, bias=m.consts[0:1, 0:1]),
                waits=(("TE", max(te_sq)),))
            P.emit("scalar", lambda e: e.activation(rstd, rstd, AF.Exp, scale=-0.5))
            pool.release(bss, "SE")
            se = P.count("SE")
            br, gwr = pool.take(hold=True)
            mm_into(P, pool.bank(br), [(m.ones_row_f, rstd)],
                    waits=(("SE", se),) + tuple(gwr))
            te = P.count("TE")
            for jc in range(NCC):
                ch = dst[:, jc, tc * 512:(tc + 1) * 512]
                sc = m.vcols[:, coloff + jc:coloff + jc + 1]
                P.emit("vector", lambda e, c=ch, sc=sc, rb=pool.bank(br):
                       e.scalar_tensor_tensor(c, c, sc, rb, ALU.mult, ALU.mult),
                       waits=(("TE", te),) if jc == 0 else ())
            pool.release(br, "VE")
    dbg_dump(m, f"q{b}", qfm)
    dbg_dump(m, f"k{b}", kfm)
    m._wq = (("DW", dw_v),)
    _vtm_make(m, WA, m.S, vtm)
    dw = dma_w(m, WA, wp)
    dbg_dump(m, f"v{b}", vtm)
    ve_v = P.count("VE")
    first = True
    for h in range(H):
        ro, hc = (h % 2) * 64, h // 2
        qv = qfm[ro:ro + 64, hc, :] if ro == 0 else qfm[64:128, hc, :]
        kv = kfm[ro:ro + 64, hc, :] if ro == 0 else kfm[64:128, hc, :]
        for wdw in range(8):
            kt0, kt1 = 2 * wdw, 2 * wdw + 1
            wx = m._wq + ((("VE", ve_v),) if first else ())
            m._wq = ()
            first = False
            bk, out = mm_group(P, pool, [(kv[:, kt0 * 128:kt0 * 128 + 128],
                                          qv[:, wdw * 256:wdw * 256 + 256])], 256,
                               waits=wx)
            te = P.count("TE")
            P.emit("vector", lambda e, out=out: e.tensor_add(
                out[:, 0:128], out[:, 0:128], m.madd_o(0, 128)),
                waits=(("TE", te),))
            vew = P.count("VE")
            P.emit("scalar", lambda e, out=out: e.activation(
                m.ea[:, 0:256], out, AF.Exp, bias=m.consts[:, 2:3]), waits=(("VE", vew),))
            pool.release(bk, "SE")
            bk2, out2 = mm_group(P, pool, [(kv[:, kt1 * 128:kt1 * 128 + 128],
                                            qv[:, wdw * 256 + 128:wdw * 256 + 256])],
                                 128)
            te = P.count("TE")
            P.emit("vector", lambda e, out2=out2: e.tensor_add(
                out2, out2, m.madd_o(0, 128)), waits=(("TE", te),))
            vew = P.count("VE")
            P.emit("scalar", lambda e, out2=out2: e.activation(
                m.ea[:, 256:384], out2, AF.Exp, bias=m.consts[:, 2:3]), waits=(("VE", vew),))
            pool.release(bk2, "SE")
            if h == 0 and wdw == 0:
                dbg_dump(m, f"ea{b}", m.ea)
            se = P.count("SE")
            for qt, pairs in ((kt0, [(m.ea[:, 0:128], vtm[:, kt0, h, 0:65])]),
                              (kt1, [(m.ea[:, 128:256], vtm[:, kt0, h, 0:65]),
                                     (m.ea[:, 256:384], vtm[:, kt1, h, 0:65])])):
                bky, gwy = pool.take()
                outy = pool.bank(bky, 65)
                mm_into(P, outy, pairs, waits=(("SE", se),) + tuple(gwy))
                te = P.count("TE")
                dst = y_tok[:, qt, h * 64:(h + 1) * 64]
                P.emit("vector", lambda e, outy=outy: e.reciprocal(
                    m.rcol, outy[:, 64:65]),
                    waits=(("TE", te), ("SE", P.count("SE"))))
                ve_r = P.count("VE")
                P.emit("scalar", lambda e, d=dst, outy=outy: e.activation(
                    d, outy[:, 0:64], AF.Copy, scale=m.rcol), waits=(("VE", ve_r),))
                pool.release(bky, "SE")
    dbg_dump(m, f"y{b}", y_tok)
    m._wq = (("DW", dw),)
    scr2d = carve(m.ar, m.off["R2"], (2, NCC, 512), F32)

    def epi_chunk(tc, jc, psum):
        te = P.count("TE")
        xch = m.S[:, jc, tc * 512:(tc + 1) * 512]
        P.emit("vector", lambda e, p=psum, x=xch, d=scr2d[:, tc % 2, jc, :]:
               e.tensor_add(d, p, x), waits=(("TE", te),))

    def epi_tc(tc):
        outs = [m.S[:, jc, tc * 512:(tc + 1) * 512] for jc in range(NCC)]
        m.emit_postln(scr2d[:, tc % 2, :, :], outs)

    _proj_epilogue(m, WA, y_tok, epi_chunk, epi_tc)



def _emit_whb_dma(m, wsrc, h, qoff, koff):
    P = m.P
    dst = m.whb[:, :, h % 2, :, :]
    te = P.count("TE")
    P.emit("gpsimd", lambda e, d=dst[:, :, 0, :],
           s=wsrc[:, :, qoff + h * 64:qoff + h * 64 + 64]:
           e.dma_start(out=d, in_=s), waits=(("TE", te),),
           incs=(("DW", 16),), auto=False)
    P.emit("gpsimd", lambda e, d=dst[:, :, 1, :],
           s=wsrc[:, :, koff + h * 64:koff + h * 64 + 64]:
           e.dma_start(out=d, in_=s), incs=(("DW", 16),), auto=False)
    return P.count("DW")


def _perh_qk(m, wsrc, h, qh, kh, src, qoff, koff, dw, dw_next_cb=None,
             qscale=None, bq_col=None, bk_col=None):
    P, pool = m.P, m.pool
    par = h % 2
    dst = m.whb[:, :, par, :, :]
    for i, (out_t, col) in enumerate(((qh, bq_col), (kh, bk_col))):
        for tc in range(NTC):
            pairs = [(dst[:, cc, i, :], src[:, cc, tc * 512:(tc + 1) * 512])
                     for cc in range(NCC)]
            wx = ((("DW", dw),) + m._wq) if (tc == 0 and i == 0) else ()
            bk_, out = mm_group(P, pool, pairs, 512, waits=wx, rows=64)
            if tc == 0 and i == 0:
                m._wq = ()
            te = P.count("TE")
            d = out_t[0:64, tc * 512:(tc + 1) * 512]
            s64 = out
            if col is not None:
                P.emit("vector", lambda e, d=d, s=s64, c=col:
                       e.scalar_tensor_tensor(d, s, 1.0,
                                              c.broadcast_to([64, 512]),
                                              ALU.mult, ALU.add),
                       waits=(("TE", te),))
                pool.release(bk_, "VE")
            elif i == 0 and qscale is not None:
                P.emit("vector", lambda e, d=d, s=s64, q=qscale:
                       e.tensor_scalar_mul(d, s, q), waits=(("TE", te),))
                pool.release(bk_, "VE")
            else:
                P.emit("vector", lambda e, d=d, s=s64: e.tensor_copy(d, s),
                       waits=(("TE", te),))
                pool.release(bk_, "VE")
            if i == 1 and tc == 0 and dw_next_cb is not None:
                dw_next_cb()


def emit_block_b_attn(m, b):
    P, pool = m.P, m.pool
    w = m.w
    wsrc = w["L0_ffn_c_attn_w"].rearrange("(cc p) j -> p cc j", p=128)
    wp = w["L0_ffn_c_proj_w"].rearrange("(cc p) j -> p cc j", p=128)
    WA = carve(m.ar, m.off["R5"], (NCC, 1024), BF16)
    h1 = carve(m.ar, m.off["R1"], (NCC, T), BF16)
    qh = carve(m.ar, m.off["R2"], (T,), BF16)
    kh = carve(m.ar, m.off["R2"] + 4 * KB, (T,), BF16)
    m.E = carve(m.ar, m.off["R2"] + 8 * KB, (16, 512), BF16)
    vtm = carve(m.ar, m.off["R3"], (NTT, 16, 65), BF16)
    y_tok = carve(m.ar, m.off["R4"], (NTT, CD), BF16)
    scr = carve(m.ar, m.off["R2"] + 8 * KB, (NCC, 512), F32)

    m._h1ap = h1
    ln1_off = m.load_vec_cols(w["L0_ffn_ln1_w"], "b_ln1")
    ln2_off = m.load_vec_cols(w["L0_ffn_ln2_w"], "b_ln2")
    ve_pre = P.count("VE")
    for tc in range(NTC):
        chunks = [m.S[:, cc, tc * 512:(tc + 1) * 512] for cc in range(NCC)]
        outs = [h1[:, cc, tc * 512:(tc + 1) * 512] for cc in range(NCC)]
        bm, br = m.emit_ln_stats(chunks, True, 1e-5)
        if tc == 0:
            P.wait("vector", "VE", ve_pre)
        m.emit_ln_apply(chunks, outs, bm, br, scr, scale_off=ln1_off)
    dw = dma_w(m, WA, wsrc[:, :, 2048:3072])
    ve = P.count("VE")
    m._wq = (("DW", dw), ("VE", ve))
    _vtm_make(m, WA, h1, vtm)
    dw = None
    dwh = {0: _emit_whb_dma(m, wsrc, 0, 0, 1024)}

    for h in range(H):
        m._wq = ()

        def pref(h=h):
            if h + 1 < H:
                dwh[h + 1] = _emit_whb_dma(m, wsrc, h + 1, 0, 1024)
            if m._prepass_w is None or m._prepass_w[1] == 0:
                pass
        _perh_qk(m, wsrc, h, qh, kh, h1, 0, 1024, dwh[h], pref, qscale=0.125)
        if b == 0:
            if h < 8:
                P.emit("gpsimd", lambda e, h=h: e.dma_start(
                    out=m.w1bf[:, h * 2048:(h + 1) * 2048],
                    in_=m.w["L0_ffn_w1"][:, h * 2048:(h + 1) * 2048]),
                    incs=(("DW", 16),), auto=False)
            else:
                hh = h - 8
                P.emit("gpsimd", lambda e, hh=hh: e.dma_start(
                    out=m.w2bf[hh * 2048:(hh + 1) * 2048, :],
                    in_=m.w["L0_ffn_w2"][hh * 2048:(hh + 1) * 2048, :]),
                    incs=(("DW", 16),), auto=False)
            if h == 15:
                m._prepass_w = ("DW", P.count("DW"))
        if h == 0:
            dw = dma_w(m, WA, wp)
        _attn_core(m, h, qh, kh, vtm, y_tok, EXP_C_B)
    m._wq = (("DW", dw),)
    h2 = h1

    scr2d = carve(m.ar, m.off["R2"], (2, NCC, 512), F32)

    def epi_chunk(tc, jc, psum):
        te = P.count("TE")
        xch = m.S[:, jc, tc * 512:(tc + 1) * 512]
        P.emit("vector", lambda e, p=psum, x=xch, d=scr2d[:, tc % 2, jc, :]:
               e.tensor_add(d, p, x), waits=(("TE", te),))
        P.emit("vector", lambda e, x=xch, d=scr2d[:, tc % 2, jc, :]:
               e.tensor_add(x, d, x))

    def epi_tc(tc):
        outs = [h2[:, jc, tc * 512:(tc + 1) * 512] for jc in range(NCC)]
        m.emit_postln(scr2d[:, tc % 2, :, :], outs, scale_off=ln2_off)

    _proj_epilogue(m, WA, y_tok, epi_chunk, epi_tc)


def emit_block_c(m, b):
    P, pool = m.P, m.pool
    w = m.w
    wsrc = w["L1_attn_W1"].rearrange("(cc p) j -> p cc j", p=128)
    wp = w["L1_attn_W2"].rearrange("(cc p) j -> p cc j", p=128)
    WA = carve(m.ar, m.off["R5"], (NCC, 1024), BF16)
    qh = carve(m.ar, m.off["R2"], (T,), BF16)
    kh = carve(m.ar, m.off["R2"] + 4 * KB, (T,), BF16)
    m.E = carve(m.ar, m.off["R2"] + 8 * KB, (16, 512), BF16)
    vtm = carve(m.ar, m.off["R3"], (NTT, 16, 65), BF16)
    y_tok = carve(m.ar, m.off["R4"], (NTT, CD), BF16)
    scr2d = carve(m.ar, m.off["R2"], (2, NCC, 512), F32)

    b1k_off = m.load_vec_cols(w["L1_attn_b1"][0:1024], "c_b1k", per_head=True)
    b1q_off = m.load_vec_cols(w["L1_attn_b1"][1024:2048], "c_b1q", per_head=True)
    b2_off = m.load_vec_cols(w["L1_attn_b2"], "c_b2")
    te = P.count("TE")
    P.emit("sync", lambda e: e.dma_start(out=m.vrow,
           in_=w["L1_attn_b1"][2048:3072].unsqueeze(0)), waits=(("TE", te),),
           incs=(("DX", 16),), auto=False)
    dx = P.count("DX")
    ve_pre = P.count("VE")

    dw = dma_w(m, WA, wsrc[:, :, 2048:3072])
    m._wq = (("DW", dw), ("DX", dx), ("VE", ve_pre))
    _vtm_make(m, WA, m.S, vtm, bias_row=m.vrow)
    dw = None
    dwh = {0: _emit_whb_dma(m, wsrc, 0, 1024, 0)}
    for h in range(H):
        m._wq = ()
        bqc = m.vcols[0:64, b1q_off + h:b1q_off + h + 1]
        bkc = m.vcols[0:64, b1k_off + h:b1k_off + h + 1]

        def pref(h=h):
            if h + 1 < H:
                dwh[h + 1] = _emit_whb_dma(m, wsrc, h + 1, 1024, 0)
        _perh_qk(m, wsrc, h, qh, kh, m.S, 1024, 0, dwh[h], pref,
                 bq_col=bqc, bk_col=bkc)
        if b == 0 and h < 4:
            P.emit("gpsimd", lambda e, h=h: e.dma_start(
                out=m.f1bf.rearrange("c es f -> c (es f)")[
                    h * 256:(h + 1) * 256, :],
                in_=m.w["L1_ffn_f1"].rearrange("c e s2 f -> c (e s2 f)")[
                    h * 256:(h + 1) * 256, :]), incs=(("DW", 16),), auto=False)
        if b == 0 and 4 <= h < 8:
            hh = h - 4
            P.emit("gpsimd", lambda e, hh=hh: e.dma_start(
                out=m.f2bf.rearrange("es f c -> (es f) c")[
                    hh * 1024:(hh + 1) * 1024, :],
                in_=m.w["L1_ffn_f2"].rearrange("e s2 f c -> (e s2 f) c")[
                    hh * 1024:(hh + 1) * 1024, :]), incs=(("DW", 16),), auto=False)
        if b == 0 and h == 8:
            m._prepass_f = ("DW", P.count("DW"))
        if h == 0:
            dw = dma_w(m, WA, wp)
        _attn_core(m, h, qh, kh, vtm, y_tok, None)
    m._wq = (("DW", dw),)

    def epi_chunk(tc, jc, psum):
        te = P.count("TE")
        c = m.vcols[:, b2_off + jc:b2_off + jc + 1]
        P.emit("vector", lambda e, p=psum, d=scr2d[:, tc % 2, jc, :], c=c:
               e.scalar_tensor_tensor(d, p, 1.0, c.broadcast_to([128, 512]),
                                      ALU.mult, ALU.add), waits=(("TE", te),))
        xch = m.S[:, jc, tc * 512:(tc + 1) * 512]
        P.emit("vector", lambda e, x=xch, d=scr2d[:, tc % 2, jc, :]:
               e.tensor_add(d, d, x))

    def epi_tc(tc):
        outs = [m.S[:, jc, tc * 512:(tc + 1) * 512] for jc in range(NCC)]
        m.emit_postln(scr2d[:, tc % 2, :, :], outs)

    _proj_epilogue(m, WA, y_tok, epi_chunk, epi_tc)


def emit_block_b(m, b):
    emit_block_b_attn(m, b)
    m.P.fence()
    dbg_dump(m, f"bs{b}", m.S)
    dbg_dump(m, f"bh2_{b}", m._h1ap)
    dbg_dump(m, f"bgf{b}", m.gfm) if False else None
    emit_block_b_moe(m, b)
    dbg_dump(m, f"bgate{b}", m.gfm)


def emit_block_b_moe(m, b):
    P, pool = m.P, m.pool
    w = m.w
    h2 = carve(m.ar, m.off["R1"], (NCC, T), BF16)
    y_acc = carve(m.ar, m.off["R3"], (NCC, 1024), F32)
    w1b = carve(m.ar, m.off["R2"], (NCC, T), BF16)
    w2b = carve(m.ar, m.off["R4"], (16, CD), BF16)
    w1r = m.w1bf.rearrange("(cc p) f -> p cc f", p=128)
    w2r = m.w2bf.rearrange("(fc p) c -> p fc c", p=128)
    pw = m._prepass_w if m._prepass_w else ("DW", 0)

    te0 = P.count("TE")
    P.emit("gpsimd", lambda e: e.dma_start(
        out=m.rwsb, in_=w["L0_ffn_router_w"].rearrange("(cc p) e -> p cc e", p=128)),
        waits=(("TE", te0),), incs=(("DW", 16),), auto=False)
    dw_rw = P.count("DW")
    for e8 in range(8):
        P.emit("gpsimd", lambda e, e8=e8: e.memset(m.onehot[:, e8, :], 1.0))
        P.emit("gpsimd", lambda e, e8=e8: e.affine_select(
            out=m.onehot[:, e8, :], in_=m.onehot[:, e8, :],
            compare_op=ALU.is_equal, fill=0.0, base=-e8,
            pattern=[[0, 128]], channel_multiplier=1))
    for tc in range(NTC):
        bk_rl, gwl = pool.take()
        rl = pool.ap[0:8, bk_rl, :]
        mm_into(P, rl, [(m.rwsb[:, cc, :], h2[:, cc, tc * 512:(tc + 1) * 512])
                        for cc in range(NCC)],
                waits=(("DW", dw_rw),) + tuple(gwl))
        te = P.count("TE")
        P.emit("scalar", lambda e, rl=rl: e.copy(m.sq[0:8, 0:512], rl),
               waits=(("TE", te),))
        pool.release(bk_rl, "SE")
        se = P.count("SE")
        ltm_sb = m.sq[:, 112:144]
        for i in range(4):
            bk_t, gwt = pool.take()
            outv = pool.ap[:, bk_t, 0:8]
            P.emit("tensor", lambda e, o=outv,
                   s=m.sq[0:8, i * 128:(i + 1) * 128]:
                   e.transpose(o, s, m.ident_f),
                   waits=(("SE", se),) + tuple(gwt),
                   incs=(("TE", 1),))
            te = P.count("TE")
            P.emit("vector", lambda e, o=outv, d=ltm_sb[:, i * 8:(i + 1) * 8]:
                   e.tensor_copy(d, o), waits=(("TE", te),))
            pool.release(bk_t, "VE")
        ltm = ltm_sb.rearrange("p (a b) -> p a b", a=4)

        def vv(fn):
            P.emit("vector", fn, waits=(("VE", P.count("VE")),))
        m1 = m.grows[:, 0:4]
        m2 = m.grows[:, 4:8]
        p1 = m.grows[:, 8:12]
        p2 = m.grows[:, 12:16]
        eq1 = m.gscr[:, 0, :].rearrange("p (a b) -> p a b", a=4)
        lm = m.gscr[:, 1, :].rearrange("p (a b) -> p a b", a=4)
        gg = m.gscr[:, 2, 0:32].bitcast(BF16)[:, 0:32].rearrange(
            "p (a b) -> p a b", a=4)
        t1 = m.gscr[:, 1, 0:16].rearrange("p (a b) -> p a b", a=4)
        t2 = m.gscr[:, 1, 16:24].rearrange("p (a b) -> p a b", a=4)
        vv(lambda e, ltm=ltm, t1=t1: e.tensor_max(
            t1, ltm[:, :, 0::2], ltm[:, :, 1::2]))
        vv(lambda e, t1=t1, t2=t2: e.tensor_max(
            t2, t1[:, :, 0::2], t1[:, :, 1::2]))
        vv(lambda e, t2=t2, m1=m1: e.tensor_max(
            m1.rearrange("p (a o) -> p a o", o=1), t2[:, :, 0:1], t2[:, :, 1:2]))
        vv(lambda e, ltm=ltm, eq1=eq1, m1=m1: e.tensor_tensor(
            eq1, ltm, m1.rearrange("p (a o) -> p a o", o=1).broadcast_to([128, 4, 8]),
            ALU.is_equal))
        vv(lambda e, ltm=ltm, eq1=eq1, lm=lm: e.scalar_tensor_tensor(
            lm, eq1, NEG, ltm, ALU.mult, ALU.add))
        u1 = m.gscr[:, 2, 0:16].rearrange("p (a b) -> p a b", a=4)
        u2 = m.gscr[:, 2, 16:24].rearrange("p (a b) -> p a b", a=4)
        vv(lambda e, lm=lm, u1=u1: e.tensor_max(
            u1, lm[:, :, 0::2], lm[:, :, 1::2]))
        vv(lambda e, u1=u1, u2=u2: e.tensor_max(
            u2, u1[:, :, 0::2], u1[:, :, 1::2]))
        vv(lambda e, u2=u2, m2=m2: e.tensor_max(
            m2.rearrange("p (a o) -> p a o", o=1), u2[:, :, 0:1], u2[:, :, 1:2]))
        vv(lambda e, lm=lm, m2=m2: e.tensor_tensor(
            lm, lm, m2.rearrange("p (a o) -> p a o", o=1).broadcast_to([128, 4, 8]),
            ALU.is_equal))
        vv(lambda e, m1=m1, m2=m2: e.tensor_sub(m2, m2, m1))
        ve = P.count("VE")
        P.emit("scalar", lambda e, p1=p1, m2=m2: e.activation(
            p1, m2, AF.Sigmoid, scale=-1.0), waits=(("VE", ve),))
        se = P.count("SE")
        P.emit("vector", lambda e, p1=p1, p2=p2: e.tensor_scalar(
            p2, p1, -1.0, 1.0, ALU.mult, ALU.add),
            waits=(("SE", se), ("VE", P.count("VE"))))
        vv(lambda e, eq1=eq1, p1=p1: e.tensor_mul(
            eq1, eq1, p1.rearrange("p (a o) -> p a o", o=1).broadcast_to([128, 4, 8])))
        vv(lambda e, lm=lm, p2=p2: e.tensor_mul(
            lm, lm, p2.rearrange("p (a o) -> p a o", o=1).broadcast_to([128, 4, 8])))
        vv(lambda e, gg=gg, eq1=eq1, lm=lm: e.tensor_add(gg, eq1, lm))
        if tc == 0:
            dbg_dump(m, f"gsq{b}", m.sq)
        ve = P.count("VE")
        for i in range(4):
            bk_g, gwg = pool.take()
            outg = pool.ap[0:8, bk_g, 0:64].bitcast(BF16)
            P.emit("tensor", lambda e, o=outg,
                   s=m.gscr[:, 2, 0:32].bitcast(BF16)[:, i * 8:(i + 1) * 8]:
                   e.transpose(o, s, m.ident_b), waits=(("VE", ve),) + tuple(gwg),
                   incs=(("TE", 1),))
            te = P.count("TE")
            P.emit("scalar", lambda e, o=outg, d=m.gfm[:, tc * 512 + i * 128:
                   tc * 512 + (i + 1) * 128]: e.copy(d, o), waits=(("TE", te),))
            pool.release(bk_g, "SE")
    se_g = P.count("SE")
    ge_g = P.count("GE")
    for half in range(2):
        t0 = half * 1024
        P.emit("gpsimd", lambda e: e.memset(
            y_acc.rearrange("p a b -> p (a b)"), 0.0))
        ge_z = P.count("GE")
        te = P.count("TE")
        P.emit("sync", lambda e: e.dma_start(out=w1b, in_=w1r[:, :, 0:2048]),
               waits=(("TE", te), pw), incs=(("DX", 16),), auto=False)
        dw1 = ("DX", P.count("DX"))
        P.emit("sync", lambda e: e.dma_start(out=w2b, in_=w2r[:, 0:16, :]),
               waits=(pw,), incs=(("DX", 16),), auto=False)
        dw2 = ("DX", P.count("DX"))
        for e8 in range(8):
            dw1_next = []
            for tq in range(4):
                tg = t0 + tq * 256
                bg, gwg = pool.take(hold=True)
                gb = pool.bank(bg, 256)
                mm_into(P, gb, [(m.onehot[:, e8, :], m.gfm[:, tg:tg + 256])],
                        waits=(("SE", se_g), ("GE", ge_g)) + tuple(gwg))
                ve_fc = []
                for fc in range(16):
                    pairs = [(w1b[:, cc, fc * 128:(fc + 1) * 128],
                              h2[:, cc, tg:tg + 256]) for cc in range(NCC)]
                    wts = ()
                    if isinstance(dw1, list):
                        if tq == 0:
                            wts = (dw1[fc],)
                    elif tq == 0 and fc == 0:
                        wts = (dw1,)
                    bh, out = mm_group(P, pool, pairs, 256, waits=wts)
                    te = P.count("TE")
                    P.emit("scalar", lambda e, d=m.hprime[:, fc, :], s=out:
                           e.activation(d, s, AF.Gelu), waits=(("TE", te),))
                    pool.release(bh, "SE")
                    se = P.count("SE")
                    P.emit("vector", lambda e, d=m.hprime[:, fc, :], gb=gb:
                           e.tensor_mul(d, d, gb), waits=(("SE", se),))
                    ve_fc.append(P.count("VE"))
                    if e8 < 7 and tq == 3:
                        te_p = P.count("TE")
                        P.emit("sync", lambda e, e8=e8, fc=fc: e.dma_start(
                            out=w1b[:, :, fc * 128:(fc + 1) * 128],
                            in_=w1r[:, :, (e8 + 1) * 2048 + fc * 128:
                                    (e8 + 1) * 2048 + (fc + 1) * 128]),
                            waits=(("TE", te_p),), incs=(("DX", 16),), auto=False)
                        dw1_next.append(("DX", P.count("DX")))
                pool.release(bg, "VE")
                if False:
                    te = P.count("TE")
                    P.emit("gpsimd", lambda e, e8=e8: e.dma_start(
                        out=w1b, in_=w1r[:, :, (e8 + 1) * 2048:(e8 + 2) * 2048]),
                        waits=(("TE", te),), incs=(("DW", 16),), auto=False)
                    dw1 = P.count("DW")
                for cc in range(NCC):
                    by, gwy = pool.take()
                    out = pool.bank(by, 256)
                    for fc in range(16):
                        wts = [("VE", ve_fc[fc])]
                        if fc == 0:
                            wts += list(gwy)
                        if tq == 0 and cc == 0 and fc == 0:
                            wts.append(dw2)
                        P.emit("tensor",
                               lambda e, l=w2b[:, fc, cc * 128:(cc + 1) * 128],
                               r=m.hprime[:, fc, :], s=(fc == 0), t=(fc == 15),
                               out=out: e.matmul(out, l, r, start=s, stop=t),
                               waits=tuple(wts),
                               incs=((("TE", 1),) if fc == 15 else ()))
                    te = P.count("TE")
                    dsty = y_acc[:, cc, tq * 256:(tq + 1) * 256]
                    wv = (("TE", te),)
                    if e8 == 0 and tq == 0 and cc == 0:
                        wv = wv + (("GE", ge_z),)
                    P.emit("vector", lambda e, d=dsty, s=out: e.tensor_add(d, d, s),
                           waits=wv)
                    pool.release(by, "VE")
            if e8 < 7:
                te = P.count("TE")
                P.emit("sync", lambda e, e8=e8: e.dma_start(
                    out=w2b, in_=w2r[:, (e8 + 1) * 16:(e8 + 2) * 16, :]),
                    waits=(("TE", te),), incs=(("DX", 16),), auto=False)
                dw2 = ("DX", P.count("DX"))
                dw1 = dw1_next
        for tc in range(2):
            for jc in range(NCC):
                ch = y_acc[:, jc, tc * 512:(tc + 1) * 512]
                xch = m.S[:, jc, t0 + tc * 512:t0 + (tc + 1) * 512]
                P.emit("vector", lambda e, c=ch, x=xch: e.tensor_add(c, c, x))
            scrv = y_acc[:, :, tc * 512:(tc + 1) * 512]
            outs = [m.S[:, jc, t0 + tc * 512:t0 + (tc + 1) * 512]
                    for jc in range(NCC)]
            m.emit_postln(scrv, outs)
        m.P.fence(("gpsimd",))


def emit_block_d(m, b):
    P, pool = m.P, m.pool
    w = m.w
    R1, R2, R3, R4, R5 = (m.off[k] for k in ("R1", "R2", "R3", "R4", "R5"))
    x_tok = carve(m.ar, R1, (NTT, CD), BF16)
    disp = carve(m.ar, R2, (NCC, 16, 128), BF16)
    packs = carve(m.ar, R3, (NCC, 16, 128), BF16)
    logits_tm = carve(m.ar, R4, (NTT, 128), F32)
    l_fm = carve(m.ar, R4 + 8 * KB, (16, 128), F32)
    maxg = carve(m.ar, R4 + 16 * KB, (64,), F32)
    P_fm = carve(m.ar, R4 + 17 * KB, (NTT, 128), BF16)
    P4 = carve(m.ar, R4, (NTT, 4, 128), BF16)
    inner = carve(m.ar, R4 + 21 * KB, (128, 16), BF16)[0:32, :, :]
    bias_nat = carve(m.ar, R4 + 29 * KB, (32,), F32)
    interT = carve(m.ar, R4 + 30 * KB, (4, 128), BF16)
    ctrl_sb = carve(m.ar, R5, (NCC, 128), BF16)
    f1blk = carve(m.ar, R5, (NCC, 2, 16, 32), BF16)
    f2str = carve(m.ar, R5, (2, 4, NCC, 128), BF16)[0:32, :, :, :, :]
    bias_sb = m.sq[0:32, 128:256]
    r1 = carve(m.ar, R2, (1024,), F32)
    r2 = carve(m.ar, R2 + 4 * KB, (512,), F32)
    r3 = carve(m.ar, R2 + 6 * KB, (256,), F32)
    r4 = carve(m.ar, R2 + 7 * KB, (128,), F32)

    te0 = P.count("TE")
    P.emit("sync", lambda e: e.dma_start(out=m.tieq, in_=m.cdata),
           incs=(("DX", 16),), auto=False)
    P.emit("gpsimd", lambda e: e.dma_start(
        out=ctrl_sb, in_=w["L1_ffn_controller"].rearrange(
            "(cc p) e s2 -> p cc (e s2)", p=128)), waits=(("TE", te0),),
        incs=(("DW", 16),), auto=False)
    P.emit("gpsimd", lambda e: e.dma_start(
        out=bias_nat, in_=w["L1_ffn_bias"].rearrange("e s2 f -> (e s2) f")),
        incs=(("DW", 16),), auto=False)
    dw0 = P.count("DX")
    dwc = P.count("DW")
    bkb, gwb = pool.take()
    outb = pool.bank(bkb, 128)[0:32, :]
    P.emit("tensor", lambda e: e.transpose(outb, bias_nat, m.ident_f128),
           waits=(("DW", dwc),) + tuple(gwb), incs=(("TE", 1),))
    te = P.count("TE")
    P.emit("vector", lambda e: e.tensor_copy(bias_sb, outb), waits=(("TE", te),))
    pool.release(bkb, "VE")
    for tt in range(NTT):
        for cc in range(NCC):
            transpose128(P, pool, m.S[:, cc, tt * 128:(tt + 1) * 128],
                         x_tok[:, tt, cc * 128:(cc + 1) * 128], m.ident_b)
    dx0 = P.count("DX")
    for tt in range(NTT):
        pairs = [(m.S[:, cc, tt * 128:(tt + 1) * 128], ctrl_sb[:, cc, :])
                 for cc in range(NCC)]
        bk, out = mm_group(P, pool, pairs, 128,
                           waits=(("DW", dwc),) if tt == 0 else ())
        te = P.count("TE")
        wts = (("TE", te),)
        if tt == 0:
            wts = wts + (("DX", dx0),)
        P.emit("vector", lambda e, d=logits_tm[:, tt, :], s=out:
               e.scalar_tensor_tensor(d, s, 1.0,
                                      m.tieq[:, 0:1].broadcast_to([128, 128]),
                                      ALU.mult, ALU.add), waits=wts)
        pool.release(bk, "VE")
    ve = P.count("VE")
    for tt in range(NTT):
        bk, gw = pool.take()
        out = pool.bank(bk, 128)
        P.emit("tensor", lambda e, o=out, s=logits_tm[:, tt, :]:
               e.transpose(o, s, m.ident_f128),
               waits=(("VE", ve),) + tuple(gw) if tt == 0 else tuple(gw),
               incs=(("TE", 1),))
        te = P.count("TE")
        P.emit("vector", lambda e, d=l_fm[:, tt, :], s=out: e.tensor_copy(d, s),
               waits=(("TE", te),))
        pool.release(bk, "VE")

    def vv(fn):
        P.emit("vector", fn, waits=(("VE", P.count("VE")),))

    L = l_fm.rearrange("p a b -> p (a b)")
    vv(lambda e: e.tensor_max(r1, L[:, 0::2], L[:, 1::2]))
    vv(lambda e: e.tensor_max(r2, r1[:, 0::2], r1[:, 1::2]))
    vv(lambda e: e.tensor_max(r3, r2[:, 0::2], r2[:, 1::2]))
    vv(lambda e: e.tensor_max(r4, r3[:, 0::2], r3[:, 1::2]))
    vv(lambda e: e.tensor_max(maxg, r4[:, 0::2], r4[:, 1::2]))
    vv(lambda e: e.tensor_tensor(
        P_fm.rearrange("p a b -> p (a b)").rearrange("p (g t) -> p g t", g=64),
        L.rearrange("p (g t) -> p g t", g=64),
        maxg.unsqueeze(2).broadcast_to([128, 64, 32]), ALU.is_equal))
    ve_p = P.count("VE")
    for tt in range(NTT):
        bk, gw = pool.take()
        out = pool.bank_bf16(bk)
        P.emit("tensor", lambda e, o=out, s=P_fm[:, tt, :]:
               e.transpose(o, s, m.ident_b),
               waits=(("VE", ve_p),) + tuple(gw), incs=(("TE", 1),))
        te = P.count("TE")
        for gl in range(4):
            P.emit("vector", lambda e, d=P4[:, tt, gl, :], s=out, gl=gl:
                   e.scalar_tensor_tensor(
                       d, s, 1.0, m.tieq[:, 1 + gl:2 + gl].broadcast_to([128, 128]),
                       ALU.mult, ALU.mult),
                   waits=(("TE", te),) if gl == 0 else ())
        pool.release(bk, "VE")
    ve_p4 = P.count("VE")
    dbg_dump(m, f"dP{b}", P_fm)
    dbg_dump(m, f"dL{b}", l_fm)

    if m._prepass_f is None:
        P.emit("gpsimd", lambda e: e.dma_start(
            out=m.f1bf.rearrange("c es f -> c (es f)"),
            in_=w["L1_ffn_f1"].rearrange("c e s2 f -> c (e s2 f)")),
            incs=(("DW", 16),), auto=False)
        P.emit("gpsimd", lambda e: e.dma_start(
            out=m.f2bf.rearrange("es f c -> (es f) c"),
            in_=w["L1_ffn_f2"].rearrange("e s2 f c -> (e s2 f) c")),
            incs=(("DW", 16),), auto=False)
        m._prepass_f = ("DW", P.count("DW"))
    f1r = m.f1bf.rearrange("(cc p) es f -> p cc es f", p=128)
    f2w = m.f2bf
    for q in range(4):
        tts = range(4 * q, 4 * q + 4)
        te_q = P.count("TE")
        P.emit("gpsimd", lambda e: e.dma_start(
            out=f1blk[:, :, 0, :, :], in_=f1r[:, :, 0:16, :]),
            waits=(("TE", te_q), m._prepass_f), incs=(("DW", 16),), auto=False)
        dwf0 = P.count("DW")
        for i, tt in enumerate(tts):
            for cc in range(NCC):
                pairs = [(x_tok[:, tt, cc * 128:(cc + 1) * 128],
                          P4[:, tt, :, :].rearrange("p a b -> p (a b)"))]
                bk, out = mm_group(P, pool, pairs, 512,
                                   waits=(("VE", ve_p4),) if (i == 0 and cc == 0 and q == 0) else ())
                te = P.count("TE")
                dst = disp[:, cc, 4 * i:4 * i + 4, :]
                P.emit("vector", lambda e, d=dst, s=out.rearrange(
                    "p (a b) -> p a b", a=4): e.tensor_copy(d, s),
                    waits=(("TE", te),))
                pool.release(bk, "VE")
        dwf = dwf0
        ve_disp = P.count("VE")
        for blk in range(8):
            if blk < 7:
                te_b = P.count("TE")
                P.emit("gpsimd", lambda e, blk=blk: e.dma_start(
                    out=f1blk[:, :, (blk + 1) % 2, :, :],
                    in_=f1r[:, :, 16 * (blk + 1):16 * (blk + 2), :]),
                    waits=(("TE", te_b),), incs=(("DW", 16),), auto=False)
            for el in range(16):
                es = blk * 16 + el
                pairs = [(f1blk[:, cc, blk % 2, el, :],
                          disp[:, cc, :, es]) for cc in range(NCC)]
                wts = (("DW", dwf), ("VE", ve_disp)) if el == 0 else ()
                bk, out = mm_group(P, pool, pairs, 16, rows=32, waits=wts)
                te = P.count("TE")
                P.emit("scalar", lambda e, d=inner[:, es, :], s=out, es=es:
                       e.activation(d, s, AF.Relu, bias=bias_sb[:, es:es + 1]),
                       waits=(("TE", te),))
                pool.release(bk, "SE")
            dwf = P.count("DW")
        se_f1 = P.count("SE")
        nblk2 = 32
        P.emit("gpsimd", lambda e: e.dma_start(
            out=f2str[:, 0, :, :, :].rearrange("p a b c -> p a (b c)"),
            in_=f2w[0:4].rearrange("x f c -> f x c")),
            waits=(m._prepass_f,), incs=(("DW", 16),), auto=False)
        dw_blk = [P.count("DW")]
        for grp in range(4):
            banks = []
            for cc in range(NCC):
                bk, gw = pool.take(hold=True)
                banks.append((bk, gw))
            for el in range(32):
                es = grp * 32 + el
                if es % 4 == 0:
                    nb = es // 4 + 1
                    if nb < nblk2:
                        te_b = P.count("TE")
                        P.emit("gpsimd", lambda e, nb=nb: e.dma_start(
                            out=f2str[:, nb % 2, :, :, :].rearrange(
                                "p a b c -> p a (b c)"),
                            in_=f2w[4 * nb:4 * nb + 4].rearrange("x f c -> f x c")),
                            waits=(("TE", te_b),), incs=(("DW", 16),), auto=False)
                        dw_blk.append(P.count("DW"))
                for cc in range(NCC):
                    bk, gw = banks[cc]
                    outv = pool.ap[:, bk, el * 16:(el + 1) * 16]
                    wts = tuple(gw)
                    if cc == 0 and es % 4 == 0:
                        wts = wts + (("DW", dw_blk[es // 4]),)
                    if el == 0 and cc == 0:
                        wts = wts + (("SE", se_f1),)
                    mm_into(P, outv,
                            [(f2str[:, (es // 4) % 2, es % 4, cc, :],
                              inner[:, es, :])], waits=wts)
                    if el == 0:
                        banks[cc] = (bk, ())
            te = P.count("TE")
            for cc in range(NCC):
                bk, _ = banks[cc]
                src = pool.bank(bk).rearrange("p (a b) -> p a b", a=32)
                src = src.transpose([0, 2, 1])
                dst = packs[:, cc, :, grp * 32:(grp + 1) * 32]
                P.emit("vector", lambda e, d=dst, s=src: e.tensor_copy(d, s),
                       waits=(("TE", te),) if cc == 0 else ())
                pool.release(bk, "VE")
        ve_pk = P.count("VE")
        items = []
        for cc in range(NCC):
            for gl in range(16):
                items.append((cc, gl))
        pend = {}
        DEPTH = 3

        def emit_mm(idx):
            cc, gl, ve_i = pend.pop(idx)
            g = q * 16 + gl
            tt, t0 = g // 4, (g % 4) * 32
            it = interT[:, idx % 4, :]
            bk2, gw2 = pool.take()
            out2 = pool.bank(bk2, 32)
            mm_into(P, out2, [(it, P_fm[:, tt, t0:t0 + 32])],
                    waits=(("VE", ve_i),) + tuple(gw2))
            te = P.count("TE")
            dst = m.S[:, cc, g * 32:(g + 1) * 32]
            P.emit("vector", lambda e, d=dst, s=out2: e.tensor_add(d, s, d),
                   waits=(("TE", te),))
            pool.release(bk2, "VE")

        for idx, (cc, gl) in enumerate(items):
            sl = idx % 4
            it = interT[:, sl, :]
            bk, gw = pool.take()
            outt = pool.bank_bf16(bk)
            w0 = (("VE", ve_pk),) if idx == 0 else ()
            P.emit("tensor", lambda e, o=outt, s=packs[:, cc, gl, :]:
                   e.transpose(o, s, m.ident_b),
                   waits=w0 + tuple(gw), incs=(("TE", 1),))
            te = P.count("TE")
            P.emit("vector", lambda e, o=outt, it=it: e.tensor_copy(it, o),
                   waits=(("TE", te),))
            pool.release(bk, "VE")
            pend[idx] = (cc, gl, P.count("VE"))
            if idx >= DEPTH:
                emit_mm(idx - DEPTH)
        for idx in sorted(pend):
            emit_mm(idx)
    scr = carve(m.ar, R2, (NCC, 512), F32)
    ve_s = P.count("VE")
    P.wait("vector", "VE", ve_s)
    for tc in range(NTC):
        chunks = [m.S[:, cc, tc * 512:(tc + 1) * 512] for cc in range(NCC)]
        outs = chunks
        bm, br = m.emit_ln_stats(chunks, True, 1e-5)
        m.emit_ln_apply(chunks, outs, bm, br, scr)


def make_cdata():
    tie = np.tile(np.linspace(0.0, 1e-6, 32, dtype=np.float32), 4)
    cd = np.zeros((128, 8), np.float32)
    cd[:, 0] = tie
    for gl in range(4):
        cd[32 * gl:32 * gl + 32, 1 + gl] = 1.0
    return cd



def build_module(blocks=("A", "B", "C", "D"), nb=2):
    m = Mod(nb=nb)
    m.emit_setup()
    m.P.fence()
    for b in range(nb):
        m.emit_stream_in(b)
        m.P.fence()
        dbg_dump(m, f"s{b}", m.S)
        if "A" in blocks:
            emit_block_a(m, b)
            m.P.fence()
            dbg_dump(m, f"postA{b}", m.S)
        if "B" in blocks:
            emit_block_b(m, b)
            m.P.fence()
            dbg_dump(m, f"postB{b}", m.S)
        if "C" in blocks:
            emit_block_c(m, b)
            m.P.fence()
            dbg_dump(m, f"postC{b}", m.S)
        if "D" in blocks:
            emit_block_d(m, b)
            m.P.fence()
        m.emit_stream_out(b)
        m.P.fence()
    m.P.wait("sync", "DX")
    m.P.finalize()
    return m


DBG_TAGS = set()


def dbg_dump(m, tag, ap):
    if tag not in DBG_TAGS:
        return
    P = m.P
    flat = ap if ap.ndim == 2 else ap.rearrange(
        "p " + " ".join(f"d{i}" for i in range(ap.ndim - 1)) + " -> p (" +
        " ".join(f"d{i}" for i in range(ap.ndim - 1)) + ")")
    t = m.nc.dram_tensor("dbg_" + tag, [flat.shape[0], flat.shape[1]],
                         flat.dtype, kind="ExternalOutput").ap()
    P.fence()
    P.emit("sync", lambda e: e.dma_start(out=t, in_=flat),
           incs=(("DX", 16),), auto=False)
    P.fence()


GROUPS = [("A", "B", "C", "D")]


def kernel(**inputs):
    xfull = np.ascontiguousarray(inputs["L0_attn_x"], dtype=np.float32)
    shards = [xfull[2 * i:2 * i + 2] for i in range(8)]
    wmap = {n: np.ascontiguousarray(inputs[n], dtype=np.float32) for n in WSPEC}
    cur = shards
    for grp in GROUPS:
        m = build_module(blocks=grp, nb=2)
        cd = make_cdata()
        in_maps = [dict(wmap, x=cur[i], cdata=cd) for i in range(8)]
        res = run_bass_kernel_spmd(m.nc, in_maps, core_ids=list(range(8)))
        cur = [np.asarray(r["y"]) for r in res.results]
    return np.concatenate(cur, axis=0)


# version 09
# speedup vs baseline: 1.1859x; 1.0146x over previous
import numpy as np
from contextlib import ExitStack

import concourse.bass as bass
import concourse.mybir as mybir
from concourse.bass_utils import run_bass_kernel_spmd

F32 = mybir.dt.float32
BF16 = mybir.dt.bfloat16
AF = mybir.ActivationFunctionType
ALU = mybir.AluOpType
AX = mybir.AxisListType

T = 2048
CD = 1024
NCC = 8
NTT = 16
NTC = 4
H = 16
HD = 64
KB = 1024
EXP_C_A = 12.0
EXP_C_B = 6.0
NEG = -1e30

ENGINES = ("tensor", "vector", "scalar", "gpsimd", "sync")
AUTOSEM = {"vector": "VE", "scalar": "SE", "gpsimd": "GE"}


class Prog:
    def __init__(self, nc, stack):
        self.nc = nc
        self.stack = stack
        self.ops = {e: [] for e in ENGINES}
        self.sems = {}

    def sem(self, name):
        if name not in self.sems:
            h = self.stack.enter_context(self.nc.semaphore(name))
            self.sems[name] = [h, 0]
        return self.sems[name]

    def count(self, name):
        return self.sem(name)[1]

    def emit(self, eng, fn, waits=(), incs=(), auto=True):
        incs = list(incs)
        if auto and eng in AUTOSEM:
            incs.append((AUTOSEM[eng], 1))
        inc_t = []
        for name, k in incs:
            s = self.sem(name)
            s[1] += k
            inc_t.append((s[0], k))
        wait_t = []
        for name, cnt in waits:
            s = self.sem(name)
            assert cnt <= s[1], f"wait on future count {name} {cnt}>{s[1]}"
            if cnt > 0:
                wait_t.append((s[0], cnt))

        def run(e, wait_t=wait_t, fn=fn, inc_t=inc_t):
            for h, cnt in wait_t:
                e.wait_ge(h, cnt)
            ins = fn(e)
            for h, k in inc_t:
                assert ins is not None, "then_inc on missing instruction"
                ins = ins.then_inc(h, k)
        self.ops[eng].append(run)

    def wait(self, eng, name, cnt=None):
        cnt = self.count(name) if cnt is None else cnt
        if cnt > 0:
            self.emit(eng, lambda e: None, waits=((name, cnt),), auto=False)

    def snap(self):
        return {n: s[1] for n, s in self.sems.items()}

    def fence(self, engines=ENGINES):
        snap = self.snap()
        src = ("TE", "VE", "SE", "GE", "DW", "DX")
        for w in engines:
            for sname in src:
                if sname in snap:
                    self.wait(w, sname, snap[sname])

    def finalize(self):
        nc = self.nc
        with nc.Block() as block:
            for eng in ENGINES:
                getattr(block, eng)(lambda e, eng=eng: [f(e) for f in self.ops[eng]])


class PsumPool:
    def __init__(self, P, ap):
        self.P = P
        self.ap = ap
        self.guard = [None] * 8
        self.held = set()
        self.rr = 0

    def take(self, hold=False):
        for _ in range(16):
            b = self.rr % 8
            self.rr += 1
            if b not in self.held:
                break
        else:
            raise RuntimeError("no free psum bank")
        if hold:
            self.held.add(b)
        g = self.guard[b]
        self.guard[b] = None
        return b, ([g] if g else [])

    def release(self, b, semname):
        self.guard[b] = (semname, self.P.count(semname))
        self.held.discard(b)

    def bank(self, b, n=512):
        return self.ap[:, b, 0:n]

    def bank_bf16(self, b):
        return self.ap[:, b, 0:64].bitcast(BF16)


def carve(arena, off, shape, dtype):
    esz = {F32: 4, BF16: 2}[dtype]
    n = int(np.prod(shape))
    assert off % 32 == 0, off
    v = arena[:, off:off + n * esz].bitcast(dtype)
    if len(shape) == 1:
        return v
    pat = " ".join(f"d{i}" for i in range(len(shape)))
    kw = {f"d{i}": int(shape[i]) for i in range(len(shape))}
    return v.rearrange(f"p ({pat}) -> p {pat}", **kw)


def mm_group(P, pool, pairs, n_cols, waits=(), hold=False, rows=128):
    b, gw = pool.take(hold=hold)
    out = pool.bank(b, n_cols)
    if rows != 128:
        out = out[0:rows, :]
    n = len(pairs)
    for i, (l, r) in enumerate(pairs):
        w = tuple(waits) + tuple(gw) if i == 0 else ()
        P.emit("tensor",
               lambda e, l=l, r=r, s=(i == 0), t=(i == n - 1), out=out:
               e.matmul(out, l, r, start=s, stop=t),
               waits=w, incs=((("TE", 1),) if i == n - 1 else ()))
    return b, out


def mm_into(P, out, pairs, waits=(), start=True, stop=True, inc=True):
    n = len(pairs)
    for i, (l, r) in enumerate(pairs):
        w = tuple(waits) if i == 0 else ()
        P.emit("tensor",
               lambda e, l=l, r=r, s=(start and i == 0), t=(stop and i == n - 1),
               out=out: e.matmul(out, l, r, start=s, stop=t),
               waits=w,
               incs=((("TE", 1),) if (inc and i == n - 1) else ()))


def transpose128(P, pool, src_bf16, dst_bf16, ident, waits=()):
    b, gw = pool.take()
    out = pool.bank_bf16(b)
    P.emit("tensor", lambda e, out=out, s=src_bf16: e.transpose(out, s, ident),
           waits=tuple(waits) + tuple(gw), incs=(("TE", 1),))
    te = P.count("TE")
    P.emit("vector", lambda e, out=out, d=dst_bf16: e.tensor_copy(d, out),
           waits=(("TE", te),))
    pool.release(b, "VE")


WSPEC = {
    "L0_attn_w_qkv": (1024, 3072), "L0_attn_ln_w": (1024,),
    "L0_attn_w_proj": (1024, 1024),
    "L0_ffn_ln1_w": (1024,), "L0_ffn_c_attn_w": (1024, 3072),
    "L0_ffn_c_proj_w": (1024, 1024), "L0_ffn_ln2_w": (1024,),
    "L0_ffn_router_w": (1024, 8), "L0_ffn_w1": (1024, 16384),
    "L0_ffn_w2": (16384, 1024),
    "L1_attn_W1": (1024, 3072), "L1_attn_b1": (3072,),
    "L1_attn_W2": (1024, 1024), "L1_attn_b2": (1024,),
    "L1_ffn_controller": (1024, 32, 4), "L1_ffn_f1": (1024, 32, 4, 32),
    "L1_ffn_bias": (32, 4, 32), "L1_ffn_f2": (32, 4, 32, 1024),
}


class Mod:
    def __init__(self, nb=2):
        self.stack = ExitStack()
        nc = bass.Bass("TRN2", target_bir_lowering=False)
        self.nc = nc
        self.nb = nb
        self.x = nc.dram_tensor("x", [nb, T, CD], F32, kind="ExternalInput").ap()
        self.y = nc.dram_tensor("y", [nb, T, CD], F32, kind="ExternalOutput").ap()
        self.w = {n: nc.dram_tensor(n, list(s), F32, kind="ExternalInput").ap()
                  for n, s in WSPEC.items()}
        self.cdata = nc.dram_tensor("cdata", [128, 8], F32, kind="ExternalInput").ap()
        self.w1bf = nc.dram_tensor("w1bf", [1024, 16384], BF16, kind="Internal").ap()
        self.w2bf = nc.dram_tensor("w2bf", [16384, 1024], BF16, kind="Internal").ap()
        self.f1bf = nc.dram_tensor("f1bf", [1024, 128, 32], BF16, kind="Internal").ap()
        self.f2bf = nc.dram_tensor("f2bf", [128, 32, 1024], BF16, kind="Internal").ap()
        self._prepass_w = None
        self._prepass_f = None
        self.P = Prog(nc, self.stack)

        sb = lambda name, shape, dt: self.stack.enter_context(
            nc.sbuf_tensor(name, list(shape), dt))[:]
        self.ar = sb("arena", (128, 177 * KB), mybir.dt.uint8)
        self.S = carve(self.ar, 0, (NCC, T), BF16)
        self.off = {"S": 0, "R1": 32 * KB, "R2": 64 * KB, "R3": 96 * KB,
                    "R4": 129 * KB, "R5": 161 * KB}

        self.ident_b = sb("ident_b", (128, 128), BF16)
        self.ident_f = sb("ident_f", (8, 8), F32)
        self.ident_f128 = sb("ident_f128", (128, 128), F32)
        self.madd = sb("madd", (128, 896), BF16)
        self.tri01 = sb("tri01", (128, 896), BF16)
        self.ones_col_f = sb("ones_col_f", (128, 1), F32)
        self.ones_col_b = sb("ones_col_b", (128, 1), BF16)
        self.ones_row_f = sb("ones_row_f", (1, 128), F32)
        self.rows = sb("rows", (1, 1024), F32)
        self.vcols = sb("vcols", (128, 72), F32)
        self.consts = sb("consts", (128, 8), F32)
        self.tieq = sb("tieq", (128, 8), F32)
        self.sq = sb("sq", (128, 512), F32)
        self.sq2 = sb("sq2", (128, 512), F32)
        self.rcol = sb("rcol", (128, 1), F32)
        self.gscr = self.sq[:, 0:96].rearrange("p (a b) -> p a b", a=3)
        self.grows = self.sq[:, 96:112]
        self.hprime = carve(self.ar, self.off["R5"], (16, 256), BF16)
        self.gfm = carve(self.ar, self.off["R5"] + 8 * KB, (2048,), BF16)[0:8, :]
        self.onehot = carve(self.ar, self.off["R5"] + 12 * KB, (8, 128), BF16)[0:8, :, :]
        self.rwsb = carve(self.ar, self.off["R5"] + 14 * KB, (8, 8), BF16)
        self.vrow = carve(self.ar, self.off["R1"], (1024,), F32)[0:1, :]
        self.ostage = carve(self.ar, self.off["R1"] + 4 * KB, (1024,), F32)
        self.whb = sb("whb", (128, 8, 2, 2, 64), BF16)
        self.ea = sb("ea", (128, 384), BF16)
        ps = self.stack.enter_context(nc.psum_tensor("PS", [128, 8, 512], F32))
        self.pool = PsumPool(self.P, ps[:])
        self.vc_used = 0
        self.vc_cache = {}
        self._dbg_scr = None
        self._dbg_scr_ap = None
        self._wq = ()
        self.E = None
        self.WBUF = None

    def emit_setup(self):
        P = self.P
        g = "gpsimd"

        def gg(fn):
            P.emit(g, fn, waits=(("GE", P.count("GE")),))
        gg(lambda e: e.memset(self.ident_b, 0.0))
        gg(lambda e: e.affine_select(out=self.ident_b, in_=self.ident_b,
           compare_op=ALU.not_equal, fill=1.0, base=0, pattern=[[-1, 128]],
           channel_multiplier=1))
        gg(lambda e: e.memset(self.ident_f, 0.0))
        gg(lambda e: e.affine_select(out=self.ident_f, in_=self.ident_f,
           compare_op=ALU.not_equal, fill=1.0, base=0, pattern=[[-1, 8]],
           channel_multiplier=1))
        gg(lambda e: e.memset(self.ident_f128, 0.0))
        gg(lambda e: e.affine_select(out=self.ident_f128, in_=self.ident_f128,
           compare_op=ALU.not_equal, fill=1.0, base=0, pattern=[[-1, 128]],
           channel_multiplier=1))
        gg(lambda e: e.memset(self.madd, 0.0))
        gg(lambda e: e.affine_select(out=self.madd, in_=self.madd,
           compare_op=ALU.is_ge, fill=NEG, base=-384,
           pattern=[[1, 896]], channel_multiplier=-1))
        gg(lambda e: e.memset(self.tri01, 1.0))
        gg(lambda e: e.affine_select(out=self.tri01, in_=self.tri01,
           compare_op=ALU.is_ge, fill=0.0, base=-384,
           pattern=[[1, 896]], channel_multiplier=-1))
        for i, vv in enumerate((1e-6, 1e-5, -EXP_C_A, -EXP_C_B)):
            gg(lambda e, i=i, vv=vv: e.memset(self.consts[:, i:i + 1], vv))
        gg(lambda e: e.memset(self.ones_col_f, 1.0))
        gg(lambda e: e.memset(self.ones_col_b, 1.0))
        gg(lambda e: e.memset(self.ones_row_f, 1.0))

    def madd_o(self, o, n=512):
        return self.madd[:, 384 - 128 * o:384 - 128 * o + n]

    def load_vec_cols(self, wap, tag, scale=None, per_head=False):
        if tag in self.vc_cache:
            return self.vc_cache[tag]
        P, pool = self.P, self.pool
        n = 16 if per_head else NCC
        off = self.vc_used
        self.vc_used += n
        assert self.vc_used <= 72
        self.vc_cache[tag] = off
        te_pre = P.count("TE")
        P.emit("sync", lambda e, w=wap: e.dma_start(out=self.vrow,
               in_=w.unsqueeze(0)), waits=(("TE", te_pre),),
               incs=(("DX", 16),), auto=False)
        dx = P.count("DX")
        step = 64 if per_head else 128
        for i in range(n):
            b, gw = pool.take()
            out = pool.bank(b, 512)[0:step, 0:1]
            src = self.vrow[:, i * step:(i + 1) * step]
            P.emit("tensor", lambda e, out=out, src=src: e.transpose(
                out, src, self.ident_f[0:1, 0:1]),
                waits=(("DX", dx),) + tuple(gw), incs=(("TE", 1),))
            te = P.count("TE")
            dst = self.vcols[0:step, off + i:off + i + 1]
            if scale is None:
                P.emit("vector", lambda e, d=dst, s=out: e.tensor_copy(d, s),
                       waits=(("TE", te),))
            else:
                P.emit("vector", lambda e, d=dst, s=out, sc=scale:
                       e.tensor_scalar_mul(d, s, sc), waits=(("TE", te),))
            pool.release(b, "VE")
        return off

    def emit_stream_in(self, b):
        P, pool = self.P, self.pool
        xin = carve(self.ar, self.off["R4"], (NTT, CD), BF16)
        src = self.x[b].rearrange("(tt p) c -> p tt c", p=128)
        dws = []
        for ck in range(4):
            P.emit("gpsimd", lambda e, ck=ck: e.dma_start(
                out=xin[:, 4 * ck:4 * ck + 4, :], in_=src[:, 4 * ck:4 * ck + 4, :]),
                incs=(("DW", 16),), auto=False)
            dws.append(P.count("DW"))
        for tt in range(NTT):
            for cc in range(NCC):
                transpose128(P, pool, xin[:, tt, cc * 128:(cc + 1) * 128],
                             self.S[:, cc, tt * 128:(tt + 1) * 128], self.ident_b,
                             waits=(("DW", dws[tt // 4]),) if cc == 0 else ())

    def emit_stream_out(self, b):
        P, pool = self.P, self.pool
        dx_prev = 0
        for tt in range(NTT):
            first_evac = True
            for cc in range(NCC):
                bk, gw = pool.take()
                out = pool.bank_bf16(bk)
                src = self.S[:, cc, tt * 128:(tt + 1) * 128]
                P.emit("tensor", lambda e, out=out, s=src: e.transpose(
                    out, s, self.ident_b), waits=tuple(gw), incs=(("TE", 1),))
                te = P.count("TE")
                dst = self.ostage[:, cc * 128:(cc + 1) * 128]
                w = (("TE", te),)
                if first_evac and dx_prev:
                    w = w + (("DX", dx_prev),)
                    first_evac = False
                P.emit("scalar", lambda e, d=dst, s=out: e.copy(d, s), waits=w)
                pool.release(bk, "SE")
            se = P.count("SE")
            dst = self.y[b, tt * 128:(tt + 1) * 128, :]
            P.emit("sync", lambda e, d=dst: e.dma_start(out=d, in_=self.ostage),
                   waits=(("SE", se),), incs=(("DX", 16),), auto=False)
            dx_prev = P.count("DX")

    def emit_ln_stats(self, chunks, bf16_chunks, eps):
        P, pool = self.P, self.pool
        bs, gs = pool.take(hold=True)
        bq, gq = pool.take(hold=True)
        sm = pool.ap[0:1, bs, :]
        ss = pool.ap[0:1, bq, :]
        ones = self.ones_col_b if bf16_chunks else self.ones_col_f
        te_sq = [0, 0]
        ve0 = P.count("VE")
        for i, ch in enumerate(chunks):
            sqb = self.sq if i % 2 == 0 else self.sq2
            w = (("TE", te_sq[i % 2]),) if te_sq[i % 2] else ()
            P.emit("vector", lambda e, ch=ch, sqb=sqb: e.tensor_mul(sqb, ch, ch),
                   waits=w)
            ve = P.count("VE")
            mm_into(P, sm, [(ones, ch)],
                    waits=tuple(gs) + ((("VE", ve0),) if i == 0 else ()),
                    start=(i == 0), stop=(i == 7), inc=True)
            gs = ()
            mm_into(P, ss, [(self.ones_col_f, sqb)],
                    waits=(("VE", ve),) + tuple(gq),
                    start=(i == 0), stop=(i == 7), inc=True)
            gq = ()
            te_sq[i % 2] = P.count("TE")
        te = P.count("TE")
        mean = self.rows[0:1, 0:512]
        var = self.rows[0:1, 512:1024]
        rstd = mean
        tmp = self.sq[0:1, 0:512]
        P.emit("scalar", lambda e, sm=sm: e.mul(mean, sm, 1.0 / CD), waits=(("TE", te),))
        P.emit("scalar", lambda e, ss=ss: e.mul(var, ss, 1.0 / CD))
        pool.release(bs, "SE")
        pool.release(bq, "SE")
        se = P.count("SE")
        bm, gwm = pool.take(hold=True)
        mm_into(P, pool.bank(bm), [(self.ones_row_f, mean)],
                waits=(("SE", se),) + tuple(gwm))
        te_bm = P.count("TE")
        P.emit("vector", lambda e, tmp=tmp: e.tensor_mul(tmp, mean, mean),
               waits=(("SE", se),))
        P.emit("vector", lambda e, tmp=tmp: e.tensor_sub(var, var, tmp))
        ve = P.count("VE")
        epsc = self.consts[0:1, 0:1] if eps == 1e-6 else self.consts[0:1, 1:2]
        P.emit("scalar", lambda e, epsc=epsc: e.activation(var, var, AF.Ln, bias=epsc),
               waits=(("VE", ve), ("TE", te_bm)))
        P.emit("scalar", lambda e, rstd=rstd: e.activation(rstd, var, AF.Exp, scale=-0.5))
        se = P.count("SE")
        br, gwr = pool.take(hold=True)
        mm_into(P, pool.bank(br), [(self.ones_row_f, rstd)],
                waits=(("SE", se),) + tuple(gwr))
        return bm, br

    def emit_ln_apply(self, chunks, out_chunks, bm, br, scr, scale_off=None):
        P, pool = self.P, self.pool
        te = P.count("TE")
        mb = pool.bank(bm)
        rb = pool.bank(br)
        for i, (ch, outv) in enumerate(zip(chunks, out_chunks)):
            t = scr[:, i, :]
            P.emit("vector", lambda e, t=t, ch=ch, mb=mb: e.tensor_sub(t, ch, mb),
                   waits=(("TE", te),) if i == 0 else ())
            if scale_off is None:
                P.emit("vector", lambda e, o=outv, t=t, rb=rb: e.tensor_mul(o, t, rb))
            else:
                sc = self.vcols[:, scale_off + i:scale_off + i + 1]
                P.emit("vector", lambda e, o=outv, t=t, rb=rb, sc=sc:
                       e.scalar_tensor_tensor(o, t, sc, rb, ALU.mult, ALU.mult))
        pool.release(bm, "VE")
        pool.release(br, "VE")

    def emit_postln(self, scr, out_chunks, scale_off=None, eps=1e-5):
        chunks = [scr[:, jc, :] for jc in range(NCC)]
        bm, br = self.emit_ln_stats(chunks, False, eps)
        self.emit_ln_apply(chunks, out_chunks, bm, br, scr, scale_off=scale_off)



def dma_w(m, dst, src):
    P = m.P
    te = P.count("TE")
    P.emit("gpsimd", lambda e, d=dst, s=src: e.dma_start(out=d, in_=s),
           waits=(("TE", te),), incs=(("DW", 16),), auto=False)
    return P.count("DW")


def _vtm_make(m, wbuf, src, vtm, bias_row=None):
    P, pool = m.P, m.pool
    ge_pre = P.count("GE")
    P.emit("gpsimd", lambda e, v=vtm: e.memset(
        v.rearrange("p a b c -> p (a b c)"), 1.0))
    ge = P.count("GE")
    first = True
    for tt in range(NTT):
        for half in range(2):
            pairs = [(src[:, cc, tt * 128:(tt + 1) * 128],
                      wbuf[:, cc, half * 512:(half + 1) * 512])
                     for cc in range(NCC)]
            if bias_row is not None:
                pairs.append((m.ones_row_f, bias_row[0:1, half * 512:(half + 1) * 512]))
            bk, out = mm_group(P, pool, pairs, 512, waits=m._wq)
            m._wq = ()
            te = P.count("TE")
            dst = vtm[:, tt, 8 * half:8 * half + 8, 0:64]
            w = (("TE", te),)
            if first:
                w = w + (("GE", ge),)
                first = False
            P.emit("vector", lambda e, d=dst, s=out.rearrange("p (h d) -> p h d", h=8):
                   e.tensor_copy(d, s), waits=w)
            pool.release(bk, "VE")


def _attn_core(m, h, qv, kv, vtm, y_tok, softmax_c):
    P, pool = m.P, m.pool
    E = m.E
    se_kc = [0] * 16
    ge_kc = [0] * 16
    for q5 in range(4):
        for kc in range(4 * q5 + 4):
            bk, out = mm_group(
                P, pool, [(kv[0:64, kc * 128:(kc + 1) * 128],
                           qv[0:64, q5 * 512:(q5 + 1) * 512])], 512, waits=m._wq)
            m._wq = ()
            te = P.count("TE")
            o = kc - 4 * q5
            w = (("TE", te),)
            dst = E[:, kc, :]
            if softmax_c is not None:
                if o >= 0:
                    P.emit("vector", lambda e, out=out, o=o: e.tensor_add(
                        out, out, m.madd_o(o)), waits=w)
                    w = (("VE", P.count("VE")),)
                cb = m.consts[:, 3:4]
                P.emit("scalar", lambda e, d=dst, s=out, cb=cb:
                       e.activation(d, s, AF.Exp, bias=cb), waits=w)
                se_kc[kc] = P.count("SE")
                pool.release(bk, "SE")
            else:
                if o >= 0:
                    P.emit("vector", lambda e, d=dst, s=out, o=o: e.tensor_tensor(
                        d, s, m.tri01[:, 384 - 128 * o:896 - 128 * o], ALU.mult),
                        waits=w)
                    pool.release(bk, "VE")
                    ge_kc[kc] = ("VE", P.count("VE"))
                else:
                    P.emit("scalar", lambda e, d=dst, s=out: e.activation(
                        d, s, AF.Copy), waits=w)
                    pool.release(bk, "SE")
                    se_kc[kc] = P.count("SE")
                    ge_kc[kc] = None
        ve = P.count("VE")
        for qt in range(4 * q5, 4 * q5 + 4):
            ql = qt - 4 * q5
            pairs = [(E[:, kc, ql * 128:(ql + 1) * 128], vtm[:, kc, h, 0:65])
                     for kc in range(qt + 1)]
            bk, gw = pool.take()
            out = pool.bank(bk, 65)
            if softmax_c is None:
                wav = (("VE", ve),) + tuple(gw)
                if ge_kc[qt] is not None:
                    wav = wav + (ge_kc[qt],)
                else:
                    wav = wav + (("SE", se_kc[qt]),)
            else:
                wav = (("SE", se_kc[qt]), ("VE", ve)) + tuple(gw)
            mm_into(P, out, pairs, waits=wav)
            te = P.count("TE")
            dst = y_tok[:, qt, h * 64:(h + 1) * 64]
            if softmax_c is not None:
                P.emit("vector", lambda e, out=out: e.reciprocal(
                    m.rcol, out[:, 64:65]),
                    waits=(("TE", te), ("SE", P.count("SE"))))
                ve_r = P.count("VE")
                P.emit("scalar", lambda e, d=dst, out=out: e.activation(
                    d, out[:, 0:64], AF.Copy, scale=m.rcol), waits=(("VE", ve_r),))
                pool.release(bk, "SE")
            else:
                P.emit("vector", lambda e, d=dst, out=out: e.tensor_copy(
                    d, out[:, 0:64]), waits=(("TE", te),))
                pool.release(bk, "VE")


def _proj_epilogue(m, wbuf, y_tok, epi_chunk, epi_tc, pipeline=True):
    P, pool = m.P, m.pool
    yfm = carve(m.ar, m.off["R3"], (NCC, 512), BF16)
    w0 = (("SE", P.count("SE")), ("VE", P.count("VE")))
    for tc in range(NTC):
        for i, tt in enumerate(range(4 * tc, 4 * tc + 4)):
            for cc in range(NCC):
                transpose128(P, pool, y_tok[:, tt, cc * 128:(cc + 1) * 128],
                             yfm[:, cc, i * 128:(i + 1) * 128], m.ident_b,
                             waits=w0)
                w0 = ()
        ve = P.count("VE")
        for jc in range(NCC):
            pairs = [(wbuf[:, cc, jc * 128:(jc + 1) * 128],
                      yfm[:, cc, :]) for cc in range(NCC)]
            bk, out = mm_group(P, pool, pairs, 512,
                               waits=m._wq + ((("VE", ve),) if jc == 0 else ()))
            m._wq = ()
            epi_chunk(tc, jc, out)
            pool.release(bk, "VE")
        if not pipeline:
            epi_tc(tc)
        elif tc > 0:
            epi_tc(tc - 1)
    if pipeline:
        epi_tc(NTC - 1)



def emit_block_a(m, b):
    P, pool = m.P, m.pool
    w = m.w
    wq = w["L0_attn_w_qkv"].rearrange("(cc p) j -> p cc j", p=128)
    wp = w["L0_attn_w_proj"].rearrange("(cc p) j -> p cc j", p=128)
    WA = carve(m.ar, m.off["R5"], (NCC, 1024), BF16)
    WB = carve(m.ar, m.off["R4"], (NCC, 1024), BF16)
    qfm = carve(m.ar, m.off["R1"], (NCC, T), BF16)
    kfm = carve(m.ar, m.off["R2"], (NCC, T), BF16)
    vtm = carve(m.ar, m.off["R3"], (NTT, 16, 65), BF16)
    y_tok = carve(m.ar, m.off["R4"], (NTT, CD), BF16)

    lnw_off = m.load_vec_cols(w["L0_attn_ln_w"], "a_lnw")
    lnwq_off = m.load_vec_cols(w["L0_attn_ln_w"], "a_lnwq", scale=0.125)
    ve_pre = P.count("VE")

    dw = dma_w(m, WA, wq[:, :, 0:1024])
    dwk = dma_w(m, WB, wq[:, :, 1024:2048])
    for part, (dst, coloff, wbuf) in enumerate(
            ((qfm, lnwq_off, WA), (kfm, lnw_off, WB))):
        m._wq = (("DW", dw if part == 0 else dwk), ("VE", ve_pre))
        if part == 1:
            dw_v = dma_w(m, WA, wq[:, :, 2048:3072])
        for tc in range(NTC):
            bss, gss = pool.take(hold=True)
            ss = pool.ap[0:1, bss, :]
            te_sq = [0, 0]
            for jc in range(NCC):
                pairs = [(wbuf[:, cc, jc * 128:(jc + 1) * 128],
                          m.S[:, cc, tc * 512:(tc + 1) * 512]) for cc in range(NCC)]
                bk, out = mm_group(P, pool, pairs, 512, waits=m._wq)
                m._wq = ()
                te = P.count("TE")
                ch = dst[:, jc, tc * 512:(tc + 1) * 512]
                P.emit("vector", lambda e, d=ch, s=out: e.tensor_copy(d, s),
                       waits=(("TE", te),))
                pool.release(bk, "VE")
                sqb = m.sq if jc % 2 == 0 else m.sq2
                wsq = (("TE", te_sq[jc % 2]),) if te_sq[jc % 2] else ()
                P.emit("vector", lambda e, c=ch, sqb=sqb: e.tensor_mul(sqb, c, c),
                       waits=wsq)
                ve = P.count("VE")
                mm_into(P, ss, [(m.ones_col_f, sqb)],
                        waits=(("VE", ve),) + tuple(gss),
                        start=(jc == 0), stop=(jc == 7), inc=True)
                gss = ()
                te_sq[jc % 2] = P.count("TE")
            rstd = m.rows[0:1, 512:1024]
            P.emit("scalar", lambda e, ss=ss: e.activation(
                rstd, ss, AF.Ln, scale=1.0 / CD, bias=m.consts[0:1, 0:1]),
                waits=(("TE", max(te_sq)),))
            P.emit("scalar", lambda e: e.activation(rstd, rstd, AF.Exp, scale=-0.5))
            pool.release(bss, "SE")
            se = P.count("SE")
            br, gwr = pool.take(hold=True)
            mm_into(P, pool.bank(br), [(m.ones_row_f, rstd)],
                    waits=(("SE", se),) + tuple(gwr))
            te = P.count("TE")
            for jc in range(NCC):
                ch = dst[:, jc, tc * 512:(tc + 1) * 512]
                sc = m.vcols[:, coloff + jc:coloff + jc + 1]
                P.emit("vector", lambda e, c=ch, sc=sc, rb=pool.bank(br):
                       e.scalar_tensor_tensor(c, c, sc, rb, ALU.mult, ALU.mult),
                       waits=(("TE", te),) if jc == 0 else ())
            pool.release(br, "VE")
    dbg_dump(m, f"q{b}", qfm)
    dbg_dump(m, f"k{b}", kfm)
    m._wq = (("DW", dw_v),)
    _vtm_make(m, WA, m.S, vtm)
    dw = dma_w(m, WA, wp)
    dbg_dump(m, f"v{b}", vtm)
    ve_v = P.count("VE")
    first = True
    for h in range(H):
        ro, hc = (h % 2) * 64, h // 2
        qv = qfm[ro:ro + 64, hc, :] if ro == 0 else qfm[64:128, hc, :]
        kv = kfm[ro:ro + 64, hc, :] if ro == 0 else kfm[64:128, hc, :]
        for wdw in range(8):
            kt0, kt1 = 2 * wdw, 2 * wdw + 1
            wx = m._wq + ((("VE", ve_v),) if first else ())
            m._wq = ()
            first = False
            bk, out = mm_group(P, pool, [(kv[:, kt0 * 128:kt0 * 128 + 128],
                                          qv[:, wdw * 256:wdw * 256 + 256])], 256,
                               waits=wx)
            te = P.count("TE")
            P.emit("vector", lambda e, out=out: e.tensor_add(
                out[:, 0:128], out[:, 0:128], m.madd_o(0, 128)),
                waits=(("TE", te),))
            vew = P.count("VE")
            P.emit("scalar", lambda e, out=out: e.activation(
                m.ea[:, 0:256], out, AF.Exp, bias=m.consts[:, 2:3]), waits=(("VE", vew),))
            pool.release(bk, "SE")
            bk2, out2 = mm_group(P, pool, [(kv[:, kt1 * 128:kt1 * 128 + 128],
                                            qv[:, wdw * 256 + 128:wdw * 256 + 256])],
                                 128)
            te = P.count("TE")
            P.emit("vector", lambda e, out2=out2: e.tensor_add(
                out2, out2, m.madd_o(0, 128)), waits=(("TE", te),))
            vew = P.count("VE")
            P.emit("scalar", lambda e, out2=out2: e.activation(
                m.ea[:, 256:384], out2, AF.Exp, bias=m.consts[:, 2:3]), waits=(("VE", vew),))
            pool.release(bk2, "SE")
            if h == 0 and wdw == 0:
                dbg_dump(m, f"ea{b}", m.ea)
            se = P.count("SE")
            for qt, pairs in ((kt0, [(m.ea[:, 0:128], vtm[:, kt0, h, 0:65])]),
                              (kt1, [(m.ea[:, 128:256], vtm[:, kt0, h, 0:65]),
                                     (m.ea[:, 256:384], vtm[:, kt1, h, 0:65])])):
                bky, gwy = pool.take()
                outy = pool.bank(bky, 65)
                mm_into(P, outy, pairs, waits=(("SE", se),) + tuple(gwy))
                te = P.count("TE")
                dst = y_tok[:, qt, h * 64:(h + 1) * 64]
                P.emit("vector", lambda e, outy=outy: e.reciprocal(
                    m.rcol, outy[:, 64:65]),
                    waits=(("TE", te), ("SE", P.count("SE"))))
                ve_r = P.count("VE")
                P.emit("scalar", lambda e, d=dst, outy=outy: e.activation(
                    d, outy[:, 0:64], AF.Copy, scale=m.rcol), waits=(("VE", ve_r),))
                pool.release(bky, "SE")
    dbg_dump(m, f"y{b}", y_tok)
    m._wq = (("DW", dw),)
    scr2d = carve(m.ar, m.off["R2"], (2, NCC, 512), F32)

    def epi_chunk(tc, jc, psum):
        te = P.count("TE")
        xch = m.S[:, jc, tc * 512:(tc + 1) * 512]
        P.emit("vector", lambda e, p=psum, x=xch, d=scr2d[:, tc % 2, jc, :]:
               e.tensor_add(d, p, x), waits=(("TE", te),))

    def epi_tc(tc):
        outs = [m.S[:, jc, tc * 512:(tc + 1) * 512] for jc in range(NCC)]
        m.emit_postln(scr2d[:, tc % 2, :, :], outs)

    _proj_epilogue(m, WA, y_tok, epi_chunk, epi_tc)



def _emit_whb_dma(m, wsrc, h, qoff, koff):
    P = m.P
    dst = m.whb[:, :, h % 2, :, :]
    te = P.count("TE")
    P.emit("gpsimd", lambda e, d=dst[:, :, 0, :],
           s=wsrc[:, :, qoff + h * 64:qoff + h * 64 + 64]:
           e.dma_start(out=d, in_=s), waits=(("TE", te),),
           incs=(("DW", 16),), auto=False)
    P.emit("gpsimd", lambda e, d=dst[:, :, 1, :],
           s=wsrc[:, :, koff + h * 64:koff + h * 64 + 64]:
           e.dma_start(out=d, in_=s), incs=(("DW", 16),), auto=False)
    return P.count("DW")


def _perh_qk(m, wsrc, h, qh, kh, src, qoff, koff, dw, dw_next_cb=None,
             qscale=None, bq_col=None, bk_col=None):
    P, pool = m.P, m.pool
    par = h % 2
    dst = m.whb[:, :, par, :, :]
    for i, (out_t, col) in enumerate(((qh, bq_col), (kh, bk_col))):
        for tc in range(NTC):
            pairs = [(dst[:, cc, i, :], src[:, cc, tc * 512:(tc + 1) * 512])
                     for cc in range(NCC)]
            wx = ((("DW", dw),) + m._wq) if (tc == 0 and i == 0) else ()
            bk_, out = mm_group(P, pool, pairs, 512, waits=wx, rows=64)
            if tc == 0 and i == 0:
                m._wq = ()
            te = P.count("TE")
            d = out_t[0:64, tc * 512:(tc + 1) * 512]
            s64 = out
            if col is not None:
                P.emit("vector", lambda e, d=d, s=s64, c=col:
                       e.scalar_tensor_tensor(d, s, 1.0,
                                              c.broadcast_to([64, 512]),
                                              ALU.mult, ALU.add),
                       waits=(("TE", te),))
                pool.release(bk_, "VE")
            elif i == 0 and qscale is not None:
                P.emit("vector", lambda e, d=d, s=s64, q=qscale:
                       e.tensor_scalar_mul(d, s, q), waits=(("TE", te),))
                pool.release(bk_, "VE")
            else:
                P.emit("vector", lambda e, d=d, s=s64: e.tensor_copy(d, s),
                       waits=(("TE", te),))
                pool.release(bk_, "VE")
            if i == 1 and tc == 0 and dw_next_cb is not None:
                dw_next_cb()


def emit_block_b_attn(m, b):
    P, pool = m.P, m.pool
    w = m.w
    wsrc = w["L0_ffn_c_attn_w"].rearrange("(cc p) j -> p cc j", p=128)
    wp = w["L0_ffn_c_proj_w"].rearrange("(cc p) j -> p cc j", p=128)
    WA = carve(m.ar, m.off["R5"], (NCC, 1024), BF16)
    h1 = carve(m.ar, m.off["R1"], (NCC, T), BF16)
    qh = carve(m.ar, m.off["R2"], (T,), BF16)
    kh = carve(m.ar, m.off["R2"] + 4 * KB, (T,), BF16)
    m.E = carve(m.ar, m.off["R2"] + 8 * KB, (16, 512), BF16)
    vtm = carve(m.ar, m.off["R3"], (NTT, 16, 65), BF16)
    y_tok = carve(m.ar, m.off["R4"], (NTT, CD), BF16)
    scr = carve(m.ar, m.off["R2"] + 8 * KB, (NCC, 512), F32)

    m._h1ap = h1
    ln1_off = m.load_vec_cols(w["L0_ffn_ln1_w"], "b_ln1")
    ln2_off = m.load_vec_cols(w["L0_ffn_ln2_w"], "b_ln2")
    ve_pre = P.count("VE")
    for tc in range(NTC):
        chunks = [m.S[:, cc, tc * 512:(tc + 1) * 512] for cc in range(NCC)]
        outs = [h1[:, cc, tc * 512:(tc + 1) * 512] for cc in range(NCC)]
        bm, br = m.emit_ln_stats(chunks, True, 1e-5)
        if tc == 0:
            P.wait("vector", "VE", ve_pre)
        m.emit_ln_apply(chunks, outs, bm, br, scr, scale_off=ln1_off)
    dw = dma_w(m, WA, wsrc[:, :, 2048:3072])
    ve = P.count("VE")
    m._wq = (("DW", dw), ("VE", ve))
    _vtm_make(m, WA, h1, vtm)
    dw = None
    dwh = {0: _emit_whb_dma(m, wsrc, 0, 0, 1024)}

    for h in range(H):
        m._wq = ()

        def pref(h=h):
            if h + 1 < H:
                dwh[h + 1] = _emit_whb_dma(m, wsrc, h + 1, 0, 1024)
            if m._prepass_w is None or m._prepass_w[1] == 0:
                pass
        _perh_qk(m, wsrc, h, qh, kh, h1, 0, 1024, dwh[h], pref, qscale=0.125)
        if b == 0:
            if h < 8:
                P.emit("gpsimd", lambda e, h=h: e.dma_start(
                    out=m.w1bf[:, h * 2048:(h + 1) * 2048],
                    in_=m.w["L0_ffn_w1"][:, h * 2048:(h + 1) * 2048]),
                    incs=(("DW", 16),), auto=False)
            else:
                hh = h - 8
                P.emit("gpsimd", lambda e, hh=hh: e.dma_start(
                    out=m.w2bf[hh * 2048:(hh + 1) * 2048, :],
                    in_=m.w["L0_ffn_w2"][hh * 2048:(hh + 1) * 2048, :]),
                    incs=(("DW", 16),), auto=False)
            if h == 15:
                m._prepass_w = ("DW", P.count("DW"))
        if h == 0:
            dw = dma_w(m, WA, wp)
        _attn_core(m, h, qh, kh, vtm, y_tok, EXP_C_B)
    m._wq = (("DW", dw),)
    h2 = h1

    scr2d = carve(m.ar, m.off["R2"], (2, NCC, 512), F32)

    def epi_chunk(tc, jc, psum):
        te = P.count("TE")
        xch = m.S[:, jc, tc * 512:(tc + 1) * 512]
        P.emit("vector", lambda e, p=psum, x=xch, d=scr2d[:, tc % 2, jc, :]:
               e.tensor_add(d, p, x), waits=(("TE", te),))
        P.emit("vector", lambda e, x=xch, d=scr2d[:, tc % 2, jc, :]:
               e.tensor_add(x, d, x))

    def epi_tc(tc):
        outs = [h2[:, jc, tc * 512:(tc + 1) * 512] for jc in range(NCC)]
        m.emit_postln(scr2d[:, tc % 2, :, :], outs, scale_off=ln2_off)

    _proj_epilogue(m, WA, y_tok, epi_chunk, epi_tc)


def emit_block_c(m, b):
    P, pool = m.P, m.pool
    w = m.w
    wsrc = w["L1_attn_W1"].rearrange("(cc p) j -> p cc j", p=128)
    wp = w["L1_attn_W2"].rearrange("(cc p) j -> p cc j", p=128)
    WA = carve(m.ar, m.off["R5"], (NCC, 1024), BF16)
    qh = carve(m.ar, m.off["R2"], (T,), BF16)
    kh = carve(m.ar, m.off["R2"] + 4 * KB, (T,), BF16)
    m.E = carve(m.ar, m.off["R2"] + 8 * KB, (16, 512), BF16)
    vtm = carve(m.ar, m.off["R3"], (NTT, 16, 65), BF16)
    y_tok = carve(m.ar, m.off["R4"], (NTT, CD), BF16)
    scr2d = carve(m.ar, m.off["R2"], (2, NCC, 512), F32)

    b1k_off = m.load_vec_cols(w["L1_attn_b1"][0:1024], "c_b1k", per_head=True)
    b1q_off = m.load_vec_cols(w["L1_attn_b1"][1024:2048], "c_b1q", per_head=True)
    b2_off = m.load_vec_cols(w["L1_attn_b2"], "c_b2")
    te = P.count("TE")
    P.emit("sync", lambda e: e.dma_start(out=m.vrow,
           in_=w["L1_attn_b1"][2048:3072].unsqueeze(0)), waits=(("TE", te),),
           incs=(("DX", 16),), auto=False)
    dx = P.count("DX")
    ve_pre = P.count("VE")

    dw = dma_w(m, WA, wsrc[:, :, 2048:3072])
    m._wq = (("DW", dw), ("DX", dx), ("VE", ve_pre))
    _vtm_make(m, WA, m.S, vtm, bias_row=m.vrow)
    dw = None
    dwh = {0: _emit_whb_dma(m, wsrc, 0, 1024, 0)}
    for h in range(H):
        m._wq = ()
        bqc = m.vcols[0:64, b1q_off + h:b1q_off + h + 1]
        bkc = m.vcols[0:64, b1k_off + h:b1k_off + h + 1]

        def pref(h=h):
            if h + 1 < H:
                dwh[h + 1] = _emit_whb_dma(m, wsrc, h + 1, 1024, 0)
        _perh_qk(m, wsrc, h, qh, kh, m.S, 1024, 0, dwh[h], pref,
                 bq_col=bqc, bk_col=bkc)
        if b == 0 and h < 4:
            P.emit("gpsimd", lambda e, h=h: e.dma_start(
                out=m.f1bf.rearrange("c es f -> c (es f)")[
                    h * 256:(h + 1) * 256, :],
                in_=m.w["L1_ffn_f1"].rearrange("c e s2 f -> c (e s2 f)")[
                    h * 256:(h + 1) * 256, :]), incs=(("DW", 16),), auto=False)
        if b == 0 and 4 <= h < 8:
            hh = h - 4
            P.emit("gpsimd", lambda e, hh=hh: e.dma_start(
                out=m.f2bf.rearrange("es f c -> (es f) c")[
                    hh * 1024:(hh + 1) * 1024, :],
                in_=m.w["L1_ffn_f2"].rearrange("e s2 f c -> (e s2 f) c")[
                    hh * 1024:(hh + 1) * 1024, :]), incs=(("DW", 16),), auto=False)
        if b == 0 and h == 8:
            m._prepass_f = ("DW", P.count("DW"))
        if h == 0:
            dw = dma_w(m, WA, wp)
        _attn_core(m, h, qh, kh, vtm, y_tok, None)
    m._wq = (("DW", dw),)

    def epi_chunk(tc, jc, psum):
        te = P.count("TE")
        c = m.vcols[:, b2_off + jc:b2_off + jc + 1]
        P.emit("vector", lambda e, p=psum, d=scr2d[:, tc % 2, jc, :], c=c:
               e.scalar_tensor_tensor(d, p, 1.0, c.broadcast_to([128, 512]),
                                      ALU.mult, ALU.add), waits=(("TE", te),))
        xch = m.S[:, jc, tc * 512:(tc + 1) * 512]
        P.emit("vector", lambda e, x=xch, d=scr2d[:, tc % 2, jc, :]:
               e.tensor_add(d, d, x))

    def epi_tc(tc):
        outs = [m.S[:, jc, tc * 512:(tc + 1) * 512] for jc in range(NCC)]
        m.emit_postln(scr2d[:, tc % 2, :, :], outs)

    _proj_epilogue(m, WA, y_tok, epi_chunk, epi_tc)


def emit_block_b(m, b):
    emit_block_b_attn(m, b)
    m.P.fence()
    dbg_dump(m, f"bs{b}", m.S)
    dbg_dump(m, f"bh2_{b}", m._h1ap)
    dbg_dump(m, f"bgf{b}", m.gfm) if False else None
    emit_block_b_moe(m, b)
    dbg_dump(m, f"bgate{b}", m.gfm)


def emit_block_b_moe(m, b):
    P, pool = m.P, m.pool
    w = m.w
    h2 = carve(m.ar, m.off["R1"], (NCC, T), BF16)
    y_acc = carve(m.ar, m.off["R3"], (NCC, 1024), F32)
    w1b = carve(m.ar, m.off["R2"], (NCC, T), BF16)
    w2b = carve(m.ar, m.off["R4"], (16, CD), BF16)
    w1r = m.w1bf.rearrange("(cc p) f -> p cc f", p=128)
    w2r = m.w2bf.rearrange("(fc p) c -> p fc c", p=128)
    pw = m._prepass_w if m._prepass_w else ("DW", 0)

    te0 = P.count("TE")
    P.emit("gpsimd", lambda e: e.dma_start(
        out=m.rwsb, in_=w["L0_ffn_router_w"].rearrange("(cc p) e -> p cc e", p=128)),
        waits=(("TE", te0),), incs=(("DW", 16),), auto=False)
    dw_rw = P.count("DW")
    for e8 in range(8):
        P.emit("gpsimd", lambda e, e8=e8: e.memset(m.onehot[:, e8, :], 1.0),
               waits=(("GE", P.count("GE")),))
        P.emit("gpsimd", lambda e, e8=e8: e.affine_select(
            out=m.onehot[:, e8, :], in_=m.onehot[:, e8, :],
            compare_op=ALU.is_equal, fill=0.0, base=-e8,
            pattern=[[0, 128]], channel_multiplier=1),
            waits=(("GE", P.count("GE")),))
    for tc in range(NTC):
        bk_rl, gwl = pool.take()
        rl = pool.ap[0:8, bk_rl, :]
        mm_into(P, rl, [(m.rwsb[:, cc, :], h2[:, cc, tc * 512:(tc + 1) * 512])
                        for cc in range(NCC)],
                waits=(("DW", dw_rw),) + tuple(gwl))
        te = P.count("TE")
        P.emit("scalar", lambda e, rl=rl: e.copy(m.sq[0:8, 0:512], rl),
               waits=(("TE", te),))
        pool.release(bk_rl, "SE")
        se = P.count("SE")
        ltm_sb = m.sq[:, 112:144]
        for i in range(4):
            bk_t, gwt = pool.take()
            outv = pool.ap[:, bk_t, 0:8]
            P.emit("tensor", lambda e, o=outv,
                   s=m.sq[0:8, i * 128:(i + 1) * 128]:
                   e.transpose(o, s, m.ident_f),
                   waits=(("SE", se),) + tuple(gwt),
                   incs=(("TE", 1),))
            te = P.count("TE")
            P.emit("vector", lambda e, o=outv, d=ltm_sb[:, i * 8:(i + 1) * 8]:
                   e.tensor_copy(d, o), waits=(("TE", te),))
            pool.release(bk_t, "VE")
        ltm = ltm_sb.rearrange("p (a b) -> p a b", a=4)

        def vv(fn):
            P.emit("vector", fn, waits=(("VE", P.count("VE")),))
        m1 = m.grows[:, 0:4]
        m2 = m.grows[:, 4:8]
        p1 = m.grows[:, 8:12]
        p2 = m.grows[:, 12:16]
        eq1 = m.gscr[:, 0, :].rearrange("p (a b) -> p a b", a=4)
        lm = m.gscr[:, 1, :].rearrange("p (a b) -> p a b", a=4)
        gg = m.gscr[:, 2, 0:32].bitcast(BF16)[:, 0:32].rearrange(
            "p (a b) -> p a b", a=4)
        t1 = m.gscr[:, 1, 0:16].rearrange("p (a b) -> p a b", a=4)
        t2 = m.gscr[:, 1, 16:24].rearrange("p (a b) -> p a b", a=4)
        vv(lambda e, ltm=ltm, t1=t1: e.tensor_max(
            t1, ltm[:, :, 0::2], ltm[:, :, 1::2]))
        vv(lambda e, t1=t1, t2=t2: e.tensor_max(
            t2, t1[:, :, 0::2], t1[:, :, 1::2]))
        vv(lambda e, t2=t2, m1=m1: e.tensor_max(
            m1.rearrange("p (a o) -> p a o", o=1), t2[:, :, 0:1], t2[:, :, 1:2]))
        vv(lambda e, ltm=ltm, eq1=eq1, m1=m1: e.tensor_tensor(
            eq1, ltm, m1.rearrange("p (a o) -> p a o", o=1).broadcast_to([128, 4, 8]),
            ALU.is_equal))
        vv(lambda e, ltm=ltm, eq1=eq1, lm=lm: e.scalar_tensor_tensor(
            lm, eq1, NEG, ltm, ALU.mult, ALU.add))
        u1 = m.gscr[:, 2, 0:16].rearrange("p (a b) -> p a b", a=4)
        u2 = m.gscr[:, 2, 16:24].rearrange("p (a b) -> p a b", a=4)
        vv(lambda e, lm=lm, u1=u1: e.tensor_max(
            u1, lm[:, :, 0::2], lm[:, :, 1::2]))
        vv(lambda e, u1=u1, u2=u2: e.tensor_max(
            u2, u1[:, :, 0::2], u1[:, :, 1::2]))
        vv(lambda e, u2=u2, m2=m2: e.tensor_max(
            m2.rearrange("p (a o) -> p a o", o=1), u2[:, :, 0:1], u2[:, :, 1:2]))
        vv(lambda e, lm=lm, m2=m2: e.tensor_tensor(
            lm, lm, m2.rearrange("p (a o) -> p a o", o=1).broadcast_to([128, 4, 8]),
            ALU.is_equal))
        vv(lambda e, m1=m1, m2=m2: e.tensor_sub(m2, m2, m1))
        ve = P.count("VE")
        P.emit("scalar", lambda e, p1=p1, m2=m2: e.activation(
            p1, m2, AF.Sigmoid, scale=-1.0), waits=(("VE", ve),))
        se = P.count("SE")
        P.emit("vector", lambda e, p1=p1, p2=p2: e.tensor_scalar(
            p2, p1, -1.0, 1.0, ALU.mult, ALU.add),
            waits=(("SE", se), ("VE", P.count("VE"))))
        vv(lambda e, eq1=eq1, p1=p1: e.tensor_mul(
            eq1, eq1, p1.rearrange("p (a o) -> p a o", o=1).broadcast_to([128, 4, 8])))
        vv(lambda e, lm=lm, p2=p2: e.tensor_mul(
            lm, lm, p2.rearrange("p (a o) -> p a o", o=1).broadcast_to([128, 4, 8])))
        vv(lambda e, gg=gg, eq1=eq1, lm=lm: e.tensor_add(gg, eq1, lm))
        if tc == 0:
            dbg_dump(m, f"gsq{b}", m.sq)
        ve = P.count("VE")
        for i in range(4):
            bk_g, gwg = pool.take()
            outg = pool.ap[0:8, bk_g, 0:64].bitcast(BF16)
            P.emit("tensor", lambda e, o=outg,
                   s=m.gscr[:, 2, 0:32].bitcast(BF16)[:, i * 8:(i + 1) * 8]:
                   e.transpose(o, s, m.ident_b), waits=(("VE", ve),) + tuple(gwg),
                   incs=(("TE", 1),))
            te = P.count("TE")
            P.emit("scalar", lambda e, o=outg, d=m.gfm[:, tc * 512 + i * 128:
                   tc * 512 + (i + 1) * 128]: e.copy(d, o), waits=(("TE", te),))
            pool.release(bk_g, "SE")
    se_g = P.count("SE")
    ge_g = P.count("GE")
    for half in range(2):
        t0 = half * 1024
        P.emit("gpsimd", lambda e: e.memset(
            y_acc.rearrange("p a b -> p (a b)"), 0.0))
        ge_z = P.count("GE")
        te = P.count("TE")
        P.emit("sync", lambda e: e.dma_start(out=w1b, in_=w1r[:, :, 0:2048]),
               waits=(("TE", te), pw), incs=(("DX", 16),), auto=False)
        dw1 = ("DX", P.count("DX"))
        P.emit("sync", lambda e: e.dma_start(out=w2b, in_=w2r[:, 0:16, :]),
               waits=(pw,), incs=(("DX", 16),), auto=False)
        dw2 = ("DX", P.count("DX"))
        for e8 in range(8):
            dw1_next = []
            for tq in range(4):
                tg = t0 + tq * 256
                bg, gwg = pool.take(hold=True)
                gb = pool.bank(bg, 256)
                mm_into(P, gb, [(m.onehot[:, e8, :], m.gfm[:, tg:tg + 256])],
                        waits=(("SE", se_g), ("GE", ge_g)) + tuple(gwg))
                ve_fc = []
                for fc in range(16):
                    pairs = [(w1b[:, cc, fc * 128:(fc + 1) * 128],
                              h2[:, cc, tg:tg + 256]) for cc in range(NCC)]
                    wts = ()
                    if isinstance(dw1, list):
                        if tq == 0:
                            wts = (dw1[fc],)
                    elif tq == 0 and fc == 0:
                        wts = (dw1,)
                    bh, out = mm_group(P, pool, pairs, 256, waits=wts)
                    te = P.count("TE")
                    P.emit("scalar", lambda e, d=m.hprime[:, fc, :], s=out:
                           e.activation(d, s, AF.Gelu), waits=(("TE", te),))
                    pool.release(bh, "SE")
                    se = P.count("SE")
                    P.emit("vector", lambda e, d=m.hprime[:, fc, :], gb=gb:
                           e.tensor_mul(d, d, gb), waits=(("SE", se),))
                    ve_fc.append(P.count("VE"))
                    if e8 < 7 and tq == 3:
                        te_p = P.count("TE")
                        P.emit("sync", lambda e, e8=e8, fc=fc: e.dma_start(
                            out=w1b[:, :, fc * 128:(fc + 1) * 128],
                            in_=w1r[:, :, (e8 + 1) * 2048 + fc * 128:
                                    (e8 + 1) * 2048 + (fc + 1) * 128]),
                            waits=(("TE", te_p),), incs=(("DX", 16),), auto=False)
                        dw1_next.append(("DX", P.count("DX")))
                pool.release(bg, "VE")
                if False:
                    te = P.count("TE")
                    P.emit("gpsimd", lambda e, e8=e8: e.dma_start(
                        out=w1b, in_=w1r[:, :, (e8 + 1) * 2048:(e8 + 2) * 2048]),
                        waits=(("TE", te),), incs=(("DW", 16),), auto=False)
                    dw1 = P.count("DW")
                for cc in range(NCC):
                    by, gwy = pool.take()
                    out = pool.bank(by, 256)
                    for fc in range(16):
                        wts = [("VE", ve_fc[fc])]
                        if fc == 0:
                            wts += list(gwy)
                        if tq == 0 and cc == 0 and fc == 0:
                            wts.append(dw2)
                        P.emit("tensor",
                               lambda e, l=w2b[:, fc, cc * 128:(cc + 1) * 128],
                               r=m.hprime[:, fc, :], s=(fc == 0), t=(fc == 15),
                               out=out: e.matmul(out, l, r, start=s, stop=t),
                               waits=tuple(wts),
                               incs=((("TE", 1),) if fc == 15 else ()))
                    te = P.count("TE")
                    dsty = y_acc[:, cc, tq * 256:(tq + 1) * 256]
                    wv = (("TE", te),)
                    if e8 == 0 and tq == 0 and cc == 0:
                        wv = wv + (("GE", ge_z),)
                    P.emit("vector", lambda e, d=dsty, s=out: e.tensor_add(d, d, s),
                           waits=wv)
                    pool.release(by, "VE")
            if e8 < 7:
                te = P.count("TE")
                P.emit("sync", lambda e, e8=e8: e.dma_start(
                    out=w2b, in_=w2r[:, (e8 + 1) * 16:(e8 + 2) * 16, :]),
                    waits=(("TE", te),), incs=(("DX", 16),), auto=False)
                dw2 = ("DX", P.count("DX"))
                dw1 = dw1_next
        for tc in range(2):
            for jc in range(NCC):
                ch = y_acc[:, jc, tc * 512:(tc + 1) * 512]
                xch = m.S[:, jc, t0 + tc * 512:t0 + (tc + 1) * 512]
                P.emit("vector", lambda e, c=ch, x=xch: e.tensor_add(c, c, x))
            scrv = y_acc[:, :, tc * 512:(tc + 1) * 512]
            outs = [m.S[:, jc, t0 + tc * 512:t0 + (tc + 1) * 512]
                    for jc in range(NCC)]
            m.emit_postln(scrv, outs)
        m.P.fence(("gpsimd",))


def emit_block_d(m, b):
    P, pool = m.P, m.pool
    w = m.w
    R1, R2, R3, R4, R5 = (m.off[k] for k in ("R1", "R2", "R3", "R4", "R5"))
    x_tok = carve(m.ar, R1, (NTT, CD), BF16)
    disp = carve(m.ar, R2, (NCC, 16, 128), BF16)
    packs = carve(m.ar, R3, (NCC, 16, 128), BF16)
    logits_tm = carve(m.ar, R4, (NTT, 128), F32)
    l_fm = carve(m.ar, R4 + 8 * KB, (16, 128), F32)
    maxg = carve(m.ar, R4 + 16 * KB, (64,), F32)
    P_fm = carve(m.ar, R4 + 17 * KB, (NTT, 128), BF16)
    P4 = carve(m.ar, R4, (NTT, 4, 128), BF16)
    inner = carve(m.ar, R4 + 21 * KB, (128, 16), BF16)[0:32, :, :]
    bias_nat = carve(m.ar, R4 + 29 * KB, (32,), F32)
    interT = carve(m.ar, R4 + 30 * KB, (4, 128), BF16)
    ctrl_sb = carve(m.ar, R5, (NCC, 128), BF16)
    f1blk = carve(m.ar, R5, (NCC, 2, 16, 32), BF16)
    f2str = carve(m.ar, R5, (2, 4, NCC, 128), BF16)[0:32, :, :, :, :]
    bias_sb = m.sq[0:32, 128:256]
    r1 = carve(m.ar, R2, (1024,), F32)
    r2 = carve(m.ar, R2 + 4 * KB, (512,), F32)
    r3 = carve(m.ar, R2 + 6 * KB, (256,), F32)
    r4 = carve(m.ar, R2 + 7 * KB, (128,), F32)

    te0 = P.count("TE")
    P.emit("sync", lambda e: e.dma_start(out=m.tieq, in_=m.cdata),
           incs=(("DX", 16),), auto=False)
    P.emit("gpsimd", lambda e: e.dma_start(
        out=ctrl_sb, in_=w["L1_ffn_controller"].rearrange(
            "(cc p) e s2 -> p cc (e s2)", p=128)), waits=(("TE", te0),),
        incs=(("DW", 16),), auto=False)
    P.emit("gpsimd", lambda e: e.dma_start(
        out=bias_nat, in_=w["L1_ffn_bias"].rearrange("e s2 f -> (e s2) f")),
        incs=(("DW", 16),), auto=False)
    dw0 = P.count("DX")
    dwc = P.count("DW")
    bkb, gwb = pool.take()
    outb = pool.bank(bkb, 128)[0:32, :]
    P.emit("tensor", lambda e: e.transpose(outb, bias_nat, m.ident_f128),
           waits=(("DW", dwc),) + tuple(gwb), incs=(("TE", 1),))
    te = P.count("TE")
    P.emit("vector", lambda e: e.tensor_copy(bias_sb, outb), waits=(("TE", te),))
    pool.release(bkb, "VE")
    for tt in range(NTT):
        for cc in range(NCC):
            transpose128(P, pool, m.S[:, cc, tt * 128:(tt + 1) * 128],
                         x_tok[:, tt, cc * 128:(cc + 1) * 128], m.ident_b)
    dx0 = P.count("DX")
    for tt in range(NTT):
        pairs = [(m.S[:, cc, tt * 128:(tt + 1) * 128], ctrl_sb[:, cc, :])
                 for cc in range(NCC)]
        bk, out = mm_group(P, pool, pairs, 128,
                           waits=(("DW", dwc),) if tt == 0 else ())
        te = P.count("TE")
        wts = (("TE", te),)
        if tt == 0:
            wts = wts + (("DX", dx0),)
        P.emit("vector", lambda e, d=logits_tm[:, tt, :], s=out:
               e.scalar_tensor_tensor(d, s, 1.0,
                                      m.tieq[:, 0:1].broadcast_to([128, 128]),
                                      ALU.mult, ALU.add), waits=wts)
        pool.release(bk, "VE")
    ve = P.count("VE")
    for tt in range(NTT):
        bk, gw = pool.take()
        out = pool.bank(bk, 128)
        P.emit("tensor", lambda e, o=out, s=logits_tm[:, tt, :]:
               e.transpose(o, s, m.ident_f128),
               waits=(("VE", ve),) + tuple(gw) if tt == 0 else tuple(gw),
               incs=(("TE", 1),))
        te = P.count("TE")
        P.emit("vector", lambda e, d=l_fm[:, tt, :], s=out: e.tensor_copy(d, s),
               waits=(("TE", te),))
        pool.release(bk, "VE")

    def vv(fn):
        P.emit("vector", fn, waits=(("VE", P.count("VE")),))

    L = l_fm.rearrange("p a b -> p (a b)")
    vv(lambda e: e.tensor_max(r1, L[:, 0::2], L[:, 1::2]))
    vv(lambda e: e.tensor_max(r2, r1[:, 0::2], r1[:, 1::2]))
    vv(lambda e: e.tensor_max(r3, r2[:, 0::2], r2[:, 1::2]))
    vv(lambda e: e.tensor_max(r4, r3[:, 0::2], r3[:, 1::2]))
    vv(lambda e: e.tensor_max(maxg, r4[:, 0::2], r4[:, 1::2]))
    vv(lambda e: e.tensor_tensor(
        P_fm.rearrange("p a b -> p (a b)").rearrange("p (g t) -> p g t", g=64),
        L.rearrange("p (g t) -> p g t", g=64),
        maxg.unsqueeze(2).broadcast_to([128, 64, 32]), ALU.is_equal))
    ve_p = P.count("VE")
    for tt in range(NTT):
        bk, gw = pool.take()
        out = pool.bank_bf16(bk)
        P.emit("tensor", lambda e, o=out, s=P_fm[:, tt, :]:
               e.transpose(o, s, m.ident_b),
               waits=(("VE", ve_p),) + tuple(gw), incs=(("TE", 1),))
        te = P.count("TE")
        for gl in range(4):
            P.emit("vector", lambda e, d=P4[:, tt, gl, :], s=out, gl=gl:
                   e.scalar_tensor_tensor(
                       d, s, 1.0, m.tieq[:, 1 + gl:2 + gl].broadcast_to([128, 128]),
                       ALU.mult, ALU.mult),
                   waits=(("TE", te),) if gl == 0 else ())
        pool.release(bk, "VE")
    ve_p4 = P.count("VE")
    dbg_dump(m, f"dP{b}", P_fm)
    dbg_dump(m, f"dL{b}", l_fm)

    if m._prepass_f is None:
        P.emit("gpsimd", lambda e: e.dma_start(
            out=m.f1bf.rearrange("c es f -> c (es f)"),
            in_=w["L1_ffn_f1"].rearrange("c e s2 f -> c (e s2 f)")),
            incs=(("DW", 16),), auto=False)
        P.emit("gpsimd", lambda e: e.dma_start(
            out=m.f2bf.rearrange("es f c -> (es f) c"),
            in_=w["L1_ffn_f2"].rearrange("e s2 f c -> (e s2 f) c")),
            incs=(("DW", 16),), auto=False)
        m._prepass_f = ("DW", P.count("DW"))
    f1r = m.f1bf.rearrange("(cc p) es f -> p cc es f", p=128)
    f2w = m.f2bf
    for q in range(4):
        tts = range(4 * q, 4 * q + 4)
        te_q = P.count("TE")
        P.emit("gpsimd", lambda e: e.dma_start(
            out=f1blk[:, :, 0, :, :], in_=f1r[:, :, 0:16, :]),
            waits=(("TE", te_q), m._prepass_f), incs=(("DW", 16),), auto=False)
        dwf0 = P.count("DW")
        for i, tt in enumerate(tts):
            for cc in range(NCC):
                pairs = [(x_tok[:, tt, cc * 128:(cc + 1) * 128],
                          P4[:, tt, :, :].rearrange("p a b -> p (a b)"))]
                bk, out = mm_group(P, pool, pairs, 512,
                                   waits=(("VE", ve_p4),) if (i == 0 and cc == 0 and q == 0) else ())
                te = P.count("TE")
                dst = disp[:, cc, 4 * i:4 * i + 4, :]
                P.emit("vector", lambda e, d=dst, s=out.rearrange(
                    "p (a b) -> p a b", a=4): e.tensor_copy(d, s),
                    waits=(("TE", te),))
                pool.release(bk, "VE")
        dwf = dwf0
        ve_disp = P.count("VE")
        for blk in range(8):
            if blk < 7:
                te_b = P.count("TE")
                P.emit("gpsimd", lambda e, blk=blk: e.dma_start(
                    out=f1blk[:, :, (blk + 1) % 2, :, :],
                    in_=f1r[:, :, 16 * (blk + 1):16 * (blk + 2), :]),
                    waits=(("TE", te_b),), incs=(("DW", 16),), auto=False)
            for el in range(16):
                es = blk * 16 + el
                pairs = [(f1blk[:, cc, blk % 2, el, :],
                          disp[:, cc, :, es]) for cc in range(NCC)]
                wts = (("DW", dwf), ("VE", ve_disp)) if el == 0 else ()
                bk, out = mm_group(P, pool, pairs, 16, rows=32, waits=wts)
                te = P.count("TE")
                P.emit("scalar", lambda e, d=inner[:, es, :], s=out, es=es:
                       e.activation(d, s, AF.Relu, bias=bias_sb[:, es:es + 1]),
                       waits=(("TE", te),))
                pool.release(bk, "SE")
            dwf = P.count("DW")
        se_f1 = P.count("SE")
        nblk2 = 32
        P.emit("gpsimd", lambda e: e.dma_start(
            out=f2str[:, 0, :, :, :].rearrange("p a b c -> p a (b c)"),
            in_=f2w[0:4].rearrange("x f c -> f x c")),
            waits=(m._prepass_f,), incs=(("DW", 16),), auto=False)
        dw_blk = [P.count("DW")]
        for grp in range(4):
            banks = []
            for cc in range(NCC):
                bk, gw = pool.take(hold=True)
                banks.append((bk, gw))
            for el in range(32):
                es = grp * 32 + el
                if es % 4 == 0:
                    nb = es // 4 + 1
                    if nb < nblk2:
                        te_b = P.count("TE")
                        P.emit("gpsimd", lambda e, nb=nb: e.dma_start(
                            out=f2str[:, nb % 2, :, :, :].rearrange(
                                "p a b c -> p a (b c)"),
                            in_=f2w[4 * nb:4 * nb + 4].rearrange("x f c -> f x c")),
                            waits=(("TE", te_b),), incs=(("DW", 16),), auto=False)
                        dw_blk.append(P.count("DW"))
                for cc in range(NCC):
                    bk, gw = banks[cc]
                    outv = pool.ap[:, bk, el * 16:(el + 1) * 16]
                    wts = tuple(gw)
                    if cc == 0 and es % 4 == 0:
                        wts = wts + (("DW", dw_blk[es // 4]),)
                    if el == 0 and cc == 0:
                        wts = wts + (("SE", se_f1),)
                    mm_into(P, outv,
                            [(f2str[:, (es // 4) % 2, es % 4, cc, :],
                              inner[:, es, :])], waits=wts)
                    if el == 0:
                        banks[cc] = (bk, ())
            te = P.count("TE")
            for cc in range(NCC):
                bk, _ = banks[cc]
                src = pool.bank(bk).rearrange("p (a b) -> p a b", a=32)
                src = src.transpose([0, 2, 1])
                dst = packs[:, cc, :, grp * 32:(grp + 1) * 32]
                P.emit("vector", lambda e, d=dst, s=src: e.tensor_copy(d, s),
                       waits=(("TE", te),) if cc == 0 else ())
                pool.release(bk, "VE")
        ve_pk = P.count("VE")
        items = []
        for cc in range(NCC):
            for gl in range(16):
                items.append((cc, gl))
        pend = {}
        DEPTH = 3

        def emit_mm(idx):
            cc, gl, ve_i = pend.pop(idx)
            g = q * 16 + gl
            tt, t0 = g // 4, (g % 4) * 32
            it = interT[:, idx % 4, :]
            bk2, gw2 = pool.take()
            out2 = pool.bank(bk2, 32)
            mm_into(P, out2, [(it, P_fm[:, tt, t0:t0 + 32])],
                    waits=(("VE", ve_i),) + tuple(gw2))
            te = P.count("TE")
            dst = m.S[:, cc, g * 32:(g + 1) * 32]
            P.emit("vector", lambda e, d=dst, s=out2: e.tensor_add(d, s, d),
                   waits=(("TE", te),))
            pool.release(bk2, "VE")

        for idx, (cc, gl) in enumerate(items):
            sl = idx % 4
            it = interT[:, sl, :]
            bk, gw = pool.take()
            outt = pool.bank_bf16(bk)
            w0 = (("VE", ve_pk),) if idx == 0 else ()
            P.emit("tensor", lambda e, o=outt, s=packs[:, cc, gl, :]:
                   e.transpose(o, s, m.ident_b),
                   waits=w0 + tuple(gw), incs=(("TE", 1),))
            te = P.count("TE")
            P.emit("vector", lambda e, o=outt, it=it: e.tensor_copy(it, o),
                   waits=(("TE", te),))
            pool.release(bk, "VE")
            pend[idx] = (cc, gl, P.count("VE"))
            if idx >= DEPTH:
                emit_mm(idx - DEPTH)
        for idx in sorted(pend):
            emit_mm(idx)
    scr = carve(m.ar, R2, (NCC, 512), F32)
    ve_s = P.count("VE")
    P.wait("vector", "VE", ve_s)
    for tc in range(NTC):
        chunks = [m.S[:, cc, tc * 512:(tc + 1) * 512] for cc in range(NCC)]
        outs = chunks
        bm, br = m.emit_ln_stats(chunks, True, 1e-5)
        m.emit_ln_apply(chunks, outs, bm, br, scr)


def make_cdata():
    tie = np.tile(np.linspace(0.0, 1e-6, 32, dtype=np.float32), 4)
    cd = np.zeros((128, 8), np.float32)
    cd[:, 0] = tie
    for gl in range(4):
        cd[32 * gl:32 * gl + 32, 1 + gl] = 1.0
    return cd



def build_module(blocks=("A", "B", "C", "D"), nb=2):
    m = Mod(nb=nb)
    m.emit_setup()
    m.P.fence()
    for b in range(nb):
        m.emit_stream_in(b)
        m.P.fence()
        dbg_dump(m, f"s{b}", m.S)
        if "A" in blocks:
            emit_block_a(m, b)
            m.P.fence()
            dbg_dump(m, f"postA{b}", m.S)
        if "B" in blocks:
            emit_block_b(m, b)
            m.P.fence()
            dbg_dump(m, f"postB{b}", m.S)
        if "C" in blocks:
            emit_block_c(m, b)
            m.P.fence()
            dbg_dump(m, f"postC{b}", m.S)
        if "D" in blocks:
            emit_block_d(m, b)
            m.P.fence()
        m.emit_stream_out(b)
        m.P.fence()
    m.P.wait("sync", "DX")
    m.P.finalize()
    return m


DBG_TAGS = set()


def dbg_dump(m, tag, ap):
    if tag not in DBG_TAGS:
        return
    P = m.P
    flat = ap if ap.ndim == 2 else ap.rearrange(
        "p " + " ".join(f"d{i}" for i in range(ap.ndim - 1)) + " -> p (" +
        " ".join(f"d{i}" for i in range(ap.ndim - 1)) + ")")
    t = m.nc.dram_tensor("dbg_" + tag, [flat.shape[0], flat.shape[1]],
                         flat.dtype, kind="ExternalOutput").ap()
    P.fence()
    P.emit("sync", lambda e: e.dma_start(out=t, in_=flat),
           incs=(("DX", 16),), auto=False)
    P.fence()


GROUPS = [("A", "B", "C", "D")]


def kernel(**inputs):
    xfull = np.ascontiguousarray(inputs["L0_attn_x"], dtype=np.float32)
    shards = [xfull[2 * i:2 * i + 2] for i in range(8)]
    wmap = {n: np.ascontiguousarray(inputs[n], dtype=np.float32) for n in WSPEC}
    cur = shards
    for grp in GROUPS:
        m = build_module(blocks=grp, nb=2)
        cd = make_cdata()
        in_maps = [dict(wmap, x=cur[i], cdata=cd) for i in range(8)]
        res = run_bass_kernel_spmd(m.nc, in_maps, core_ids=list(range(8)))
        cur = [np.asarray(r["y"]) for r in res.results]
    return np.concatenate(cur, axis=0)


# version 10 (final)
# speedup vs baseline: 1.1983x; 1.0105x over previous
import numpy as np
from contextlib import ExitStack

import concourse.bass as bass
import concourse.mybir as mybir
from concourse.bass_utils import run_bass_kernel_spmd

F32 = mybir.dt.float32
BF16 = mybir.dt.bfloat16
AF = mybir.ActivationFunctionType
ALU = mybir.AluOpType
AX = mybir.AxisListType

T = 2048
CD = 1024
NCC = 8
NTT = 16
NTC = 4
H = 16
HD = 64
KB = 1024
EXP_C_A = 12.0
EXP_C_B = 6.0
NEG = -1e30

ENGINES = ("tensor", "vector", "scalar", "gpsimd", "sync")
AUTOSEM = {"vector": "VE", "scalar": "SE", "gpsimd": "GE"}


class Prog:
    def __init__(self, nc, stack):
        self.nc = nc
        self.stack = stack
        self.ops = {e: [] for e in ENGINES}
        self.sems = {}

    def sem(self, name):
        if name not in self.sems:
            h = self.stack.enter_context(self.nc.semaphore(name))
            self.sems[name] = [h, 0]
        return self.sems[name]

    def count(self, name):
        return self.sem(name)[1]

    def emit(self, eng, fn, waits=(), incs=(), auto=True):
        incs = list(incs)
        if auto and eng in AUTOSEM:
            incs.append((AUTOSEM[eng], 1))
        inc_t = []
        for name, k in incs:
            s = self.sem(name)
            s[1] += k
            inc_t.append((s[0], k))
        wait_t = []
        for name, cnt in waits:
            s = self.sem(name)
            assert cnt <= s[1], f"wait on future count {name} {cnt}>{s[1]}"
            if cnt > 0:
                wait_t.append((s[0], cnt))

        def run(e, wait_t=wait_t, fn=fn, inc_t=inc_t):
            for h, cnt in wait_t:
                e.wait_ge(h, cnt)
            ins = fn(e)
            for h, k in inc_t:
                assert ins is not None, "then_inc on missing instruction"
                ins = ins.then_inc(h, k)
        self.ops[eng].append(run)

    def wait(self, eng, name, cnt=None):
        cnt = self.count(name) if cnt is None else cnt
        if cnt > 0:
            self.emit(eng, lambda e: None, waits=((name, cnt),), auto=False)

    def snap(self):
        return {n: s[1] for n, s in self.sems.items()}

    def fence(self, engines=ENGINES):
        snap = self.snap()
        src = ("TE", "VE", "SE", "GE", "DW", "DX")
        for w in engines:
            for sname in src:
                if sname in snap:
                    self.wait(w, sname, snap[sname])

    def finalize(self):
        nc = self.nc
        with nc.Block() as block:
            for eng in ENGINES:
                getattr(block, eng)(lambda e, eng=eng: [f(e) for f in self.ops[eng]])


class PsumPool:
    def __init__(self, P, ap):
        self.P = P
        self.ap = ap
        self.guard = [None] * 8
        self.held = set()
        self.rr = 0

    def take(self, hold=False):
        for _ in range(16):
            b = self.rr % 8
            self.rr += 1
            if b not in self.held:
                break
        else:
            raise RuntimeError("no free psum bank")
        if hold:
            self.held.add(b)
        g = self.guard[b]
        self.guard[b] = None
        return b, ([g] if g else [])

    def release(self, b, semname):
        self.guard[b] = (semname, self.P.count(semname))
        self.held.discard(b)

    def bank(self, b, n=512):
        return self.ap[:, b, 0:n]

    def bank_bf16(self, b):
        return self.ap[:, b, 0:64].bitcast(BF16)


def carve(arena, off, shape, dtype):
    esz = {F32: 4, BF16: 2}[dtype]
    n = int(np.prod(shape))
    assert off % 32 == 0, off
    v = arena[:, off:off + n * esz].bitcast(dtype)
    if len(shape) == 1:
        return v
    pat = " ".join(f"d{i}" for i in range(len(shape)))
    kw = {f"d{i}": int(shape[i]) for i in range(len(shape))}
    return v.rearrange(f"p ({pat}) -> p {pat}", **kw)


def mm_group(P, pool, pairs, n_cols, waits=(), hold=False, rows=128):
    b, gw = pool.take(hold=hold)
    out = pool.bank(b, n_cols)
    if rows != 128:
        out = out[0:rows, :]
    n = len(pairs)
    for i, (l, r) in enumerate(pairs):
        w = tuple(waits) + tuple(gw) if i == 0 else ()
        P.emit("tensor",
               lambda e, l=l, r=r, s=(i == 0), t=(i == n - 1), out=out:
               e.matmul(out, l, r, start=s, stop=t),
               waits=w, incs=((("TE", 1),) if i == n - 1 else ()))
    return b, out


def mm_into(P, out, pairs, waits=(), start=True, stop=True, inc=True):
    n = len(pairs)
    for i, (l, r) in enumerate(pairs):
        w = tuple(waits) if i == 0 else ()
        P.emit("tensor",
               lambda e, l=l, r=r, s=(start and i == 0), t=(stop and i == n - 1),
               out=out: e.matmul(out, l, r, start=s, stop=t),
               waits=w,
               incs=((("TE", 1),) if (inc and i == n - 1) else ()))


def transpose128(P, pool, src_bf16, dst_bf16, ident, waits=()):
    b, gw = pool.take()
    out = pool.bank_bf16(b)
    P.emit("tensor", lambda e, out=out, s=src_bf16: e.transpose(out, s, ident),
           waits=tuple(waits) + tuple(gw), incs=(("TE", 1),))
    te = P.count("TE")
    P.emit("vector", lambda e, out=out, d=dst_bf16: e.tensor_copy(d, out),
           waits=(("TE", te),))
    pool.release(b, "VE")


WSPEC = {
    "L0_attn_w_qkv": (1024, 3072), "L0_attn_ln_w": (1024,),
    "L0_attn_w_proj": (1024, 1024),
    "L0_ffn_ln1_w": (1024,), "L0_ffn_c_attn_w": (1024, 3072),
    "L0_ffn_c_proj_w": (1024, 1024), "L0_ffn_ln2_w": (1024,),
    "L0_ffn_router_w": (1024, 8), "L0_ffn_w1": (1024, 16384),
    "L0_ffn_w2": (16384, 1024),
    "L1_attn_W1": (1024, 3072), "L1_attn_b1": (3072,),
    "L1_attn_W2": (1024, 1024), "L1_attn_b2": (1024,),
    "L1_ffn_controller": (1024, 32, 4), "L1_ffn_f1": (1024, 32, 4, 32),
    "L1_ffn_bias": (32, 4, 32), "L1_ffn_f2": (32, 4, 32, 1024),
}


class Mod:
    def __init__(self, nb=2):
        self.stack = ExitStack()
        nc = bass.Bass("TRN2", target_bir_lowering=False)
        self.nc = nc
        self.nb = nb
        self.x = nc.dram_tensor("x", [nb, T, CD], F32, kind="ExternalInput").ap()
        self.y = nc.dram_tensor("y", [nb, T, CD], F32, kind="ExternalOutput").ap()
        self.w = {n: nc.dram_tensor(n, list(s), F32, kind="ExternalInput").ap()
                  for n, s in WSPEC.items()}
        self.cdata = nc.dram_tensor("cdata", [128, 8], F32, kind="ExternalInput").ap()
        self.w1bf = nc.dram_tensor("w1bf", [1024, 16384], BF16, kind="Internal").ap()
        self.w2bf = nc.dram_tensor("w2bf", [16384, 1024], BF16, kind="Internal").ap()
        self.f1bf = nc.dram_tensor("f1bf", [1024, 128, 32], BF16, kind="Internal").ap()
        self.f2bf = nc.dram_tensor("f2bf", [128, 32, 1024], BF16, kind="Internal").ap()
        self._prepass_w = None
        self._prepass_f = None
        self.P = Prog(nc, self.stack)

        sb = lambda name, shape, dt: self.stack.enter_context(
            nc.sbuf_tensor(name, list(shape), dt))[:]
        self.ar = sb("arena", (128, 177 * KB), mybir.dt.uint8)
        self.S = carve(self.ar, 0, (NCC, T), BF16)
        self.off = {"S": 0, "R1": 32 * KB, "R2": 64 * KB, "R3": 96 * KB,
                    "R4": 129 * KB, "R5": 161 * KB}

        self.ident_b = sb("ident_b", (128, 128), BF16)
        self.ident_f = sb("ident_f", (8, 8), F32)
        self.ident_f128 = sb("ident_f128", (128, 128), F32)
        self.madd = sb("madd", (128, 896), BF16)
        self.tri01 = sb("tri01", (128, 896), BF16)
        self.ones_col_f = sb("ones_col_f", (128, 1), F32)
        self.ones_col_b = sb("ones_col_b", (128, 1), BF16)
        self.ones_row_f = sb("ones_row_f", (1, 128), F32)
        self.rows = sb("rows", (1, 1024), F32)
        self.vcols = sb("vcols", (128, 72), F32)
        self.consts = sb("consts", (128, 8), F32)
        self.tieq = sb("tieq", (128, 8), F32)
        self.sq = sb("sq", (128, 512), F32)
        self.sq2 = sb("sq2", (128, 512), F32)
        self.sqb = sb("sqb", (128, 2, 512), BF16)
        self.rcol = sb("rcol", (128, 1), F32)
        self.gscr = self.sq[:, 0:96].rearrange("p (a b) -> p a b", a=3)
        self.grows = self.sq[:, 96:112]
        self.hprime = carve(self.ar, self.off["R5"], (16, 256), BF16)
        self.gfm = carve(self.ar, self.off["R5"] + 8 * KB, (2048,), BF16)[0:8, :]
        self.onehot = carve(self.ar, self.off["R5"] + 12 * KB, (8, 128), BF16)[0:8, :, :]
        self.rwsb = carve(self.ar, self.off["R5"] + 14 * KB, (8, 8), BF16)
        self.vrow = carve(self.ar, self.off["R1"], (1024,), F32)[0:1, :]
        self.ostage = carve(self.ar, self.off["R1"] + 4 * KB, (1024,), F32)
        self.whb = sb("whb", (128, 8, 2, 2, 64), BF16)
        self.ea = sb("ea", (128, 384), BF16)
        ps = self.stack.enter_context(nc.psum_tensor("PS", [128, 8, 512], F32))
        self.pool = PsumPool(self.P, ps[:])
        self.vc_used = 0
        self.vc_cache = {}
        self._dbg_scr = None
        self._dbg_scr_ap = None
        self._wq = ()
        self.E = None
        self.WBUF = None

    def emit_setup(self):
        P = self.P
        g = "gpsimd"

        def gg(fn):
            P.emit(g, fn, waits=(("GE", P.count("GE")),))
        gg(lambda e: e.memset(self.ident_b, 0.0))
        gg(lambda e: e.affine_select(out=self.ident_b, in_=self.ident_b,
           compare_op=ALU.not_equal, fill=1.0, base=0, pattern=[[-1, 128]],
           channel_multiplier=1))
        gg(lambda e: e.memset(self.ident_f, 0.0))
        gg(lambda e: e.affine_select(out=self.ident_f, in_=self.ident_f,
           compare_op=ALU.not_equal, fill=1.0, base=0, pattern=[[-1, 8]],
           channel_multiplier=1))
        gg(lambda e: e.memset(self.ident_f128, 0.0))
        gg(lambda e: e.affine_select(out=self.ident_f128, in_=self.ident_f128,
           compare_op=ALU.not_equal, fill=1.0, base=0, pattern=[[-1, 128]],
           channel_multiplier=1))
        gg(lambda e: e.memset(self.madd, 0.0))
        gg(lambda e: e.affine_select(out=self.madd, in_=self.madd,
           compare_op=ALU.is_ge, fill=NEG, base=-384,
           pattern=[[1, 896]], channel_multiplier=-1))
        gg(lambda e: e.memset(self.tri01, 1.0))
        gg(lambda e: e.affine_select(out=self.tri01, in_=self.tri01,
           compare_op=ALU.is_ge, fill=0.0, base=-384,
           pattern=[[1, 896]], channel_multiplier=-1))
        for i, vv in enumerate((1e-6, 1e-5, -EXP_C_A, -EXP_C_B)):
            gg(lambda e, i=i, vv=vv: e.memset(self.consts[:, i:i + 1], vv))
        gg(lambda e: e.memset(self.ones_col_f, 1.0))
        gg(lambda e: e.memset(self.ones_col_b, 1.0))
        gg(lambda e: e.memset(self.ones_row_f, 1.0))

    def madd_o(self, o, n=512):
        return self.madd[:, 384 - 128 * o:384 - 128 * o + n]

    def load_vec_cols(self, wap, tag, scale=None, per_head=False):
        if tag in self.vc_cache:
            return self.vc_cache[tag]
        P, pool = self.P, self.pool
        n = 16 if per_head else NCC
        off = self.vc_used
        self.vc_used += n
        assert self.vc_used <= 72
        self.vc_cache[tag] = off
        te_pre = P.count("TE")
        P.emit("sync", lambda e, w=wap: e.dma_start(out=self.vrow,
               in_=w.unsqueeze(0)), waits=(("TE", te_pre),),
               incs=(("DX", 16),), auto=False)
        dx = P.count("DX")
        step = 64 if per_head else 128
        for i in range(n):
            b, gw = pool.take()
            out = pool.bank(b, 512)[0:step, 0:1]
            src = self.vrow[:, i * step:(i + 1) * step]
            P.emit("tensor", lambda e, out=out, src=src: e.transpose(
                out, src, self.ident_f[0:1, 0:1]),
                waits=(("DX", dx),) + tuple(gw), incs=(("TE", 1),))
            te = P.count("TE")
            dst = self.vcols[0:step, off + i:off + i + 1]
            if scale is None:
                P.emit("vector", lambda e, d=dst, s=out: e.tensor_copy(d, s),
                       waits=(("TE", te),))
            else:
                P.emit("vector", lambda e, d=dst, s=out, sc=scale:
                       e.tensor_scalar_mul(d, s, sc), waits=(("TE", te),))
            pool.release(b, "VE")
        return off

    def emit_stream_in(self, b):
        P, pool = self.P, self.pool
        xin = carve(self.ar, self.off["R4"], (NTT, CD), BF16)
        src = self.x[b].rearrange("(tt p) c -> p tt c", p=128)
        dws = []
        for ck in range(4):
            P.emit("gpsimd", lambda e, ck=ck: e.dma_start(
                out=xin[:, 4 * ck:4 * ck + 4, :], in_=src[:, 4 * ck:4 * ck + 4, :]),
                incs=(("DW", 16),), auto=False)
            dws.append(P.count("DW"))
        for tt in range(NTT):
            for cc in range(NCC):
                transpose128(P, pool, xin[:, tt, cc * 128:(cc + 1) * 128],
                             self.S[:, cc, tt * 128:(tt + 1) * 128], self.ident_b,
                             waits=(("DW", dws[tt // 4]),) if cc == 0 else ())

    def emit_stream_out(self, b):
        P, pool = self.P, self.pool
        dx_prev = 0
        for tt in range(NTT):
            first_evac = True
            for cc in range(NCC):
                bk, gw = pool.take()
                out = pool.bank_bf16(bk)
                src = self.S[:, cc, tt * 128:(tt + 1) * 128]
                P.emit("tensor", lambda e, out=out, s=src: e.transpose(
                    out, s, self.ident_b), waits=tuple(gw), incs=(("TE", 1),))
                te = P.count("TE")
                dst = self.ostage[:, cc * 128:(cc + 1) * 128]
                w = (("TE", te),)
                if first_evac and dx_prev:
                    w = w + (("DX", dx_prev),)
                    first_evac = False
                P.emit("scalar", lambda e, d=dst, s=out: e.copy(d, s), waits=w)
                pool.release(bk, "SE")
            se = P.count("SE")
            dst = self.y[b, tt * 128:(tt + 1) * 128, :]
            P.emit("sync", lambda e, d=dst: e.dma_start(out=d, in_=self.ostage),
                   waits=(("SE", se),), incs=(("DX", 16),), auto=False)
            dx_prev = P.count("DX")

    def emit_ln_stats(self, chunks, bf16_chunks, eps):
        P, pool = self.P, self.pool
        bs, gs = pool.take(hold=True)
        bq, gq = pool.take(hold=True)
        sm = pool.ap[0:1, bs, :]
        ss = pool.ap[0:1, bq, :]
        ones = self.ones_col_b if bf16_chunks else self.ones_col_f
        te_sq = [0, 0]
        ve0 = P.count("VE")
        for i, ch in enumerate(chunks):
            sqb = self.sqb[:, i % 2, :]
            w = (("TE", te_sq[i % 2]),) if te_sq[i % 2] else ()
            P.emit("vector", lambda e, ch=ch, sqb=sqb: e.tensor_mul(sqb, ch, ch),
                   waits=w)
            ve = P.count("VE")
            mm_into(P, sm, [(ones, ch)],
                    waits=tuple(gs) + ((("VE", ve0),) if i == 0 else ()),
                    start=(i == 0), stop=(i == 7), inc=True)
            gs = ()
            mm_into(P, ss, [(self.ones_col_b, sqb)],
                    waits=(("VE", ve),) + tuple(gq),
                    start=(i == 0), stop=(i == 7), inc=True)
            gq = ()
            te_sq[i % 2] = P.count("TE")
        te = P.count("TE")
        mean = self.rows[0:1, 0:512]
        var = self.rows[0:1, 512:1024]
        rstd = mean
        tmp = self.sq[0:1, 0:512]
        P.emit("scalar", lambda e, sm=sm: e.mul(mean, sm, 1.0 / CD), waits=(("TE", te),))
        P.emit("scalar", lambda e, ss=ss: e.mul(var, ss, 1.0 / CD))
        pool.release(bs, "SE")
        pool.release(bq, "SE")
        se = P.count("SE")
        bm, gwm = pool.take(hold=True)
        mm_into(P, pool.bank(bm), [(self.ones_row_f, mean)],
                waits=(("SE", se),) + tuple(gwm))
        te_bm = P.count("TE")
        P.emit("vector", lambda e, tmp=tmp: e.tensor_mul(tmp, mean, mean),
               waits=(("SE", se),))
        P.emit("vector", lambda e, tmp=tmp: e.tensor_sub(var, var, tmp))
        ve = P.count("VE")
        epsc = self.consts[0:1, 0:1] if eps == 1e-6 else self.consts[0:1, 1:2]
        P.emit("scalar", lambda e, epsc=epsc: e.activation(var, var, AF.Ln, bias=epsc),
               waits=(("VE", ve), ("TE", te_bm)))
        P.emit("scalar", lambda e, rstd=rstd: e.activation(rstd, var, AF.Exp, scale=-0.5))
        se = P.count("SE")
        br, gwr = pool.take(hold=True)
        mm_into(P, pool.bank(br), [(self.ones_row_f, rstd)],
                waits=(("SE", se),) + tuple(gwr))
        return bm, br

    def emit_ln_apply(self, chunks, out_chunks, bm, br, scr, scale_off=None):
        P, pool = self.P, self.pool
        te = P.count("TE")
        mb = pool.bank(bm)
        rb = pool.bank(br)
        for i, (ch, outv) in enumerate(zip(chunks, out_chunks)):
            t = scr[:, i, :]
            P.emit("vector", lambda e, t=t, ch=ch, mb=mb: e.tensor_sub(t, ch, mb),
                   waits=(("TE", te),) if i == 0 else ())
            if scale_off is None:
                P.emit("vector", lambda e, o=outv, t=t, rb=rb: e.tensor_mul(o, t, rb))
            else:
                sc = self.vcols[:, scale_off + i:scale_off + i + 1]
                P.emit("vector", lambda e, o=outv, t=t, rb=rb, sc=sc:
                       e.scalar_tensor_tensor(o, t, sc, rb, ALU.mult, ALU.mult))
        pool.release(bm, "VE")
        pool.release(br, "VE")

    def emit_postln(self, scr, out_chunks, scale_off=None, eps=1e-5):
        chunks = [scr[:, jc, :] for jc in range(NCC)]
        bm, br = self.emit_ln_stats(chunks, False, eps)
        self.emit_ln_apply(chunks, out_chunks, bm, br, scr, scale_off=scale_off)



def dma_w(m, dst, src):
    P = m.P
    te = P.count("TE")
    P.emit("gpsimd", lambda e, d=dst, s=src: e.dma_start(out=d, in_=s),
           waits=(("TE", te),), incs=(("DW", 16),), auto=False)
    return P.count("DW")


def _vtm_make(m, wbuf, src, vtm, bias_row=None):
    P, pool = m.P, m.pool
    ge_pre = P.count("GE")
    P.emit("gpsimd", lambda e, v=vtm: e.memset(
        v.rearrange("p a b c -> p (a b c)"), 1.0))
    ge = P.count("GE")
    first = True
    for tt in range(NTT):
        for half in range(2):
            pairs = [(src[:, cc, tt * 128:(tt + 1) * 128],
                      wbuf[:, cc, half * 512:(half + 1) * 512])
                     for cc in range(NCC)]
            if bias_row is not None:
                pairs.append((m.ones_row_f, bias_row[0:1, half * 512:(half + 1) * 512]))
            bk, out = mm_group(P, pool, pairs, 512, waits=m._wq)
            m._wq = ()
            te = P.count("TE")
            dst = vtm[:, tt, 8 * half:8 * half + 8, 0:64]
            w = (("TE", te),)
            if first:
                w = w + (("GE", ge),)
                first = False
            P.emit("vector", lambda e, d=dst, s=out.rearrange("p (h d) -> p h d", h=8):
                   e.tensor_copy(d, s), waits=w)
            pool.release(bk, "VE")


def _attn_core(m, h, qv, kv, vtm, y_tok, softmax_c):
    P, pool = m.P, m.pool
    E = m.E
    se_kc = [0] * 16
    ge_kc = [0] * 16
    for q5 in range(4):
        for kc in range(4 * q5 + 4):
            bk, out = mm_group(
                P, pool, [(kv[0:64, kc * 128:(kc + 1) * 128],
                           qv[0:64, q5 * 512:(q5 + 1) * 512])], 512, waits=m._wq)
            m._wq = ()
            te = P.count("TE")
            o = kc - 4 * q5
            w = (("TE", te),)
            dst = E[:, kc, :]
            if softmax_c is not None:
                if o >= 0:
                    P.emit("vector", lambda e, out=out, o=o: e.tensor_add(
                        out, out, m.madd_o(o)), waits=w)
                    w = (("VE", P.count("VE")),)
                cb = m.consts[:, 3:4]
                P.emit("scalar", lambda e, d=dst, s=out, cb=cb:
                       e.activation(d, s, AF.Exp, bias=cb), waits=w)
                se_kc[kc] = P.count("SE")
                pool.release(bk, "SE")
            else:
                if o >= 0:
                    P.emit("vector", lambda e, d=dst, s=out, o=o: e.tensor_tensor(
                        d, s, m.tri01[:, 384 - 128 * o:896 - 128 * o], ALU.mult),
                        waits=w)
                    pool.release(bk, "VE")
                    ge_kc[kc] = ("VE", P.count("VE"))
                else:
                    P.emit("scalar", lambda e, d=dst, s=out: e.activation(
                        d, s, AF.Copy), waits=w)
                    pool.release(bk, "SE")
                    se_kc[kc] = P.count("SE")
                    ge_kc[kc] = None
        ve = P.count("VE")
        for qt in range(4 * q5, 4 * q5 + 4):
            ql = qt - 4 * q5
            pairs = [(E[:, kc, ql * 128:(ql + 1) * 128], vtm[:, kc, h, 0:65])
                     for kc in range(qt + 1)]
            bk, gw = pool.take()
            out = pool.bank(bk, 65)
            if softmax_c is None:
                wav = (("VE", ve),) + tuple(gw)
                if ge_kc[qt] is not None:
                    wav = wav + (ge_kc[qt],)
                else:
                    wav = wav + (("SE", se_kc[qt]),)
            else:
                wav = (("SE", se_kc[qt]), ("VE", ve)) + tuple(gw)
            mm_into(P, out, pairs, waits=wav)
            te = P.count("TE")
            dst = y_tok[:, qt, h * 64:(h + 1) * 64]
            if softmax_c is not None:
                P.emit("vector", lambda e, out=out: e.reciprocal(
                    m.rcol, out[:, 64:65]),
                    waits=(("TE", te), ("SE", P.count("SE"))))
                ve_r = P.count("VE")
                P.emit("scalar", lambda e, d=dst, out=out: e.activation(
                    d, out[:, 0:64], AF.Copy, scale=m.rcol), waits=(("VE", ve_r),))
                pool.release(bk, "SE")
            else:
                P.emit("vector", lambda e, d=dst, out=out: e.tensor_copy(
                    d, out[:, 0:64]), waits=(("TE", te),))
                pool.release(bk, "VE")


def _proj_epilogue(m, wbuf, y_tok, epi_chunk, epi_tc, pipeline=True):
    P, pool = m.P, m.pool
    yfm = carve(m.ar, m.off["R3"], (NCC, 512), BF16)
    w0 = (("SE", P.count("SE")), ("VE", P.count("VE")))
    for tc in range(NTC):
        for i, tt in enumerate(range(4 * tc, 4 * tc + 4)):
            for cc in range(NCC):
                transpose128(P, pool, y_tok[:, tt, cc * 128:(cc + 1) * 128],
                             yfm[:, cc, i * 128:(i + 1) * 128], m.ident_b,
                             waits=w0)
                w0 = ()
        ve = P.count("VE")
        for jc in range(NCC):
            pairs = [(wbuf[:, cc, jc * 128:(jc + 1) * 128],
                      yfm[:, cc, :]) for cc in range(NCC)]
            bk, out = mm_group(P, pool, pairs, 512,
                               waits=m._wq + ((("VE", ve),) if jc == 0 else ()))
            m._wq = ()
            epi_chunk(tc, jc, out)
            pool.release(bk, "VE")
        if not pipeline:
            epi_tc(tc)
        elif tc > 0:
            epi_tc(tc - 1)
    if pipeline:
        epi_tc(NTC - 1)



def emit_block_a(m, b):
    P, pool = m.P, m.pool
    w = m.w
    wq = w["L0_attn_w_qkv"].rearrange("(cc p) j -> p cc j", p=128)
    wp = w["L0_attn_w_proj"].rearrange("(cc p) j -> p cc j", p=128)
    WA = carve(m.ar, m.off["R5"], (NCC, 1024), BF16)
    WB = carve(m.ar, m.off["R4"], (NCC, 1024), BF16)
    qfm = carve(m.ar, m.off["R1"], (NCC, T), BF16)
    kfm = carve(m.ar, m.off["R2"], (NCC, T), BF16)
    vtm = carve(m.ar, m.off["R3"], (NTT, 16, 65), BF16)
    y_tok = carve(m.ar, m.off["R4"], (NTT, CD), BF16)

    lnw_off = m.load_vec_cols(w["L0_attn_ln_w"], "a_lnw")
    lnwq_off = m.load_vec_cols(w["L0_attn_ln_w"], "a_lnwq", scale=0.125)
    ve_pre = P.count("VE")

    dw = dma_w(m, WA, wq[:, :, 0:1024])
    dwk = dma_w(m, WB, wq[:, :, 1024:2048])
    for part, (dst, coloff, wbuf) in enumerate(
            ((qfm, lnwq_off, WA), (kfm, lnw_off, WB))):
        m._wq = (("DW", dw if part == 0 else dwk), ("VE", ve_pre))
        if part == 1:
            dw_v = dma_w(m, WA, wq[:, :, 2048:3072])
        for tc in range(NTC):
            bss, gss = pool.take(hold=True)
            ss = pool.ap[0:1, bss, :]
            te_sq = [0, 0]
            for jc in range(NCC):
                pairs = [(wbuf[:, cc, jc * 128:(jc + 1) * 128],
                          m.S[:, cc, tc * 512:(tc + 1) * 512]) for cc in range(NCC)]
                bk, out = mm_group(P, pool, pairs, 512, waits=m._wq)
                m._wq = ()
                te = P.count("TE")
                ch = dst[:, jc, tc * 512:(tc + 1) * 512]
                P.emit("vector", lambda e, d=ch, s=out: e.tensor_copy(d, s),
                       waits=(("TE", te),))
                pool.release(bk, "VE")
                sqb = m.sqb[:, jc % 2, :]
                wsq = (("TE", te_sq[jc % 2]),) if te_sq[jc % 2] else ()
                P.emit("vector", lambda e, c=ch, sqb=sqb: e.tensor_mul(sqb, c, c),
                       waits=wsq)
                ve = P.count("VE")
                mm_into(P, ss, [(m.ones_col_b, sqb)],
                        waits=(("VE", ve),) + tuple(gss),
                        start=(jc == 0), stop=(jc == 7), inc=True)
                gss = ()
                te_sq[jc % 2] = P.count("TE")
            rstd = m.rows[0:1, 512:1024]
            P.emit("scalar", lambda e, ss=ss: e.activation(
                rstd, ss, AF.Ln, scale=1.0 / CD, bias=m.consts[0:1, 0:1]),
                waits=(("TE", max(te_sq)),))
            P.emit("scalar", lambda e: e.activation(rstd, rstd, AF.Exp, scale=-0.5))
            pool.release(bss, "SE")
            se = P.count("SE")
            br, gwr = pool.take(hold=True)
            mm_into(P, pool.bank(br), [(m.ones_row_f, rstd)],
                    waits=(("SE", se),) + tuple(gwr))
            te = P.count("TE")
            for jc in range(NCC):
                ch = dst[:, jc, tc * 512:(tc + 1) * 512]
                sc = m.vcols[:, coloff + jc:coloff + jc + 1]
                P.emit("vector", lambda e, c=ch, sc=sc, rb=pool.bank(br):
                       e.scalar_tensor_tensor(c, c, sc, rb, ALU.mult, ALU.mult),
                       waits=(("TE", te),) if jc == 0 else ())
            pool.release(br, "VE")
    dbg_dump(m, f"q{b}", qfm)
    dbg_dump(m, f"k{b}", kfm)
    m._wq = (("DW", dw_v),)
    _vtm_make(m, WA, m.S, vtm)
    dw = dma_w(m, WA, wp)
    dbg_dump(m, f"v{b}", vtm)
    ve_v = P.count("VE")
    first = True
    for h in range(H):
        ro, hc = (h % 2) * 64, h // 2
        qv = qfm[ro:ro + 64, hc, :] if ro == 0 else qfm[64:128, hc, :]
        kv = kfm[ro:ro + 64, hc, :] if ro == 0 else kfm[64:128, hc, :]
        for wdw in range(8):
            kt0, kt1 = 2 * wdw, 2 * wdw + 1
            wx = m._wq + ((("VE", ve_v),) if first else ())
            m._wq = ()
            first = False
            bk, out = mm_group(P, pool, [(kv[:, kt0 * 128:kt0 * 128 + 128],
                                          qv[:, wdw * 256:wdw * 256 + 256])], 256,
                               waits=wx)
            te = P.count("TE")
            P.emit("vector", lambda e, out=out: e.tensor_add(
                out[:, 0:128], out[:, 0:128], m.madd_o(0, 128)),
                waits=(("TE", te),))
            vew = P.count("VE")
            P.emit("scalar", lambda e, out=out: e.activation(
                m.ea[:, 0:256], out, AF.Exp, bias=m.consts[:, 2:3]), waits=(("VE", vew),))
            pool.release(bk, "SE")
            bk2, out2 = mm_group(P, pool, [(kv[:, kt1 * 128:kt1 * 128 + 128],
                                            qv[:, wdw * 256 + 128:wdw * 256 + 256])],
                                 128)
            te = P.count("TE")
            P.emit("vector", lambda e, out2=out2: e.tensor_add(
                out2, out2, m.madd_o(0, 128)), waits=(("TE", te),))
            vew = P.count("VE")
            P.emit("scalar", lambda e, out2=out2: e.activation(
                m.ea[:, 256:384], out2, AF.Exp, bias=m.consts[:, 2:3]), waits=(("VE", vew),))
            pool.release(bk2, "SE")
            if h == 0 and wdw == 0:
                dbg_dump(m, f"ea{b}", m.ea)
            se = P.count("SE")
            for qt, pairs in ((kt0, [(m.ea[:, 0:128], vtm[:, kt0, h, 0:65])]),
                              (kt1, [(m.ea[:, 128:256], vtm[:, kt0, h, 0:65]),
                                     (m.ea[:, 256:384], vtm[:, kt1, h, 0:65])])):
                bky, gwy = pool.take()
                outy = pool.bank(bky, 65)
                mm_into(P, outy, pairs, waits=(("SE", se),) + tuple(gwy))
                te = P.count("TE")
                dst = y_tok[:, qt, h * 64:(h + 1) * 64]
                P.emit("vector", lambda e, outy=outy: e.reciprocal(
                    m.rcol, outy[:, 64:65]),
                    waits=(("TE", te), ("SE", P.count("SE"))))
                ve_r = P.count("VE")
                P.emit("scalar", lambda e, d=dst, outy=outy: e.activation(
                    d, outy[:, 0:64], AF.Copy, scale=m.rcol), waits=(("VE", ve_r),))
                pool.release(bky, "SE")
    dbg_dump(m, f"y{b}", y_tok)
    m._wq = (("DW", dw),)
    scr2d = carve(m.ar, m.off["R2"], (2, NCC, 512), F32)

    def epi_chunk(tc, jc, psum):
        te = P.count("TE")
        xch = m.S[:, jc, tc * 512:(tc + 1) * 512]
        P.emit("vector", lambda e, p=psum, x=xch, d=scr2d[:, tc % 2, jc, :]:
               e.tensor_add(d, p, x), waits=(("TE", te),))

    def epi_tc(tc):
        outs = [m.S[:, jc, tc * 512:(tc + 1) * 512] for jc in range(NCC)]
        m.emit_postln(scr2d[:, tc % 2, :, :], outs)

    _proj_epilogue(m, WA, y_tok, epi_chunk, epi_tc)



def _emit_whb_dma(m, wsrc, h, qoff, koff):
    P = m.P
    dst = m.whb[:, :, h % 2, :, :]
    te = P.count("TE")
    P.emit("gpsimd", lambda e, d=dst[:, :, 0, :],
           s=wsrc[:, :, qoff + h * 64:qoff + h * 64 + 64]:
           e.dma_start(out=d, in_=s), waits=(("TE", te),),
           incs=(("DW", 16),), auto=False)
    P.emit("gpsimd", lambda e, d=dst[:, :, 1, :],
           s=wsrc[:, :, koff + h * 64:koff + h * 64 + 64]:
           e.dma_start(out=d, in_=s), incs=(("DW", 16),), auto=False)
    return P.count("DW")


def _perh_qk(m, wsrc, h, qh, kh, src, qoff, koff, dw, dw_next_cb=None,
             qscale=None, bq_col=None, bk_col=None):
    P, pool = m.P, m.pool
    par = h % 2
    dst = m.whb[:, :, par, :, :]
    for i, (out_t, col) in enumerate(((qh, bq_col), (kh, bk_col))):
        for tc in range(NTC):
            pairs = [(dst[:, cc, i, :], src[:, cc, tc * 512:(tc + 1) * 512])
                     for cc in range(NCC)]
            wx = ((("DW", dw),) + m._wq) if (tc == 0 and i == 0) else ()
            bk_, out = mm_group(P, pool, pairs, 512, waits=wx, rows=64)
            if tc == 0 and i == 0:
                m._wq = ()
            te = P.count("TE")
            d = out_t[0:64, tc * 512:(tc + 1) * 512]
            s64 = out
            if col is not None:
                P.emit("vector", lambda e, d=d, s=s64, c=col:
                       e.scalar_tensor_tensor(d, s, 1.0,
                                              c.broadcast_to([64, 512]),
                                              ALU.mult, ALU.add),
                       waits=(("TE", te),))
                pool.release(bk_, "VE")
            elif i == 0 and qscale is not None:
                P.emit("vector", lambda e, d=d, s=s64, q=qscale:
                       e.tensor_scalar_mul(d, s, q), waits=(("TE", te),))
                pool.release(bk_, "VE")
            else:
                P.emit("vector", lambda e, d=d, s=s64: e.tensor_copy(d, s),
                       waits=(("TE", te),))
                pool.release(bk_, "VE")
            if i == 1 and tc == 0 and dw_next_cb is not None:
                dw_next_cb()


def emit_block_b_attn(m, b):
    P, pool = m.P, m.pool
    w = m.w
    wsrc = w["L0_ffn_c_attn_w"].rearrange("(cc p) j -> p cc j", p=128)
    wp = w["L0_ffn_c_proj_w"].rearrange("(cc p) j -> p cc j", p=128)
    WA = carve(m.ar, m.off["R5"], (NCC, 1024), BF16)
    h1 = carve(m.ar, m.off["R1"], (NCC, T), BF16)
    qh = carve(m.ar, m.off["R2"], (T,), BF16)
    kh = carve(m.ar, m.off["R2"] + 4 * KB, (T,), BF16)
    m.E = carve(m.ar, m.off["R2"] + 8 * KB, (16, 512), BF16)
    vtm = carve(m.ar, m.off["R3"], (NTT, 16, 65), BF16)
    y_tok = carve(m.ar, m.off["R4"], (NTT, CD), BF16)
    scr = carve(m.ar, m.off["R2"] + 8 * KB, (NCC, 512), F32)

    m._h1ap = h1
    ln1_off = m.load_vec_cols(w["L0_ffn_ln1_w"], "b_ln1")
    ln2_off = m.load_vec_cols(w["L0_ffn_ln2_w"], "b_ln2")
    ve_pre = P.count("VE")
    for tc in range(NTC):
        chunks = [m.S[:, cc, tc * 512:(tc + 1) * 512] for cc in range(NCC)]
        outs = [h1[:, cc, tc * 512:(tc + 1) * 512] for cc in range(NCC)]
        bm, br = m.emit_ln_stats(chunks, True, 1e-5)
        if tc == 0:
            P.wait("vector", "VE", ve_pre)
        m.emit_ln_apply(chunks, outs, bm, br, scr, scale_off=ln1_off)
    dw = dma_w(m, WA, wsrc[:, :, 2048:3072])
    ve = P.count("VE")
    m._wq = (("DW", dw), ("VE", ve))
    _vtm_make(m, WA, h1, vtm)
    dw = None
    dwh = {0: _emit_whb_dma(m, wsrc, 0, 0, 1024)}

    for h in range(H):
        m._wq = ()

        def pref(h=h):
            if h + 1 < H:
                dwh[h + 1] = _emit_whb_dma(m, wsrc, h + 1, 0, 1024)
            if m._prepass_w is None or m._prepass_w[1] == 0:
                pass
        _perh_qk(m, wsrc, h, qh, kh, h1, 0, 1024, dwh[h], pref, qscale=0.125)
        if b == 0:
            if h < 8:
                P.emit("gpsimd", lambda e, h=h: e.dma_start(
                    out=m.w1bf[:, h * 2048:(h + 1) * 2048],
                    in_=m.w["L0_ffn_w1"][:, h * 2048:(h + 1) * 2048]),
                    incs=(("DW", 16),), auto=False)
            else:
                hh = h - 8
                P.emit("gpsimd", lambda e, hh=hh: e.dma_start(
                    out=m.w2bf[hh * 2048:(hh + 1) * 2048, :],
                    in_=m.w["L0_ffn_w2"][hh * 2048:(hh + 1) * 2048, :]),
                    incs=(("DW", 16),), auto=False)
            if h == 15:
                m._prepass_w = ("DW", P.count("DW"))
        if h == 0:
            dw = dma_w(m, WA, wp)
        _attn_core(m, h, qh, kh, vtm, y_tok, EXP_C_B)
    m._wq = (("DW", dw),)
    h2 = h1

    scr2d = carve(m.ar, m.off["R2"], (2, NCC, 512), F32)

    def epi_chunk(tc, jc, psum):
        te = P.count("TE")
        xch = m.S[:, jc, tc * 512:(tc + 1) * 512]
        P.emit("vector", lambda e, p=psum, x=xch, d=scr2d[:, tc % 2, jc, :]:
               e.tensor_add(d, p, x), waits=(("TE", te),))
        P.emit("vector", lambda e, x=xch, d=scr2d[:, tc % 2, jc, :]:
               e.tensor_add(x, d, x))

    def epi_tc(tc):
        outs = [h2[:, jc, tc * 512:(tc + 1) * 512] for jc in range(NCC)]
        m.emit_postln(scr2d[:, tc % 2, :, :], outs, scale_off=ln2_off)

    _proj_epilogue(m, WA, y_tok, epi_chunk, epi_tc)


def emit_block_c(m, b):
    P, pool = m.P, m.pool
    w = m.w
    wsrc = w["L1_attn_W1"].rearrange("(cc p) j -> p cc j", p=128)
    wp = w["L1_attn_W2"].rearrange("(cc p) j -> p cc j", p=128)
    WA = carve(m.ar, m.off["R5"], (NCC, 1024), BF16)
    qh = carve(m.ar, m.off["R2"], (T,), BF16)
    kh = carve(m.ar, m.off["R2"] + 4 * KB, (T,), BF16)
    m.E = carve(m.ar, m.off["R2"] + 8 * KB, (16, 512), BF16)
    vtm = carve(m.ar, m.off["R3"], (NTT, 16, 65), BF16)
    y_tok = carve(m.ar, m.off["R4"], (NTT, CD), BF16)
    scr2d = carve(m.ar, m.off["R2"], (2, NCC, 512), F32)

    b1k_off = m.load_vec_cols(w["L1_attn_b1"][0:1024], "c_b1k", per_head=True)
    b1q_off = m.load_vec_cols(w["L1_attn_b1"][1024:2048], "c_b1q", per_head=True)
    b2_off = m.load_vec_cols(w["L1_attn_b2"], "c_b2")
    te = P.count("TE")
    P.emit("sync", lambda e: e.dma_start(out=m.vrow,
           in_=w["L1_attn_b1"][2048:3072].unsqueeze(0)), waits=(("TE", te),),
           incs=(("DX", 16),), auto=False)
    dx = P.count("DX")
    ve_pre = P.count("VE")

    dw = dma_w(m, WA, wsrc[:, :, 2048:3072])
    m._wq = (("DW", dw), ("DX", dx), ("VE", ve_pre))
    _vtm_make(m, WA, m.S, vtm, bias_row=m.vrow)
    dw = None
    dwh = {0: _emit_whb_dma(m, wsrc, 0, 1024, 0)}
    for h in range(H):
        m._wq = ()
        bqc = m.vcols[0:64, b1q_off + h:b1q_off + h + 1]
        bkc = m.vcols[0:64, b1k_off + h:b1k_off + h + 1]

        def pref(h=h):
            if h + 1 < H:
                dwh[h + 1] = _emit_whb_dma(m, wsrc, h + 1, 1024, 0)
        _perh_qk(m, wsrc, h, qh, kh, m.S, 1024, 0, dwh[h], pref,
                 bq_col=bqc, bk_col=bkc)
        if b == 0 and h < 4:
            P.emit("gpsimd", lambda e, h=h: e.dma_start(
                out=m.f1bf.rearrange("c es f -> c (es f)")[
                    h * 256:(h + 1) * 256, :],
                in_=m.w["L1_ffn_f1"].rearrange("c e s2 f -> c (e s2 f)")[
                    h * 256:(h + 1) * 256, :]), incs=(("DW", 16),), auto=False)
        if b == 0 and 4 <= h < 8:
            hh = h - 4
            P.emit("gpsimd", lambda e, hh=hh: e.dma_start(
                out=m.f2bf.rearrange("es f c -> (es f) c")[
                    hh * 1024:(hh + 1) * 1024, :],
                in_=m.w["L1_ffn_f2"].rearrange("e s2 f c -> (e s2 f) c")[
                    hh * 1024:(hh + 1) * 1024, :]), incs=(("DW", 16),), auto=False)
        if b == 0 and h == 8:
            m._prepass_f = ("DW", P.count("DW"))
        if h == 0:
            dw = dma_w(m, WA, wp)
        _attn_core(m, h, qh, kh, vtm, y_tok, None)
    m._wq = (("DW", dw),)

    def epi_chunk(tc, jc, psum):
        te = P.count("TE")
        c = m.vcols[:, b2_off + jc:b2_off + jc + 1]
        P.emit("vector", lambda e, p=psum, d=scr2d[:, tc % 2, jc, :], c=c:
               e.scalar_tensor_tensor(d, p, 1.0, c.broadcast_to([128, 512]),
                                      ALU.mult, ALU.add), waits=(("TE", te),))
        xch = m.S[:, jc, tc * 512:(tc + 1) * 512]
        P.emit("vector", lambda e, x=xch, d=scr2d[:, tc % 2, jc, :]:
               e.tensor_add(d, d, x))

    def epi_tc(tc):
        outs = [m.S[:, jc, tc * 512:(tc + 1) * 512] for jc in range(NCC)]
        m.emit_postln(scr2d[:, tc % 2, :, :], outs)

    _proj_epilogue(m, WA, y_tok, epi_chunk, epi_tc)


def emit_block_b(m, b):
    emit_block_b_attn(m, b)
    m.P.fence()
    dbg_dump(m, f"bs{b}", m.S)
    dbg_dump(m, f"bh2_{b}", m._h1ap)
    dbg_dump(m, f"bgf{b}", m.gfm) if False else None
    emit_block_b_moe(m, b)
    dbg_dump(m, f"bgate{b}", m.gfm)


def emit_block_b_moe(m, b):
    P, pool = m.P, m.pool
    w = m.w
    h2 = carve(m.ar, m.off["R1"], (NCC, T), BF16)
    y_acc = carve(m.ar, m.off["R3"], (NCC, 1024), F32)
    w1b = carve(m.ar, m.off["R2"], (NCC, T), BF16)
    w2b = carve(m.ar, m.off["R4"], (16, CD), BF16)
    w1r = m.w1bf.rearrange("(cc p) f -> p cc f", p=128)
    w2r = m.w2bf.rearrange("(fc p) c -> p fc c", p=128)
    pw = m._prepass_w if m._prepass_w else ("DW", 0)

    te0 = P.count("TE")
    P.emit("gpsimd", lambda e: e.dma_start(
        out=m.rwsb, in_=w["L0_ffn_router_w"].rearrange("(cc p) e -> p cc e", p=128)),
        waits=(("TE", te0),), incs=(("DW", 16),), auto=False)
    dw_rw = P.count("DW")
    for e8 in range(8):
        P.emit("gpsimd", lambda e, e8=e8: e.memset(m.onehot[:, e8, :], 1.0),
               waits=(("GE", P.count("GE")),))
        P.emit("gpsimd", lambda e, e8=e8: e.affine_select(
            out=m.onehot[:, e8, :], in_=m.onehot[:, e8, :],
            compare_op=ALU.is_equal, fill=0.0, base=-e8,
            pattern=[[0, 128]], channel_multiplier=1),
            waits=(("GE", P.count("GE")),))
    for tc in range(NTC):
        bk_rl, gwl = pool.take()
        rl = pool.ap[0:8, bk_rl, :]
        mm_into(P, rl, [(m.rwsb[:, cc, :], h2[:, cc, tc * 512:(tc + 1) * 512])
                        for cc in range(NCC)],
                waits=(("DW", dw_rw),) + tuple(gwl))
        te = P.count("TE")
        P.emit("scalar", lambda e, rl=rl: e.copy(m.sq[0:8, 0:512], rl),
               waits=(("TE", te),))
        pool.release(bk_rl, "SE")
        se = P.count("SE")
        ltm_sb = m.sq[:, 112:144]
        for i in range(4):
            bk_t, gwt = pool.take()
            outv = pool.ap[:, bk_t, 0:8]
            P.emit("tensor", lambda e, o=outv,
                   s=m.sq[0:8, i * 128:(i + 1) * 128]:
                   e.transpose(o, s, m.ident_f),
                   waits=(("SE", se),) + tuple(gwt),
                   incs=(("TE", 1),))
            te = P.count("TE")
            P.emit("vector", lambda e, o=outv, d=ltm_sb[:, i * 8:(i + 1) * 8]:
                   e.tensor_copy(d, o), waits=(("TE", te),))
            pool.release(bk_t, "VE")
        ltm = ltm_sb.rearrange("p (a b) -> p a b", a=4)

        def vv(fn):
            P.emit("vector", fn, waits=(("VE", P.count("VE")),))
        m1 = m.grows[:, 0:4]
        m2 = m.grows[:, 4:8]
        p1 = m.grows[:, 8:12]
        p2 = m.grows[:, 12:16]
        eq1 = m.gscr[:, 0, :].rearrange("p (a b) -> p a b", a=4)
        lm = m.gscr[:, 1, :].rearrange("p (a b) -> p a b", a=4)
        gg = m.gscr[:, 2, 0:32].bitcast(BF16)[:, 0:32].rearrange(
            "p (a b) -> p a b", a=4)
        t1 = m.gscr[:, 1, 0:16].rearrange("p (a b) -> p a b", a=4)
        t2 = m.gscr[:, 1, 16:24].rearrange("p (a b) -> p a b", a=4)
        vv(lambda e, ltm=ltm, t1=t1: e.tensor_max(
            t1, ltm[:, :, 0::2], ltm[:, :, 1::2]))
        vv(lambda e, t1=t1, t2=t2: e.tensor_max(
            t2, t1[:, :, 0::2], t1[:, :, 1::2]))
        vv(lambda e, t2=t2, m1=m1: e.tensor_max(
            m1.rearrange("p (a o) -> p a o", o=1), t2[:, :, 0:1], t2[:, :, 1:2]))
        vv(lambda e, ltm=ltm, eq1=eq1, m1=m1: e.tensor_tensor(
            eq1, ltm, m1.rearrange("p (a o) -> p a o", o=1).broadcast_to([128, 4, 8]),
            ALU.is_equal))
        vv(lambda e, ltm=ltm, eq1=eq1, lm=lm: e.scalar_tensor_tensor(
            lm, eq1, NEG, ltm, ALU.mult, ALU.add))
        u1 = m.gscr[:, 2, 0:16].rearrange("p (a b) -> p a b", a=4)
        u2 = m.gscr[:, 2, 16:24].rearrange("p (a b) -> p a b", a=4)
        vv(lambda e, lm=lm, u1=u1: e.tensor_max(
            u1, lm[:, :, 0::2], lm[:, :, 1::2]))
        vv(lambda e, u1=u1, u2=u2: e.tensor_max(
            u2, u1[:, :, 0::2], u1[:, :, 1::2]))
        vv(lambda e, u2=u2, m2=m2: e.tensor_max(
            m2.rearrange("p (a o) -> p a o", o=1), u2[:, :, 0:1], u2[:, :, 1:2]))
        vv(lambda e, lm=lm, m2=m2: e.tensor_tensor(
            lm, lm, m2.rearrange("p (a o) -> p a o", o=1).broadcast_to([128, 4, 8]),
            ALU.is_equal))
        vv(lambda e, m1=m1, m2=m2: e.tensor_sub(m2, m2, m1))
        ve = P.count("VE")
        P.emit("scalar", lambda e, p1=p1, m2=m2: e.activation(
            p1, m2, AF.Sigmoid, scale=-1.0), waits=(("VE", ve),))
        se = P.count("SE")
        P.emit("vector", lambda e, p1=p1, p2=p2: e.tensor_scalar(
            p2, p1, -1.0, 1.0, ALU.mult, ALU.add),
            waits=(("SE", se), ("VE", P.count("VE"))))
        vv(lambda e, eq1=eq1, p1=p1: e.tensor_mul(
            eq1, eq1, p1.rearrange("p (a o) -> p a o", o=1).broadcast_to([128, 4, 8])))
        vv(lambda e, lm=lm, p2=p2: e.tensor_mul(
            lm, lm, p2.rearrange("p (a o) -> p a o", o=1).broadcast_to([128, 4, 8])))
        vv(lambda e, gg=gg, eq1=eq1, lm=lm: e.tensor_add(gg, eq1, lm))
        if tc == 0:
            dbg_dump(m, f"gsq{b}", m.sq)
        ve = P.count("VE")
        for i in range(4):
            bk_g, gwg = pool.take()
            outg = pool.ap[0:8, bk_g, 0:64].bitcast(BF16)
            P.emit("tensor", lambda e, o=outg,
                   s=m.gscr[:, 2, 0:32].bitcast(BF16)[:, i * 8:(i + 1) * 8]:
                   e.transpose(o, s, m.ident_b), waits=(("VE", ve),) + tuple(gwg),
                   incs=(("TE", 1),))
            te = P.count("TE")
            P.emit("scalar", lambda e, o=outg, d=m.gfm[:, tc * 512 + i * 128:
                   tc * 512 + (i + 1) * 128]: e.copy(d, o), waits=(("TE", te),))
            pool.release(bk_g, "SE")
    se_g = P.count("SE")
    ge_g = P.count("GE")
    for half in range(2):
        t0 = half * 1024
        P.emit("gpsimd", lambda e: e.memset(
            y_acc.rearrange("p a b -> p (a b)"), 0.0))
        ge_z = P.count("GE")
        te = P.count("TE")
        P.emit("sync", lambda e: e.dma_start(out=w1b, in_=w1r[:, :, 0:2048]),
               waits=(("TE", te), pw), incs=(("DX", 16),), auto=False)
        dw1 = ("DX", P.count("DX"))
        P.emit("sync", lambda e: e.dma_start(out=w2b, in_=w2r[:, 0:16, :]),
               waits=(pw,), incs=(("DX", 16),), auto=False)
        dw2 = ("DX", P.count("DX"))
        for e8 in range(8):
            dw1_next = []
            for tq in range(4):
                tg = t0 + tq * 256
                bg, gwg = pool.take(hold=True)
                gb = pool.bank(bg, 256)
                mm_into(P, gb, [(m.onehot[:, e8, :], m.gfm[:, tg:tg + 256])],
                        waits=(("SE", se_g), ("GE", ge_g)) + tuple(gwg))
                ve_fc = []
                for fc in range(16):
                    pairs = [(w1b[:, cc, fc * 128:(fc + 1) * 128],
                              h2[:, cc, tg:tg + 256]) for cc in range(NCC)]
                    wts = ()
                    if isinstance(dw1, list):
                        if tq == 0:
                            wts = (dw1[fc],)
                    elif tq == 0 and fc == 0:
                        wts = (dw1,)
                    bh, out = mm_group(P, pool, pairs, 256, waits=wts)
                    te = P.count("TE")
                    P.emit("scalar", lambda e, d=m.hprime[:, fc, :], s=out:
                           e.activation(d, s, AF.Gelu), waits=(("TE", te),))
                    pool.release(bh, "SE")
                    se = P.count("SE")
                    P.emit("vector", lambda e, d=m.hprime[:, fc, :], gb=gb:
                           e.tensor_mul(d, d, gb), waits=(("SE", se),))
                    ve_fc.append(P.count("VE"))
                    if e8 < 7 and tq == 3:
                        te_p = P.count("TE")
                        P.emit("sync", lambda e, e8=e8, fc=fc: e.dma_start(
                            out=w1b[:, :, fc * 128:(fc + 1) * 128],
                            in_=w1r[:, :, (e8 + 1) * 2048 + fc * 128:
                                    (e8 + 1) * 2048 + (fc + 1) * 128]),
                            waits=(("TE", te_p),), incs=(("DX", 16),), auto=False)
                        dw1_next.append(("DX", P.count("DX")))
                pool.release(bg, "VE")
                if False:
                    te = P.count("TE")
                    P.emit("gpsimd", lambda e, e8=e8: e.dma_start(
                        out=w1b, in_=w1r[:, :, (e8 + 1) * 2048:(e8 + 2) * 2048]),
                        waits=(("TE", te),), incs=(("DW", 16),), auto=False)
                    dw1 = P.count("DW")
                for cc in range(NCC):
                    by, gwy = pool.take()
                    out = pool.bank(by, 256)
                    for fc in range(16):
                        wts = [("VE", ve_fc[fc])]
                        if fc == 0:
                            wts += list(gwy)
                        if tq == 0 and cc == 0 and fc == 0:
                            wts.append(dw2)
                        P.emit("tensor",
                               lambda e, l=w2b[:, fc, cc * 128:(cc + 1) * 128],
                               r=m.hprime[:, fc, :], s=(fc == 0), t=(fc == 15),
                               out=out: e.matmul(out, l, r, start=s, stop=t),
                               waits=tuple(wts),
                               incs=((("TE", 1),) if fc == 15 else ()))
                    te = P.count("TE")
                    dsty = y_acc[:, cc, tq * 256:(tq + 1) * 256]
                    wv = (("TE", te),)
                    if e8 == 0 and tq == 0 and cc == 0:
                        wv = wv + (("GE", ge_z),)
                    P.emit("vector", lambda e, d=dsty, s=out: e.tensor_add(d, d, s),
                           waits=wv)
                    pool.release(by, "VE")
            if e8 < 7:
                te = P.count("TE")
                P.emit("sync", lambda e, e8=e8: e.dma_start(
                    out=w2b, in_=w2r[:, (e8 + 1) * 16:(e8 + 2) * 16, :]),
                    waits=(("TE", te),), incs=(("DX", 16),), auto=False)
                dw2 = ("DX", P.count("DX"))
                dw1 = dw1_next
        for tc in range(2):
            for jc in range(NCC):
                ch = y_acc[:, jc, tc * 512:(tc + 1) * 512]
                xch = m.S[:, jc, t0 + tc * 512:t0 + (tc + 1) * 512]
                P.emit("vector", lambda e, c=ch, x=xch: e.tensor_add(c, c, x))
            scrv = y_acc[:, :, tc * 512:(tc + 1) * 512]
            outs = [m.S[:, jc, t0 + tc * 512:t0 + (tc + 1) * 512]
                    for jc in range(NCC)]
            m.emit_postln(scrv, outs)
        m.P.fence(("gpsimd",))


def emit_block_d(m, b):
    P, pool = m.P, m.pool
    w = m.w
    R1, R2, R3, R4, R5 = (m.off[k] for k in ("R1", "R2", "R3", "R4", "R5"))
    x_tok = carve(m.ar, R1, (NTT, CD), BF16)
    disp = carve(m.ar, R2, (NCC, 16, 128), BF16)
    packs = carve(m.ar, R3, (NCC, 16, 128), BF16)
    logits_tm = carve(m.ar, R4, (NTT, 128), F32)
    l_fm = carve(m.ar, R4 + 8 * KB, (16, 128), F32)
    maxg = carve(m.ar, R4 + 16 * KB, (64,), F32)
    P_fm = carve(m.ar, R4 + 17 * KB, (NTT, 128), BF16)
    P4 = carve(m.ar, R4, (NTT, 4, 128), BF16)
    inner = carve(m.ar, R4 + 21 * KB, (128, 16), BF16)[0:32, :, :]
    bias_nat = carve(m.ar, R4 + 29 * KB, (32,), F32)
    interT = carve(m.ar, R4 + 30 * KB, (4, 128), BF16)
    ctrl_sb = carve(m.ar, R5, (NCC, 128), BF16)
    f1blk = carve(m.ar, R5, (NCC, 2, 16, 32), BF16)
    f2str = carve(m.ar, R5, (2, 4, NCC, 128), BF16)[0:32, :, :, :, :]
    bias_sb = m.sq[0:32, 128:256]
    r1 = carve(m.ar, R2, (1024,), F32)
    r2 = carve(m.ar, R2 + 4 * KB, (512,), F32)
    r3 = carve(m.ar, R2 + 6 * KB, (256,), F32)
    r4 = carve(m.ar, R2 + 7 * KB, (128,), F32)

    te0 = P.count("TE")
    P.emit("sync", lambda e: e.dma_start(out=m.tieq, in_=m.cdata),
           incs=(("DX", 16),), auto=False)
    P.emit("gpsimd", lambda e: e.dma_start(
        out=ctrl_sb, in_=w["L1_ffn_controller"].rearrange(
            "(cc p) e s2 -> p cc (e s2)", p=128)), waits=(("TE", te0),),
        incs=(("DW", 16),), auto=False)
    P.emit("gpsimd", lambda e: e.dma_start(
        out=bias_nat, in_=w["L1_ffn_bias"].rearrange("e s2 f -> (e s2) f")),
        incs=(("DW", 16),), auto=False)
    dw0 = P.count("DX")
    dwc = P.count("DW")
    bkb, gwb = pool.take()
    outb = pool.bank(bkb, 128)[0:32, :]
    P.emit("tensor", lambda e: e.transpose(outb, bias_nat, m.ident_f128),
           waits=(("DW", dwc),) + tuple(gwb), incs=(("TE", 1),))
    te = P.count("TE")
    P.emit("vector", lambda e: e.tensor_copy(bias_sb, outb), waits=(("TE", te),))
    pool.release(bkb, "VE")
    for tt in range(NTT):
        for cc in range(NCC):
            transpose128(P, pool, m.S[:, cc, tt * 128:(tt + 1) * 128],
                         x_tok[:, tt, cc * 128:(cc + 1) * 128], m.ident_b)
    dx0 = P.count("DX")
    for tt in range(NTT):
        pairs = [(m.S[:, cc, tt * 128:(tt + 1) * 128], ctrl_sb[:, cc, :])
                 for cc in range(NCC)]
        bk, out = mm_group(P, pool, pairs, 128,
                           waits=(("DW", dwc),) if tt == 0 else ())
        te = P.count("TE")
        wts = (("TE", te),)
        if tt == 0:
            wts = wts + (("DX", dx0),)
        P.emit("vector", lambda e, d=logits_tm[:, tt, :], s=out:
               e.scalar_tensor_tensor(d, s, 1.0,
                                      m.tieq[:, 0:1].broadcast_to([128, 128]),
                                      ALU.mult, ALU.add), waits=wts)
        pool.release(bk, "VE")
    ve = P.count("VE")
    for tt in range(NTT):
        bk, gw = pool.take()
        out = pool.bank(bk, 128)
        P.emit("tensor", lambda e, o=out, s=logits_tm[:, tt, :]:
               e.transpose(o, s, m.ident_f128),
               waits=(("VE", ve),) + tuple(gw) if tt == 0 else tuple(gw),
               incs=(("TE", 1),))
        te = P.count("TE")
        P.emit("vector", lambda e, d=l_fm[:, tt, :], s=out: e.tensor_copy(d, s),
               waits=(("TE", te),))
        pool.release(bk, "VE")

    def vv(fn):
        P.emit("vector", fn, waits=(("VE", P.count("VE")),))

    L = l_fm.rearrange("p a b -> p (a b)")
    vv(lambda e: e.tensor_max(r1, L[:, 0::2], L[:, 1::2]))
    vv(lambda e: e.tensor_max(r2, r1[:, 0::2], r1[:, 1::2]))
    vv(lambda e: e.tensor_max(r3, r2[:, 0::2], r2[:, 1::2]))
    vv(lambda e: e.tensor_max(r4, r3[:, 0::2], r3[:, 1::2]))
    vv(lambda e: e.tensor_max(maxg, r4[:, 0::2], r4[:, 1::2]))
    vv(lambda e: e.tensor_tensor(
        P_fm.rearrange("p a b -> p (a b)").rearrange("p (g t) -> p g t", g=64),
        L.rearrange("p (g t) -> p g t", g=64),
        maxg.unsqueeze(2).broadcast_to([128, 64, 32]), ALU.is_equal))
    ve_p = P.count("VE")
    for tt in range(NTT):
        bk, gw = pool.take()
        out = pool.bank_bf16(bk)
        P.emit("tensor", lambda e, o=out, s=P_fm[:, tt, :]:
               e.transpose(o, s, m.ident_b),
               waits=(("VE", ve_p),) + tuple(gw), incs=(("TE", 1),))
        te = P.count("TE")
        for gl in range(4):
            P.emit("vector", lambda e, d=P4[:, tt, gl, :], s=out, gl=gl:
                   e.scalar_tensor_tensor(
                       d, s, 1.0, m.tieq[:, 1 + gl:2 + gl].broadcast_to([128, 128]),
                       ALU.mult, ALU.mult),
                   waits=(("TE", te),) if gl == 0 else ())
        pool.release(bk, "VE")
    ve_p4 = P.count("VE")
    dbg_dump(m, f"dP{b}", P_fm)
    dbg_dump(m, f"dL{b}", l_fm)

    if m._prepass_f is None:
        P.emit("gpsimd", lambda e: e.dma_start(
            out=m.f1bf.rearrange("c es f -> c (es f)"),
            in_=w["L1_ffn_f1"].rearrange("c e s2 f -> c (e s2 f)")),
            incs=(("DW", 16),), auto=False)
        P.emit("gpsimd", lambda e: e.dma_start(
            out=m.f2bf.rearrange("es f c -> (es f) c"),
            in_=w["L1_ffn_f2"].rearrange("e s2 f c -> (e s2 f) c")),
            incs=(("DW", 16),), auto=False)
        m._prepass_f = ("DW", P.count("DW"))
    f1r = m.f1bf.rearrange("(cc p) es f -> p cc es f", p=128)
    f2w = m.f2bf
    for q in range(4):
        tts = range(4 * q, 4 * q + 4)
        te_q = P.count("TE")
        P.emit("gpsimd", lambda e: e.dma_start(
            out=f1blk[:, :, 0, :, :], in_=f1r[:, :, 0:16, :]),
            waits=(("TE", te_q), m._prepass_f), incs=(("DW", 16),), auto=False)
        dwf0 = P.count("DW")
        for i, tt in enumerate(tts):
            for cc in range(NCC):
                pairs = [(x_tok[:, tt, cc * 128:(cc + 1) * 128],
                          P4[:, tt, :, :].rearrange("p a b -> p (a b)"))]
                bk, out = mm_group(P, pool, pairs, 512,
                                   waits=(("VE", ve_p4),) if (i == 0 and cc == 0 and q == 0) else ())
                te = P.count("TE")
                dst = disp[:, cc, 4 * i:4 * i + 4, :]
                P.emit("vector", lambda e, d=dst, s=out.rearrange(
                    "p (a b) -> p a b", a=4): e.tensor_copy(d, s),
                    waits=(("TE", te),))
                pool.release(bk, "VE")
        dwf = dwf0
        ve_disp = P.count("VE")
        for blk in range(8):
            if blk < 7:
                te_b = P.count("TE")
                P.emit("gpsimd", lambda e, blk=blk: e.dma_start(
                    out=f1blk[:, :, (blk + 1) % 2, :, :],
                    in_=f1r[:, :, 16 * (blk + 1):16 * (blk + 2), :]),
                    waits=(("TE", te_b),), incs=(("DW", 16),), auto=False)
            for el in range(16):
                es = blk * 16 + el
                pairs = [(f1blk[:, cc, blk % 2, el, :],
                          disp[:, cc, :, es]) for cc in range(NCC)]
                wts = (("DW", dwf), ("VE", ve_disp)) if el == 0 else ()
                bk, out = mm_group(P, pool, pairs, 16, rows=32, waits=wts)
                te = P.count("TE")
                P.emit("scalar", lambda e, d=inner[:, es, :], s=out, es=es:
                       e.activation(d, s, AF.Relu, bias=bias_sb[:, es:es + 1]),
                       waits=(("TE", te),))
                pool.release(bk, "SE")
            dwf = P.count("DW")
        se_f1 = P.count("SE")
        nblk2 = 32
        P.emit("gpsimd", lambda e: e.dma_start(
            out=f2str[:, 0, :, :, :].rearrange("p a b c -> p a (b c)"),
            in_=f2w[0:4].rearrange("x f c -> f x c")),
            waits=(m._prepass_f,), incs=(("DW", 16),), auto=False)
        dw_blk = [P.count("DW")]
        for grp in range(4):
            banks = []
            for cc in range(NCC):
                bk, gw = pool.take(hold=True)
                banks.append((bk, gw))
            for el in range(32):
                es = grp * 32 + el
                if es % 4 == 0:
                    nb = es // 4 + 1
                    if nb < nblk2:
                        te_b = P.count("TE")
                        P.emit("gpsimd", lambda e, nb=nb: e.dma_start(
                            out=f2str[:, nb % 2, :, :, :].rearrange(
                                "p a b c -> p a (b c)"),
                            in_=f2w[4 * nb:4 * nb + 4].rearrange("x f c -> f x c")),
                            waits=(("TE", te_b),), incs=(("DW", 16),), auto=False)
                        dw_blk.append(P.count("DW"))
                for cc in range(NCC):
                    bk, gw = banks[cc]
                    outv = pool.ap[:, bk, el * 16:(el + 1) * 16]
                    wts = tuple(gw)
                    if cc == 0 and es % 4 == 0:
                        wts = wts + (("DW", dw_blk[es // 4]),)
                    if el == 0 and cc == 0:
                        wts = wts + (("SE", se_f1),)
                    mm_into(P, outv,
                            [(f2str[:, (es // 4) % 2, es % 4, cc, :],
                              inner[:, es, :])], waits=wts)
                    if el == 0:
                        banks[cc] = (bk, ())
            te = P.count("TE")
            for cc in range(NCC):
                bk, _ = banks[cc]
                src = pool.bank(bk).rearrange("p (a b) -> p a b", a=32)
                src = src.transpose([0, 2, 1])
                dst = packs[:, cc, :, grp * 32:(grp + 1) * 32]
                P.emit("vector", lambda e, d=dst, s=src: e.tensor_copy(d, s),
                       waits=(("TE", te),) if cc == 0 else ())
                pool.release(bk, "VE")
        ve_pk = P.count("VE")
        items = []
        for cc in range(NCC):
            for gl in range(16):
                items.append((cc, gl))
        pend = {}
        DEPTH = 3

        def emit_mm(idx):
            cc, gl, ve_i = pend.pop(idx)
            g = q * 16 + gl
            tt, t0 = g // 4, (g % 4) * 32
            it = interT[:, idx % 4, :]
            bk2, gw2 = pool.take()
            out2 = pool.bank(bk2, 32)
            mm_into(P, out2, [(it, P_fm[:, tt, t0:t0 + 32])],
                    waits=(("VE", ve_i),) + tuple(gw2))
            te = P.count("TE")
            dst = m.S[:, cc, g * 32:(g + 1) * 32]
            P.emit("vector", lambda e, d=dst, s=out2: e.tensor_add(d, s, d),
                   waits=(("TE", te),))
            pool.release(bk2, "VE")

        for idx, (cc, gl) in enumerate(items):
            sl = idx % 4
            it = interT[:, sl, :]
            bk, gw = pool.take()
            outt = pool.bank_bf16(bk)
            w0 = (("VE", ve_pk),) if idx == 0 else ()
            P.emit("tensor", lambda e, o=outt, s=packs[:, cc, gl, :]:
                   e.transpose(o, s, m.ident_b),
                   waits=w0 + tuple(gw), incs=(("TE", 1),))
            te = P.count("TE")
            P.emit("vector", lambda e, o=outt, it=it: e.tensor_copy(it, o),
                   waits=(("TE", te),))
            pool.release(bk, "VE")
            pend[idx] = (cc, gl, P.count("VE"))
            if idx >= DEPTH:
                emit_mm(idx - DEPTH)
        for idx in sorted(pend):
            emit_mm(idx)
    scr = carve(m.ar, R2, (NCC, 512), F32)
    ve_s = P.count("VE")
    P.wait("vector", "VE", ve_s)
    for tc in range(NTC):
        chunks = [m.S[:, cc, tc * 512:(tc + 1) * 512] for cc in range(NCC)]
        outs = chunks
        bm, br = m.emit_ln_stats(chunks, True, 1e-5)
        m.emit_ln_apply(chunks, outs, bm, br, scr)


def make_cdata():
    tie = np.tile(np.linspace(0.0, 1e-6, 32, dtype=np.float32), 4)
    cd = np.zeros((128, 8), np.float32)
    cd[:, 0] = tie
    for gl in range(4):
        cd[32 * gl:32 * gl + 32, 1 + gl] = 1.0
    return cd



def build_module(blocks=("A", "B", "C", "D"), nb=2):
    m = Mod(nb=nb)
    m.emit_setup()
    m.P.fence()
    for b in range(nb):
        m.emit_stream_in(b)
        m.P.fence()
        dbg_dump(m, f"s{b}", m.S)
        if "A" in blocks:
            emit_block_a(m, b)
            m.P.fence()
            dbg_dump(m, f"postA{b}", m.S)
        if "B" in blocks:
            emit_block_b(m, b)
            m.P.fence()
            dbg_dump(m, f"postB{b}", m.S)
        if "C" in blocks:
            emit_block_c(m, b)
            m.P.fence()
            dbg_dump(m, f"postC{b}", m.S)
        if "D" in blocks:
            emit_block_d(m, b)
            m.P.fence()
        m.emit_stream_out(b)
        m.P.fence()
    m.P.wait("sync", "DX")
    m.P.finalize()
    return m


DBG_TAGS = set()


def dbg_dump(m, tag, ap):
    if tag not in DBG_TAGS:
        return
    P = m.P
    flat = ap if ap.ndim == 2 else ap.rearrange(
        "p " + " ".join(f"d{i}" for i in range(ap.ndim - 1)) + " -> p (" +
        " ".join(f"d{i}" for i in range(ap.ndim - 1)) + ")")
    t = m.nc.dram_tensor("dbg_" + tag, [flat.shape[0], flat.shape[1]],
                         flat.dtype, kind="ExternalOutput").ap()
    P.fence()
    P.emit("sync", lambda e: e.dma_start(out=t, in_=flat),
           incs=(("DX", 16),), auto=False)
    P.fence()


GROUPS = [("A", "B", "C", "D")]


def kernel(**inputs):
    xfull = np.ascontiguousarray(inputs["L0_attn_x"], dtype=np.float32)
    shards = [xfull[2 * i:2 * i + 2] for i in range(8)]
    wmap = {n: np.ascontiguousarray(inputs[n], dtype=np.float32) for n in WSPEC}
    cur = shards
    for grp in GROUPS:
        m = build_module(blocks=grp, nb=2)
        cd = make_cdata()
        in_maps = [dict(wmap, x=cur[i], cdata=cd) for i in range(8)]
        res = run_bass_kernel_spmd(m.nc, in_maps, core_ids=list(range(8)))
        cur = [np.asarray(r["y"]) for r in res.results]
    return np.concatenate(cur, axis=0)
